# version 05
# baseline (speedup 1.0000x reference)
import contextlib
import numpy as np
import concourse.bass as bass
import concourse.mybir as mybir
from concourse.bass_utils import run_bass_kernel_spmd

F32 = mybir.dt.float32
BF16 = mybir.dt.bfloat16
I8 = mybir.dt.int8
AF = mybir.ActivationFunctionType
ALU = mybir.AluOpType
AX = mybir.AxisListType

T = 2048
D = 1024
NT = T // 128
HF = 4096
FUV = 4224
E = 2048
NH = 16
NEG = -1.0e30

PHASES = ["p1a", "p1b", "p2a", "p2b", "p2c", "p3a", "p3b", "p4", "p5a", "p5b",
          "p6"]
N_PHASES = len(PHASES)
N_PHASES_RUN = N_PHASES
SAFE_MODE = False
DEBUG_DUMP = []

ENGS = ("sync", "gpsimd", "vector", "scalar", "tensor")


class Sched:
    def __init__(self, nc, stk):
        self.nc = nc
        self.stk = stk
        self.q = {e: [] for e in ENGS}
        self.sems = {}
        self.cnt = {}
        self.barrier_pending = {e: None for e in ENGS}
        self.site_map = {}
        self.pool_base = 0

    def sem(self, name):
        if name not in self.sems:
            self.sems[name] = self.stk.enter_context(self.nc.semaphore(name))
            self.cnt[name] = 0
        return self.sems[name]

    def site_sem(self, key, kind):
        if key not in self.site_map:
            n = sum(1 for k2 in self.site_map if self.site_map[k2][1] == kind)
            name = "q%s%d" % (kind, n)
            assert n < 34, "sem pool exhausted"
            self.site_map[key] = (name, kind)
        return self.site_map[key][0]

    def emit(self, eng, builder, waits=(), inc=None, inc_n=1):
        wmax = {}
        bp = self.barrier_pending[eng]
        if bp is not None:
            for s, v in bp.items():
                if v > 0:
                    wmax[s] = v
            self.barrier_pending[eng] = None
        def add(wv):
            if wv is None:
                return
            if isinstance(wv, list):
                for w2 in wv:
                    add(w2)
                return
            s, v = wv
            if isinstance(v, tuple):
                add(v)
                return
            if v is not None and v > 0:
                wmax[s] = max(wmax.get(s, 0), v)
        for wv in waits:
            add(wv)
        if inc in ("v", "s", "g"):
            self.sem(inc)
            prevcnt = self.cnt[inc]
            if prevcnt > 0:
                wmax[inc] = max(wmax.get(inc, 0), prevcnt)
        for s in wmax:
            self.sem(s)
        tok = None
        if inc is not None:
            self.sem(inc)
            self.cnt[inc] += inc_n
            tok = self.cnt[inc]
        self.q[eng].append((wmax, builder, inc, inc_n))
        return tok

    def dma(self, eng, out, in_, waits=(), inc="ld", slow=False, sem=None):
        def b(h, out=out, in_=in_, slow=slow):
            return h.dma_start(out=out, in_=in_,
                               allow_slow_non_contiguous=slow)
        kind = "g" if eng == "gpsimd" else "s"
        if sem is None:
            import inspect
            sem = self.site_sem(
                "d%d" % inspect.currentframe().f_back.f_lineno, kind)
        else:
            sem = self.site_sem("d_" + sem, kind)
        tok = self.emit(eng, b, waits, sem, 16)
        return (sem, tok)

    def dmat(self, out, in_, waits=(), sem=None):
        def b(h, out=out, in_=in_):
            return h.dma_start_transpose(out=out, in_=in_)
        if sem is None:
            import inspect
            sem = self.site_sem(
                "t%d" % inspect.currentframe().f_back.f_lineno, "t")
        else:
            sem = self.site_sem("t_" + sem, "t")
        return (sem, self.emit("sync", b, waits, sem, 16))

    def barrier(self):
        snap = dict(self.cnt)
        for e in ENGS:
            self.barrier_pending[e] = snap
        self.site_map = {}

    def run(self, blk):
        sch = self

        def mk(eng):
            def body(h):
                for wmax, builder, inc, inc_n in sch.q[eng]:
                    for s, v in wmax.items():
                        h.wait_ge(sch.sems[s], v)
                    inst = builder(h)
                    if inc is not None:
                        inst.then_inc(sch.sems[inc], inc_n)
            return body

        blk.sync(mk("sync"))
        blk.gpsimd(mk("gpsimd"))
        blk.vector(mk("vector"))
        blk.scalar(mk("scalar"))
        blk.tensor(mk("tensor"))


class Arena:
    def __init__(self, ap, nbytes):
        self.ap = ap
        self.nbytes = nbytes
        self.off = 0
        self.marks = []

    def alloc(self, n_elem, dt, parts=128):
        sz = n_elem * mybir.dt.size(dt)
        off = (self.off + 63) & ~63
        assert off + sz <= self.nbytes, f"arena overflow {off}+{sz}"
        self.off = off + sz
        return self.ap[:parts, off:off + sz].bitcast(dt)

    def mark(self):
        self.marks.append(self.off)

    def reset(self):
        self.off = self.marks.pop()


def blocks(ap, nblk):
    return ap.rearrange("p (k f) -> p k f", k=nblk)


def kpf(dram_ap):
    return dram_ap.rearrange("(k p) f -> p k f", p=128)


def row(dram_ap):
    return dram_ap.rearrange("(one f) -> one f", one=1)


def build_nc(n_phases=N_PHASES):
    nc = bass.Bass("TRN2", target_bir_lowering=False)

    inp = {}
    for name, shp in [
        ("L0_attn_inputs", [T, D]), ("L0_attn_W_uv", [D, FUV]),
        ("L0_attn_b_uv", [FUV]), ("L0_attn_gamma", [2, 128]),
        ("L0_attn_beta", [2, 128]), ("L0_attn_W_out", [E, D]),
        ("L0_attn_b_out", [D]), ("L0_ffn_W_fc", [HF, D]),
        ("L0_ffn_ln_w", [HF]), ("L0_ffn_ln_b", [HF]),
        ("L0_ffn_W_proj", [D, HF]), ("L1_attn_Wq", [D, D]),
        ("L1_attn_Wk", [D, D]), ("L1_attn_Wv", [D, D]),
        ("L1_attn_Wo", [D, D]), ("L1_ffn_weight", [D, D]),
    ]:
        inp[name] = nc.dram_tensor(name, shp, F32, kind="ExternalInput").ap()
    out = nc.dram_tensor("out", [T, D], F32, kind="ExternalOutput").ap()

    def scr(name, shp, dt):
        return nc.dram_tensor(name, shp, dt, kind="Internal").ap()

    xA = scr("xA", [T, D], F32)
    xB = scr("xB", [T, D], F32)
    s_wfcqT = scr("s_wfcqT", [D, HF], BF16)
    s_wprojqT = scr("s_wprojqT", [HF, D], BF16)
    s_wuv_h = scr("s_wuv_h", [D, FUV], BF16)
    s_wuv_l = scr("s_wuv_l", [D, FUV], BF16)
    s_wout_h = scr("s_wout_h", [E, D], BF16)
    s_wout_l = scr("s_wout_l", [E, D], BF16)
    s_wT = {}
    for w in ("q", "k", "v", "o"):
        s_wT[w] = (scr(f"s_w{w}T_h", [D, D], BF16),
                   scr(f"s_w{w}T_l", [D, D], BF16))
    s_wl1qT = scr("s_wl1qT", [D, D], BF16)
    s_hnT_h = scr("s_hnT_h", [HF, T], BF16)
    s_hnT_l = scr("s_hnT_l", [HF, T], BF16)
    s_u_h = scr("s_u_h", [T, E], BF16)
    s_u_l = scr("s_u_l", [T, E], BF16)
    s_v_h = scr("s_v_h", [T, E], BF16)
    s_v_l = scr("s_v_l", [T, E], BF16)
    s_qh = scr("s_qh", [128, T], BF16)
    s_ql = scr("s_ql", [128, T], BF16)
    s_kh = scr("s_kh", [128, T], BF16)
    s_kl = scr("s_kl", [128, T], BF16)
    s_uavT_h = scr("s_uavT_h", [E, T], BF16)
    s_uavT_l = scr("s_uavT_l", [E, T], BF16)
    s_q1T_h = scr("s_q1T_h", [D, T], BF16)
    s_q1T_l = scr("s_q1T_l", [D, T], BF16)
    s_k1T_h = scr("s_k1T_h", [D, T], BF16)
    s_k1T_l = scr("s_k1T_l", [D, T], BF16)
    s_v1_h = scr("s_v1_h", [T, D], BF16)
    s_v1_l = scr("s_v1_l", [T, D], BF16)
    s_dbgA = scr("s_dbgA", [128, 2048], F32)
    s_dbgB = scr("s_dbgB", [128, 2048], F32)
    s_dbgC = scr("s_dbgC", [128, 2048], F32)
    s_pr = scr("s_pr", [128], F32)
    s_scal = scr("s_scal", [8], F32)
    s_bconst = scr("s_bconst", [D], F32)

    stk = contextlib.ExitStack()
    with stk:
        SB_BYTES = 207 * 1024
        arena_t = stk.enter_context(nc.sbuf_tensor("arena", [128, SB_BYTES],
                                                   I8))
        A = Arena(arena_t[:], SB_BYTES)
        pbig1 = stk.enter_context(nc.psum_tensor("pbig1", [128, 2048], F32))
        pbig2 = stk.enter_context(nc.psum_tensor("pbig2", [128, 2048], F32))

        def bank(i):
            pb = pbig1 if i < 4 else pbig2
            return pb[:, (i % 4) * 512:(i % 4 + 1) * 512]

        def bankpair(i):
            pb = pbig1 if i < 4 else pbig2
            return pb[:, (i % 4) * 512:(i % 4 + 2) * 512]

        sch = Sched(nc, stk)
        for s in ("ld", "lg", "tp", "st", "sg", "pe", "v", "s", "g"):
            sch.sem(s)
        bank_free = {i: None for i in range(8)}

        def v_op(builder, waits=(), inc="v"):
            return sch.emit("vector", builder, waits, inc)

        def s_op(builder, waits=(), inc="s"):
            return sch.emit("scalar", builder, waits, inc)

        def g_op(builder, waits=(), inc="g"):
            return sch.emit("gpsimd", builder, waits, inc)

        def t_mm(psum_ap, lhsT, rhs, start, stop, waits=(), inc=None):
            def b(h, o=psum_ap, l=lhsT, r=rhs, st=start, sp=stop):
                return h.matmul(o, l, r, start=st, stop=sp)
            if inc is None:
                return sch.emit("tensor", b, waits)
            sch.emit("tensor", b, waits)
            sch.emit("tensor", lambda h: h.drain())
            return sch.emit("tensor", lambda h: h.drain(), (), inc)

        def alloc_ln_slot(with_yT=True):
            d = {
                "xt": A.alloc(D, F32),
                "y32": A.alloc(D, F32),
                "yhi": A.alloc(D, BF16),
                "ylo": A.alloc(D, BF16),
                "st": A.alloc(16, F32),
            }
            if with_yT:
                d["yThi"] = A.alloc(D, BF16)
                d["yTlo"] = A.alloc(D, BF16)
            return d

        def emit_ln_tile(xsrc, t, B, prevtok=None, yT_dsts=None, yT_waits=(),
                         extra_x_waits=(), tag="z"):
            xt, y32, yhi, ylo, stt = (B["xt"], B["y32"], B["yhi"], B["ylo"],
                                      B["st"])
            tok = {}
            slot_free = [w for w in extra_x_waits if w is not None]
            if prevtok is not None:
                slot_free += [("v", prevtok["ylo"]), ("s", prevtok["yhi"])]
            tok["x"] = sch.dma("gpsimd", xt, xsrc[t * 128:(t + 1) * 128, :],
                               waits=slot_free, sem=tag + "x")
            s1n, s2, mn, msq, va, vv, sd, rs, nm = [stt[:, i:i + 1]
                                                    for i in range(9)]
            wl = [tok["x"]] + slot_free
            v_op(lambda h: h.reduce_sum(s1n, xt, axis=AX.X, negate=True), wl)
            tsq = s_op(lambda h: h.activation(y32, xt, AF.Square,
                                              accum_out=s2), wl)
            v_op(lambda h: h.tensor_scalar(mn, s1n, 1.0 / D, None, ALU.mult))
            v_op(lambda h: h.tensor_mul(msq, mn, mn))
            v_op(lambda h: h.tensor_scalar(va, s2, 1.0 / D, 1e-5, ALU.mult,
                                           ALU.add), [("s", tsq)])
            tvv = v_op(lambda h: h.tensor_sub(vv, va, msq))
            tsd = s_op(lambda h: h.activation(sd, vv, AF.Sqrt), [("v", tvv)])
            v_op(lambda h: h.reciprocal(rs, sd), [("s", tsd)])
            tnm = v_op(lambda h: h.tensor_mul(nm, mn, rs))
            whi = [("v", tnm)]
            if prevtok is not None:
                whi.append(prevtok["yThi"])
            tok["yhi"] = s_op(
                lambda h: h.activation(yhi, xt, AF.Identity, bias=nm,
                                       scale=rs), whi)
            tok["y32"] = v_op(
                lambda h: h.tensor_scalar(y32, xt, rs, nm, ALU.mult, ALU.add),
                [("s", tok["yhi"])])
            wlo = []
            if prevtok is not None:
                wlo.append(prevtok["yTlo"])
            tok["ylo"] = v_op(lambda h: h.tensor_sub(ylo, y32, yhi), wlo)
            if yT_dsts is None:
                yT_dsts = (blocks(B["yThi"], 8), blocks(B["yTlo"], 8))
            tok["yThi"] = sch.dmat(yT_dsts[0], blocks(yhi, 8),
                                   [("s", tok["yhi"])] + list(yT_waits),
                                   sem=tag + "h")
            tok["yTlo"] = sch.dmat(yT_dsts[1], blocks(ylo, 8),
                                   [("v", tok["ylo"])] + list(yT_waits),
                                   sem=tag + "l")
            return tok

        A.mark()
        wt = [A.alloc(FUV, F32), A.alloc(FUV, F32)]
        qh = A.alloc(FUV, BF16)
        ql = A.alloc(FUV, BF16)
        qr = A.alloc(FUV, BF16)
        qT = A.alloc(FUV, BF16)
        qT2 = A.alloc(FUV, BF16)
        acc = A.alloc(64, F32)
        prrow = A.alloc(128, F32, parts=1)
        sc1 = A.alloc(4, F32, parts=1)
        thr_rep = A.alloc(1, F32)
        nthr_rep = A.alloc(1, F32)

        def prep_ternary(wsrc, nt, cols, dst, mean_div, use_max, eps,
                         scal_idx):
            last_v = None
            for t in range(nt):
                sl = t % 2
                wld = sch.dma("gpsimd", wt[sl][:, :cols],
                              wsrc[t * 128:(t + 1) * 128, :],
                              waits=[("v", last_v)] if last_v else ())
                last_v = v_op(lambda h, t=t, sl=sl: h.reduce_sum(
                    acc[:, t:t + 1], wt[sl][:, :cols], axis=AX.X,
                    apply_absolute_value=True), [wld])
            tacc = v_op(lambda h: h.reduce_sum(acc[:, 63:64], acc[:, 0:nt],
                                               axis=AX.X), [("v", last_v)])
            tst = sch.dma("gpsimd", s_pr.rearrange("(p one) -> p one", one=1),
                          acc[:, 63:64], waits=[("v", tacc)], inc="st")
            tld = sch.dma("gpsimd", prrow, row(s_pr), waits=[tst])
            v_op(lambda h: h.reduce_sum(sc1[:, 0:1], prrow, axis=AX.X),
                 [tld])
            if use_max:
                v_op(lambda h: h.tensor_scalar(sc1[:, 1:2], sc1[:, 0:1],
                                               mean_div, None, ALU.mult))
                v_op(lambda h: h.tensor_scalar(sc1[:, 1:2], sc1[:, 1:2], eps,
                                               None, ALU.max))
            else:
                v_op(lambda h: h.tensor_scalar(sc1[:, 1:2], sc1[:, 0:1],
                                               mean_div, eps, ALU.mult,
                                               ALU.add))
            tth = v_op(lambda h: h.tensor_scalar(sc1[:, 2:3], sc1[:, 1:2], 0.5,
                                                 None, ALU.mult))
            tsc = sch.dma("gpsimd", row(s_scal[2 * scal_idx:2 * scal_idx + 2]),
                          sc1[:, 1:3], waits=[("v", tth)], inc="st")
            tthr = sch.dma("gpsimd", thr_rep,
                           row(s_scal[2 * scal_idx + 1:2 * scal_idx + 2])
                           .broadcast_to([128, 1]), waits=[tsc])
            tnthr = v_op(lambda h: h.tensor_scalar(nthr_rep, thr_rep, -1.0,
                                                   None, ALU.mult),
                         [tthr])
            nb = cols // 128
            last_q = tnthr
            last_tp = None
            last_st = None
            for t in range(nt):
                sl = t % 2
                wld = sch.dma("gpsimd", wt[sl][:, :cols],
                              wsrc[t * 128:(t + 1) * 128, :],
                              waits=[("v", last_q)])
                v_op(lambda h, sl=sl: h.tensor_scalar(
                    qh[:, :cols], wt[sl][:, :cols], thr_rep, None, ALU.is_gt),
                    [wld])
                v_op(lambda h, sl=sl: h.tensor_scalar(
                    ql[:, :cols], wt[sl][:, :cols], nthr_rep, None,
                    ALU.is_lt))
                tq = v_op(lambda h: h.tensor_sub(qr[:, :cols], qh[:, :cols],
                                                 ql[:, :cols]),
                          [last_tp] if last_tp else ())
                last_q = tq
                ttp = sch.dmat(blocks(qT[:, :cols], nb),
                               blocks(qr[:, :cols], nb),
                               [("v", tq), last_st])
                last_tp = ttp
                last_st = sch.dma("gpsimd", kpf(dst[:, t * 128:(t + 1) * 128]),
                                  blocks(qT[:, :cols], nb),
                                  waits=[ttp], sem="pq%d" % (t % 2))

        def prep_split(wsrc, nt, cols, dsth, dstl, transpose_to=False):
            last_lo = None
            last_hi_cons = None
            last_lo_cons = None
            last_st1 = None
            last_st2 = None
            for t in range(nt):
                sl = t % 2
                wld = sch.dma("gpsimd", wt[sl][:, :cols],
                              wsrc[t * 128:(t + 1) * 128, :],
                              waits=[("v", last_lo)] if last_lo else ())
                thi = s_op(lambda h, sl=sl: h.copy(qh[:, :cols],
                                                   wt[sl][:, :cols]),
                           [wld, last_hi_cons])
                tlo = v_op(lambda h, sl=sl: h.tensor_sub(
                    ql[:, :cols], wt[sl][:, :cols], qh[:, :cols]),
                    [("s", thi), last_lo_cons])
                last_lo = tlo
                nb = cols // 128
                if transpose_to:
                    t1 = sch.dmat(blocks(qT[:, :cols], nb),
                                  blocks(qh[:, :cols], nb),
                                  [("s", thi), last_st1])
                    t2 = sch.dmat(blocks(qT2[:, :cols], nb),
                                  blocks(ql[:, :cols], nb),
                                  [("v", tlo), last_st2])
                    last_hi_cons = t1
                    last_st1 = ("st", sch.dma(
                        "gpsimd", kpf(dsth[:, t * 128:(t + 1) * 128]),
                        blocks(qT[:, :cols], nb), waits=[t1],
                        inc="st"))
                    last_st2 = ("st", sch.dma(
                        "gpsimd", kpf(dstl[:, t * 128:(t + 1) * 128]),
                        blocks(qT2[:, :cols], nb), waits=[t2],
                        inc="st"))
                else:
                    t1 = sch.dma("gpsimd", dsth[t * 128:(t + 1) * 128, :],
                                 qh[:, :cols], waits=[("s", thi)],
                                 sem="ph%d" % (t % 2))
                    last_lo_cons = sch.dma(
                        "gpsimd", dstl[t * 128:(t + 1) * 128, :],
                        ql[:, :cols], waits=[("v", tlo)],
                        sem="pl%d" % (t % 2))
                    last_hi_cons = t1

        prep_ternary(inp["L0_ffn_W_fc"], 32, D, s_wfcqT, 1.0 / (HF * D),
                     False, 1e-6, 0)
        sch.barrier()
        prep_ternary(inp["L0_ffn_W_proj"], 8, HF, s_wprojqT, 1.0 / (HF * D),
                     False, 1e-6, 1)
        sch.barrier()
        prep_split(inp["L0_attn_W_uv"], 8, FUV, s_wuv_h, s_wuv_l)
        sch.barrier()
        prep_split(inp["L0_attn_W_out"], 16, D, s_wout_h, s_wout_l)
        sch.barrier()
        for w, src in (("q", "L1_attn_Wq"), ("k", "L1_attn_Wk"),
                       ("v", "L1_attn_Wv"), ("o", "L1_attn_Wo")):
            prep_split(inp[src], 8, D, s_wT[w][0], s_wT[w][1],
                       transpose_to=True)
            sch.barrier()
        prep_ternary(inp["L1_ffn_weight"], 8, D, s_wl1qT, 1.0 / (D * D),
                     True, 1e-5, 2)
        A.reset()
        sch.barrier()

        def ffn0_a(xsrc):
            A.mark()
            wfcq = A.alloc(8 * HF, BF16)
            lnw_rep = A.alloc(HF, F32)
            lnb = [alloc_ln_slot() for _ in range(2)]
            hbuf = A.alloc(HF, F32)
            hnc = A.alloc(HF, F32)
            hnhi = A.alloc(HF, BF16)
            hnlo = A.alloc(HF, BF16)
            hnThi = A.alloc(HF, BF16)
            hnTlo = A.alloc(HF, BF16)
            hsq = A.alloc(512, F32)
            hst = A.alloc(32, F32)
            twl = sch.dma("sync", blocks(wfcq, 8), kpf(s_wfcqT))
            tlnw = sch.dma("sync", lnw_rep,
                           row(inp["L0_ffn_ln_w"]).broadcast_to([128, HF]))
            prev = {}
            ptok = [None, None]
            for t in range(NT):
                sl = t % 2
                tok = emit_ln_tile(xsrc, t, lnb[sl], prevtok=ptok[sl],
                                   yT_waits=prev.get(("ytfree", sl), ()),
                                   tag="a%d" % sl)
                ptok[sl] = tok
                lastmm = None
                tsqr = None
                for c in range(8):
                    b = c % 2
                    bk = bank(b)
                    w0 = [tok["yThi"], tok["yTlo"],
                          twl, bank_free[b]]
                    for pi, yT in ((0, lnb[sl]["yThi"]), (1, lnb[sl]["yTlo"])):
                        for kd in range(8):
                            st_ = (pi == 0 and kd == 0)
                            sp_ = (pi == 1 and kd == 7)
                            lastmm = t_mm(
                                bk, yT[:, kd * 128:(kd + 1) * 128],
                                wfcq[:, kd * HF + c * 512:
                                     kd * HF + (c + 1) * 512],
                                st_, sp_, waits=w0 if st_ else (),
                                inc="pe" if sp_ else None)
                    gw = [("pe", lastmm)]
                    if c == 0 and ("hfree" in prev):
                        gw.append(prev["hfree"])
                    tg = s_op(lambda h, c=c, bk=bk: h.activation(
                        hbuf[:, c * 512:(c + 1) * 512], bk, AF.Gelu,
                        accum_out=hst[:, c:c + 1]), gw)
                    bank_free[b] = ("s", tg)
                    tsqr = s_op(lambda h, c=c: h.activation(
                        hsq, hbuf[:, c * 512:(c + 1) * 512], AF.Square,
                        accum_out=hst[:, 8 + c:9 + c]))
                sA, sB, mh, mq, vga, vv2, sdh, rsh, nmh = [
                    hst[:, 16 + i:17 + i] for i in range(9)]
                v_op(lambda h: h.reduce_sum(sA, hst[:, 0:8], axis=AX.X,
                                            negate=True), [("s", tsqr)])
                v_op(lambda h: h.reduce_sum(sB, hst[:, 8:16], axis=AX.X))
                v_op(lambda h: h.tensor_scalar(mh, sA, 1.0 / HF, None,
                                               ALU.mult))
                v_op(lambda h: h.tensor_mul(mq, mh, mh))
                v_op(lambda h: h.tensor_scalar(vga, sB, 1.0 / HF, 1e-5,
                                               ALU.mult, ALU.add))
                t2 = v_op(lambda h: h.tensor_sub(vv2, vga, mq))
                t3 = s_op(lambda h: h.activation(sdh, vv2, AF.Sqrt),
                          [("v", t2)])
                v_op(lambda h: h.reciprocal(rsh, sdh), [("s", t3)])
                t4 = v_op(lambda h: h.tensor_mul(nmh, mh, rsh))
                thnc = v_op(lambda h: h.tensor_scalar(hnc, hbuf, rsh, nmh,
                                                      ALU.mult, ALU.add),
                            [("v", t4), ("s", tsqr)])
                tp32 = v_op(lambda h: h.tensor_mul(hbuf, hnc, lnw_rep),
                            [tlnw])
                thh = s_op(lambda h: h.copy(hnhi, hbuf),
                           [("v", tp32), prev.get("tthi")])
                thl = v_op(lambda h: h.tensor_sub(hnlo, hbuf, hnhi),
                           [("s", thh), prev.get("ttlo")])
                prev["hfree"] = ("v", thl)
                tthi = sch.dmat(blocks(hnThi, 32), blocks(hnhi, 32),
                                [("s", thh), prev.get("s1")])
                ttlo = sch.dmat(blocks(hnTlo, 32), blocks(hnlo, 32),
                                [("v", thl), prev.get("s2")])
                prev["tthi"] = tthi
                prev["ttlo"] = ttlo
                s1 = sch.dma("gpsimd", kpf(s_hnT_h[:, t * 128:(t + 1) * 128]),
                             blocks(hnThi, 32), waits=[tthi],
                             inc="st")
                s2_ = sch.dma("gpsimd",
                              kpf(s_hnT_l[:, t * 128:(t + 1) * 128]),
                              blocks(hnTlo, 32), waits=[ttlo],
                              inc="st")
                prev["s1"] = s1
                prev["s2"] = s2_
                prev[("ytfree", sl)] = [("pe", lastmm)]
                if SAFE_MODE:
                    sch.barrier()
            A.reset()

        def ffn0_b(xsrc, xdst):
            A.mark()
            wpq = A.alloc(32 * D, BF16)
            lnbcol = A.alloc(32, F32)
            lnbrh = A.alloc(32 * 128, BF16)
            lnbrl = A.alloc(32 * 128, BF16)
            lnbr32 = A.alloc(32 * 128, F32)
            ones128 = A.alloc(128, F32)
            bc = A.alloc(D, F32, parts=1)
            bcrep = A.alloc(D, F32)
            t1buf = A.alloc(D, F32)
            slots = [{
                "hTh": A.alloc(HF, BF16),
                "hTl": A.alloc(HF, BF16),
                "xt": A.alloc(D, F32),
                "o": A.alloc(D, F32),
            } for _ in range(2)]
            twl = sch.dma("sync", blocks(wpq, 32), kpf(s_wprojqT))
            tlb = sch.dma("sync", lnbcol,
                          inp["L0_ffn_ln_b"].rearrange("(c p) -> p c", p=128),
                          slow=True)
            v_op(lambda h: h.memset(ones128, 1.0))
            t2 = None
            for kc in range(32):
                t2 = v_op(lambda h, kc=kc: h.tensor_scalar(
                    lnbr32[:, kc * 128:(kc + 1) * 128], ones128,
                    lnbcol[:, kc:kc + 1], None, ALU.mult), [tlb])
            th1 = s_op(lambda h: h.copy(lnbrh, lnbr32), [("v", t2)])
            th2 = v_op(lambda h: h.tensor_sub(lnbrl, lnbr32, lnbrh),
                       [("s", th1)])
            mmt = [None, None]
            for n in range(2):
                for pi, lb in ((0, lnbrh), (1, lnbrl)):
                    for kc in range(32):
                        st_ = (pi == 0 and kc == 0)
                        sp_ = (pi == 1 and kc == 31)
                        mmt[n] = t_mm(
                            bank(6 + n), lb[:, kc * 128:(kc + 1) * 128],
                            wpq[:, kc * D + n * 512: kc * D + (n + 1) * 512],
                            st_, sp_,
                            waits=([twl, ("v", th2),
                                    bank_free[6 + n]] if st_ else ()),
                            inc="pe" if sp_ else None)
            tb1 = v_op(lambda h: h.tensor_copy(bc[0:1, 0:512],
                                               bank(6)[0:1, :]),
                       [("pe", mmt[0])])
            tb2 = v_op(lambda h: h.tensor_copy(bc[0:1, 512:1024],
                                               bank(7)[0:1, :]),
                       [("pe", mmt[1])])
            bank_free[6] = ("v", tb1)
            bank_free[7] = ("v", tb2)
            tb3 = sch.dma("gpsimd",
                          row(s_bconst), bc,
                          waits=[("v", tb1), ("v", tb2)], inc="st")
            thb = sch.dma("gpsimd", bcrep,
                          row(s_bconst).broadcast_to([128, D]),
                          waits=[tb3])
            prev = {}
            for t in range(NT):
                sl = t % 2
                S = slots[sl]
                lw = prev.get(("free", sl), ())
                th = sch.dma("gpsimd", blocks(S["hTh"], 32),
                             kpf(s_hnT_h[:, t * 128:(t + 1) * 128]), waits=lw,
                             sem="bh%d" % sl)
                tl = sch.dma("gpsimd", blocks(S["hTl"], 32),
                             kpf(s_hnT_l[:, t * 128:(t + 1) * 128]),
                             sem="bl%d" % sl)
                tx = sch.dma("gpsimd", S["xt"],
                             xsrc[t * 128:(t + 1) * 128, :],
                             sem="bx%d" % sl)
                pb = 0 if sl == 0 else 2
                opsum = bankpair(pb)
                mmtok = None
                for pi in range(2):
                    hT = (S["hTh"], S["hTl"])[pi]
                    for kc in range(32):
                        for n in range(2):
                            st_ = (pi == 0 and kc == 0)
                            sp_ = (pi == 1 and kc == 31)
                            w = ()
                            if st_:
                                w = [th, tl, twl,
                                     bank_free[pb + n]]
                            mmtok = t_mm(
                                opsum[:, n * 512:(n + 1) * 512],
                                hT[:, kc * 128:(kc + 1) * 128],
                                wpq[:, kc * D + n * 512:
                                    kc * D + (n + 1) * 512],
                                st_, sp_, waits=w,
                                inc="pe" if (sp_ and n == 1) else None)
                te1 = v_op(lambda h, S=S, opsum=opsum: h.scalar_tensor_tensor(
                    t1buf, opsum, 0.5, S["xt"], ALU.mult, ALU.add),
                    [("pe", mmtok), tx, thb,
                     prev.get("t1free"), prev.get(("sfree", sl))])
                bank_free[pb] = ("v", te1)
                bank_free[pb + 1] = ("v", te1)
                te1 = v_op(lambda h, S=S: h.scalar_tensor_tensor(
                    S["o"], bcrep, 0.5, t1buf, ALU.mult, ALU.add))
                prev["t1free"] = ("v", te1)
                tst = sch.dma("gpsimd", xdst[t * 128:(t + 1) * 128, :],
                              S["o"], waits=[("v", te1)], sem="bs%d" % sl)
                prev[("free", sl)] = [("pe", mmtok), ("v", te1)]
                prev[("sfree", sl)] = tst
                if SAFE_MODE:
                    sch.barrier()
            A.reset()

        def l0a_a(xsrc):
            A.mark()
            wuvh = A.alloc(8 * FUV, BF16)
            wuvl = A.alloc(8 * FUV, BF16)
            buvcol = A.alloc(33, F32)
            gam = A.alloc(2, F32)
            bet = A.alloc(2, F32)
            lnb = [alloc_ln_slot(with_yT=False)]
            yTh_m = A.alloc(8 * 512, BF16)
            yTl_m = A.alloc(8 * 512, BF16)
            s32 = A.alloc(512, F32)
            shi = A.alloc(512, BF16)
            slo = A.alloc(512, BF16)
            sThi = A.alloc(512, BF16)
            sTlo = A.alloc(512, BF16)
            q32 = A.alloc(512, F32)
            k32 = A.alloc(512, F32)
            qhi = A.alloc(512, BF16)
            qlo = A.alloc(512, BF16)
            khi = A.alloc(512, BF16)
            klo = A.alloc(512, BF16)
            twh = sch.dma("sync", blocks(wuvh, 8), kpf(s_wuv_h))
            twl = sch.dma("sync", blocks(wuvl, 8), kpf(s_wuv_l))
            tbc = sch.dma("sync", buvcol,
                          inp["L0_attn_b_uv"].rearrange("(c p) -> p c", p=128),
                          slow=True)
            tgm = sch.dma("sync", gam,
                          inp["L0_attn_gamma"].rearrange("g p -> p g"),
                          slow=True)
            tbt = sch.dma("sync", bet,
                          inp["L0_attn_beta"].rearrange("g p -> p g"),
                          slow=True)
            prev = {}
            ptok = [None]
            for mc in range(4):
                mtoks = []
                for si in range(4):
                    t = mc * 4 + si
                    tok = emit_ln_tile(
                        xsrc, t, lnb[0], prevtok=ptok[0],
                        yT_dsts=(blocks(yTh_m, 8)
                                 [:, :, si * 128:(si + 1) * 128],
                                 blocks(yTl_m, 8)
                                 [:, :, si * 128:(si + 1) * 128]),
                        yT_waits=prev.get("ymfree", ()), tag="c")
                    ptok[0] = tok
                    mtoks.append(tok)
                wmac = [twh, twl, tbc]
                for tok in mtoks:
                    wmac += [tok["yThi"], tok["yTlo"]]
                lastmm = None
                for ft in range(33):
                    b = ft % 8
                    pairs = ((wuvh, yTh_m), (wuvh, yTl_m), (wuvl, yTh_m))
                    for pi, (wv, yv) in enumerate(pairs):
                        for kd in range(8):
                            st_ = (pi == 0 and kd == 0)
                            sp_ = (pi == 2 and kd == 7)
                            lastmm = t_mm(
                                bank(b),
                                wv[:, kd * FUV + ft * 128:
                                   kd * FUV + (ft + 1) * 128],
                                yv[:, kd * 512:(kd + 1) * 512], st_, sp_,
                                waits=(wmac + [bank_free[b]]) if st_ else (),
                                inc="pe" if sp_ else None)
                    tbs = s_op(lambda h, b=b, ft=ft: h.activation(
                        s32, bank(b), AF.Silu, bias=buvcol[:, ft:ft + 1],
                        scale=1.0),
                        [("pe", lastmm), prev.get("s32free")])
                    bank_free[b] = ("s", tbs)
                    if ft == 32:
                        tq32 = s_op(lambda h: h.activation(
                            q32, s32, AF.Identity, bias=bet[:, 0:1],
                            scale=gam[:, 0:1]),
                            [tgm, tbt,
                             prev.get("q32free")])
                        tk32 = s_op(lambda h: h.activation(
                            k32, s32, AF.Identity, bias=bet[:, 1:2],
                            scale=gam[:, 1:2]), [prev.get("k32free")])
                        prev["s32free"] = ("s", tk32)
                        tqh = s_op(lambda h: h.copy(qhi, q32),
                                   [prev.get("sp_qh")])
                        tql = v_op(lambda h: h.tensor_sub(qlo, q32, qhi),
                                   [("s", tqh), prev.get("sp_ql")])
                        tkh = s_op(lambda h: h.copy(khi, k32),
                                   [prev.get("sp_kh")])
                        tkl = v_op(lambda h: h.tensor_sub(klo, k32, khi),
                                   [("s", tkh), prev.get("sp_kl")])
                        prev["q32free"] = ("v", tql)
                        prev["k32free"] = ("v", tkl)
                        cs = slice(mc * 512, (mc + 1) * 512)
                        for nm_, buf, tk, eng, dst in (
                                ("sp_qh", qhi, tqh, "s", s_qh),
                                ("sp_ql", qlo, tql, "v", s_ql),
                                ("sp_kh", khi, tkh, "s", s_kh),
                                ("sp_kl", klo, tkl, "v", s_kl)):
                            ts_ = sch.dma("gpsimd", dst[:, cs], buf,
                                          waits=[(eng, tk)], sem=nm_)
                            prev[nm_] = ts_
                    else:
                        tsh = s_op(lambda h: h.copy(shi, s32),
                                   [prev.get("shifree")])
                        tsl = v_op(lambda h: h.tensor_sub(slo, s32, shi),
                                   [("s", tsh), prev.get("slofree")])
                        prev["s32free"] = ("v", tsl)
                        tt1 = sch.dmat(blocks(sThi, 4), blocks(shi, 4),
                                       [("s", tsh), prev.get("sthfree")])
                        tt2 = sch.dmat(blocks(sTlo, 4), blocks(slo, 4),
                                       [("v", tsl), prev.get("stlfree")])
                        prev["shifree"] = tt1
                        prev["slofree"] = tt2
                        if ft < 16:
                            dh, dl, col = s_u_h, s_u_l, ft * 128
                        else:
                            dh, dl, col = s_v_h, s_v_l, (ft - 16) * 128
                        rows = slice(mc * 512, (mc + 1) * 512)
                        su1 = sch.dma(
                            "gpsimd",
                            dh[rows, col:col + 128]
                            .rearrange("(b p) f -> p b f", p=128),
                            blocks(sThi, 4), waits=[tt1], inc="st")
                        su2 = sch.dma(
                            "gpsimd",
                            dl[rows, col:col + 128]
                            .rearrange("(b p) f -> p b f", p=128),
                            blocks(sTlo, 4), waits=[tt2], inc="st")
                        prev["sthfree"] = su1
                        prev["stlfree"] = su2
                prev["ymfree"] = [("pe", lastmm)]
                if SAFE_MODE:
                    sch.barrier()
            A.reset()

        def l0a_b():
            A.mark()
            vh_sb = A.alloc(16 * E, BF16)
            vl_sb = A.alloc(16 * E, BF16)
            qh_sb = A.alloc(T, BF16)
            ql_sb = A.alloc(T, BF16)
            kh_sb = A.alloc(T, BF16)
            kl_sb = A.alloc(T, BF16)
            r32 = A.alloc(E, F32)
            uav = A.alloc(E, F32)
            sq32 = uav
            sqhi = A.alloc(E, BF16)
            sqlo = A.alloc(E, BF16)
            sqThi = A.alloc(E, BF16)
            sqTlo = A.alloc(E, BF16)
            uth = A.alloc(E, BF16)
            utl = A.alloc(E, BF16)
            ut32 = r32
            uavhi = A.alloc(E, BF16)
            uavlo = A.alloc(E, BF16)
            uavThi = A.alloc(E, BF16)
            uavTlo = A.alloc(E, BF16)
            tv1 = sch.dma("sync", blocks(vh_sb, 16),
                          s_v_h.rearrange("(c p) e -> p c e", p=128))
            tv2 = sch.dma("sync", blocks(vl_sb, 16),
                          s_v_l.rearrange("(c p) e -> p c e", p=128))
            tq1 = sch.dma("sync", qh_sb, s_qh)
            tq2 = sch.dma("sync", ql_sb, s_ql)
            tk1 = sch.dma("sync", kh_sb, s_kh)
            tk2 = sch.dma("sync", kl_sb, s_kl)
            wload = [x for x in (tv1, tv2, tq1, tq2, tk1, tk2)]
            prev = {}
            for t in range(NT):
                tul1 = sch.dma("gpsimd", uth,
                               s_u_h[t * 128:(t + 1) * 128, :],
                               waits=prev.get("ufree", ()))
                tul2 = sch.dma("gpsimd", utl,
                               s_u_l[t * 128:(t + 1) * 128, :])
                lastmm = None
                qsl = slice(t * 128, (t + 1) * 128)
                pairs = ((qh_sb, kh_sb), (qh_sb, kl_sb), (ql_sb, kh_sb))
                for ms in range(4):
                    for pi, (qv, kv) in enumerate(pairs):
                        st_ = (pi == 0)
                        sp_ = (pi == 2)
                        lastmm = t_mm(
                            bank(ms), qv[:, qsl],
                            kv[:, ms * 512:(ms + 1) * 512], st_, sp_,
                            waits=(wload + [bank_free[ms]]) if st_ else (),
                            inc="pe" if (sp_ and ms == 3) else None)
                tr = None
                for bb in range(4):
                    tr = s_op(lambda h, bb=bb: h.activation(
                        r32[:, bb * 512:(bb + 1) * 512], bank(bb),
                        AF.Relu, scale=1.0 / T),
                        [("pe", lastmm), prev.get("r32free"),
                         prev.get("ut32free")] if bb == 0 else ())

                for ms in range(4):
                    bank_free[ms] = ("s", tr)
                tsq = s_op(lambda h: h.activation(sq32, r32, AF.Square),
                           [prev.get("sqfree")])
                prev["r32free"] = ("s", tsq)
                tsh = s_op(lambda h: h.copy(sqhi, sq32), [prev.get("shfree")])
                tsl = v_op(lambda h: h.tensor_sub(sqlo, sq32, sqhi),
                           [("s", tsh), prev.get("slfree")])
                prev["sqfree"] = ("v", tsl)

                t1 = sch.dmat(blocks(sqThi, 16), blocks(sqhi, 16),
                              [("s", tsh), prev.get("avdone")])
                t2 = sch.dmat(blocks(sqTlo, 16), blocks(sqlo, 16),
                              [("v", tsl), prev.get("avdone")])

                prev["shfree"] = t1
                prev["slfree"] = t2
                lastmm = None
                pairs = ((sqThi, vh_sb), (sqThi, vl_sb), (sqTlo, vh_sb))
                for es in range(4):
                    for pi, (sv, vv) in enumerate(pairs):
                        for kc in range(16):
                            st_ = (pi == 0 and kc == 0)
                            sp_ = (pi == 2 and kc == 15)
                            lastmm = t_mm(
                                bank(4 + es), sv[:, kc * 128:(kc + 1) * 128],
                                vv[:, kc * E + es * 512:
                                   kc * E + (es + 1) * 512], st_, sp_,
                                waits=([t1, t2,
                                        bank_free[4 + es]] if st_ else ()),
                                inc="pe" if (sp_ and es == 3) else None)
                prev["avdone"] = ("pe", lastmm)
                tadd = v_op(lambda h: h.tensor_add(ut32, uth, utl),
                            [tul1, tul2, ("s", tsq)])
                tua = None
                for bb in range(4):
                    tua = v_op(lambda h, bb=bb: h.tensor_mul(
                        uav[:, bb * 512:(bb + 1) * 512],
                        ut32[:, bb * 512:(bb + 1) * 512], bank(4 + bb)),
                        [("pe", lastmm), prev.get("uavfree"),
                         prev.get("tuhtok"), ("s", tsh), ("v", tadd)]
                        if bb == 0 else ())
                for es in range(4):
                    bank_free[4 + es] = ("v", tua)
                prev["ufree"] = [("v", tadd)]
                prev["ut32free"] = ("v", tua)
                tuh = s_op(lambda h: h.copy(uavhi, uav),
                           [("v", tua), prev.get("uhfree")])
                prev["tuhtok"] = ("s", tuh)
                tul2 = v_op(lambda h: h.tensor_sub(uavlo, uav, uavhi),
                            [("s", tuh), prev.get("ulfree")])
                prev["uavfree"] = ("v", tul2)
                prev["sqfree"] = [("v", tul2), ("s", tuh)]
                t3 = sch.dmat(blocks(uavThi, 16), blocks(uavhi, 16),
                              [("s", tuh), prev.get("uths")])
                t4 = sch.dmat(blocks(uavTlo, 16), blocks(uavlo, 16),
                              [("v", tul2), prev.get("utls")])
                prev["uhfree"] = t3
                prev["ulfree"] = t4
                su1 = sch.dma("gpsimd",
                              kpf(s_uavT_h[:, t * 128:(t + 1) * 128]),
                              blocks(uavThi, 16), waits=[t3],
                              inc="st")
                su2 = sch.dma("gpsimd",
                              kpf(s_uavT_l[:, t * 128:(t + 1) * 128]),
                              blocks(uavTlo, 16), waits=[t4],
                              inc="st")
                prev["uths"] = su1
                prev["utls"] = su2
                if SAFE_MODE:
                    sch.barrier()
            A.reset()

        def l0a_c(xsrc, xdst):
            A.mark()
            wouth = A.alloc(16 * D, BF16)
            woutl = A.alloc(16 * D, BF16)
            borep = A.alloc(D, F32)
            t1b = A.alloc(D, F32)
            slots = [{
                "uTh": A.alloc(E, BF16),
                "uTl": A.alloc(E, BF16),
                "xt": A.alloc(D, F32),
                "o": A.alloc(D, F32),
            } for _ in range(2)]
            tw1 = sch.dma("sync", blocks(wouth, 16), kpf(s_wout_h))
            tw2 = sch.dma("sync", blocks(woutl, 16), kpf(s_wout_l))
            tb0 = sch.dma("sync", borep,
                          row(inp["L0_attn_b_out"]).broadcast_to([128, D]))
            prev = {}
            for t in range(NT):
                sl = t % 2
                S = slots[sl]
                lw = prev.get(("free", sl), ())
                th = sch.dma("gpsimd", blocks(S["uTh"], 16),
                             kpf(s_uavT_h[:, t * 128:(t + 1) * 128]),
                             waits=lw, sem="eh%d" % sl)
                tl = sch.dma("gpsimd", blocks(S["uTl"], 16),
                             kpf(s_uavT_l[:, t * 128:(t + 1) * 128]),
                             sem="el%d" % sl)
                tx = sch.dma("gpsimd", S["xt"],
                             xsrc[t * 128:(t + 1) * 128, :],
                             sem="ex%d" % sl)
                pb = 0 if sl == 0 else 2
                opsum = bankpair(pb)
                mmtok = None
                pairs = ((S["uTh"], wouth), (S["uTh"], woutl),
                         (S["uTl"], wouth))
                for pi, (uv, wv) in enumerate(pairs):
                    for kc in range(16):
                        for n in range(2):
                            st_ = (pi == 0 and kc == 0)
                            sp_ = (pi == 2 and kc == 15)
                            w = ()
                            if st_:
                                w = [th, tl, tw1,
                                     tw2, bank_free[pb + n]]
                            mmtok = t_mm(
                                opsum[:, n * 512:(n + 1) * 512],
                                uv[:, kc * 128:(kc + 1) * 128],
                                wv[:, kc * D + n * 512:
                                   kc * D + (n + 1) * 512],
                                st_, sp_, waits=w,
                                inc="pe" if (sp_ and n == 1) else None)
                te0 = v_op(lambda h, S=S, opsum=opsum: h.scalar_tensor_tensor(
                    t1b, opsum, 1.0, S["xt"], ALU.mult, ALU.add),
                    [("pe", mmtok), tx, tb0,
                     prev.get("t1bfree"), prev.get(("sfree", sl))])
                bank_free[pb] = ("v", te0)
                bank_free[pb + 1] = ("v", te0)
                te1 = v_op(lambda h, S=S: h.tensor_add(S["o"], t1b, borep))
                prev["t1bfree"] = ("v", te1)
                tst = sch.dma("gpsimd", xdst[t * 128:(t + 1) * 128, :],
                              S["o"], waits=[("v", te1)], sem="bs%d" % sl)
                prev[("free", sl)] = [("pe", mmtok), ("v", te1)]
                prev[("sfree", sl)] = tst
                if SAFE_MODE:
                    sch.barrier()
            A.reset()

        def l1_ffn(xsrc, xdst):
            A.mark()
            wq = A.alloc(8 * D, BF16)
            hsc = A.alloc(1, F32)
            lnb = [alloc_ln_slot() for _ in range(2)]
            obuf = [A.alloc(D, F32) for _ in range(2)]
            twl = sch.dma("sync", blocks(wq, 8), kpf(s_wl1qT))
            thsc = sch.dma("sync", hsc,
                           row(s_scal[5:6]).broadcast_to([128, 1]))
            prev = {}
            ptok = [None, None]
            for t in range(NT):
                sl = t % 2
                tok = emit_ln_tile(xsrc, t, lnb[sl], prevtok=ptok[sl],
                                   yT_waits=prev.get(("ytfree", sl), ()),
                                   extra_x_waits=[prev.get(("xlate", sl))],
                                   tag="f%d" % sl)
                ptok[sl] = tok
                pb = 0 if sl == 0 else 2
                opsum = bankpair(pb)
                mmtok = None
                for pi in range(2):
                    yT = (lnb[sl]["yThi"], lnb[sl]["yTlo"])[pi]
                    for kd in range(8):
                        for n in range(2):
                            st_ = (pi == 0 and kd == 0)
                            sp_ = (pi == 1 and kd == 7)
                            w = ()
                            if st_:
                                w = [tok["yThi"], tok["yTlo"],
                                     twl, bank_free[pb + n]]
                            mmtok = t_mm(
                                opsum[:, n * 512:(n + 1) * 512],
                                yT[:, kd * 128:(kd + 1) * 128],
                                wq[:, kd * D + n * 512:
                                   kd * D + (n + 1) * 512],
                                st_, sp_, waits=w,
                                inc="pe" if (sp_ and n == 1) else None)
                te1 = v_op(lambda h, sl=sl, opsum=opsum:
                           h.scalar_tensor_tensor(obuf[sl], opsum, hsc,
                                                  lnb[sl]["xt"], ALU.mult,
                                                  ALU.add),
                           [("pe", mmtok), thsc,
                            prev.get(("sfree", sl))])
                bank_free[pb] = ("v", te1)
                bank_free[pb + 1] = ("v", te1)
                tst = sch.dma("gpsimd", xdst[t * 128:(t + 1) * 128, :],
                              obuf[sl], waits=[("v", te1)], sem="fs%d" % sl)
                prev[("ytfree", sl)] = [("pe", mmtok)]
                prev[("sfree", sl)] = tst
                prev[("xlate", sl)] = ("v", te1)
                if SAFE_MODE:
                    sch.barrier()
            A.reset()

        def l1a_a(xsrc):
            A.mark()
            wq = {}
            for w in ("q", "k", "v"):
                wq[w] = (A.alloc(8 * D, BF16), A.alloc(8 * D, BF16))
            lnb = [alloc_ln_slot(with_yT=False) for _ in range(2)]
            yTh_m = A.alloc(8 * 512, BF16)
            yTl_m = A.alloc(8 * 512, BF16)
            eh = A.alloc(512, BF16)
            el = A.alloc(512, BF16)
            vh = A.alloc(D, BF16)
            vl = A.alloc(D, BF16)
            tld = {}
            for w in ("q", "k", "v"):
                tld[w] = (sch.dma("sync", blocks(wq[w][0], 8),
                                  kpf(s_wT[w][0]), sem="w%sh" % w),
                          sch.dma("sync", blocks(wq[w][1], 8),
                                  kpf(s_wT[w][1]), sem="w%sl" % w))
            dstmap = {"q": (s_q1T_h, s_q1T_l), "k": (s_k1T_h, s_k1T_l)}
            prev = {}
            ptok = [None, None]
            for mc in range(4):
                mtoks = []
                for si in range(4):
                    t = mc * 4 + si
                    sl = t % 2
                    tok = emit_ln_tile(
                        xsrc, t, lnb[sl], prevtok=ptok[sl],
                        yT_dsts=(blocks(yTh_m, 8)
                                 [:, :, si * 128:(si + 1) * 128],
                                 blocks(yTl_m, 8)
                                 [:, :, si * 128:(si + 1) * 128]),
                        yT_waits=prev.get("ymfree", ()), tag="g%d" % sl)
                    ptok[sl] = tok
                    mtoks.append(tok)
                wmac = []
                for tok in mtoks:
                    wmac += [tok["yThi"], tok["yTlo"]]
                lastmm = None
                for w in ("q", "k"):
                    wh, wl_ = wq[w]
                    dh, dl = dstmap[w]
                    for mt in range(8):
                        b = mt % 4
                        pairs = ((wh, yTh_m), (wh, yTl_m), (wl_, yTh_m))
                        for pi, (wv, yv) in enumerate(pairs):
                            for kd in range(8):
                                st_ = (pi == 0 and kd == 0)
                                sp_ = (pi == 2 and kd == 7)
                                lastmm = t_mm(
                                    bank(b),
                                    wv[:, kd * D + mt * 128:
                                       kd * D + (mt + 1) * 128],
                                    yv[:, kd * 512:(kd + 1) * 512], st_, sp_,
                                    waits=(wmac + [tld[w][0],
                                                   tld[w][1],
                                                   bank_free[b]])
                                    if st_ else (),
                                    inc="pe" if sp_ else None)
                        teh = s_op(lambda h, b=b: h.copy(eh, bank(b)),
                                   [("pe", lastmm), prev.get("ehfree")])
                        tel = v_op(lambda h, b=b: h.tensor_sub(el, bank(b),
                                                               eh),
                                   [("s", teh), prev.get("elfree")])
                        bank_free[b] = ("v", tel)
                        se1 = sch.dma("gpsimd",
                                      dh[mt * 128:(mt + 1) * 128,
                                         mc * 512:(mc + 1) * 512], eh,
                                      waits=[("s", teh)], inc="st")
                        se2 = sch.dma("gpsimd",
                                      dl[mt * 128:(mt + 1) * 128,
                                         mc * 512:(mc + 1) * 512], el,
                                      waits=[("v", tel)], inc="st")
                        prev["ehfree"] = [se1, ("v", tel)]
                        prev["elfree"] = se2
                wh, wl_ = wq["v"]
                for si in range(4):
                    t = mc * 4 + si
                    pb = 4 if si % 2 == 0 else 6
                    opsum = bankpair(pb)
                    lastmm = None
                    pairs = ((yTh_m, wh), (yTh_m, wl_), (yTl_m, wh))
                    for pi, (yv, wv) in enumerate(pairs):
                        for kd in range(8):
                            for n in range(2):
                                st_ = (pi == 0 and kd == 0)
                                sp_ = (pi == 2 and kd == 7)
                                w_ = ()
                                if st_:
                                    w_ = wmac + [tld["v"][0],
                                                 tld["v"][1],
                                                 bank_free[pb + n]]
                                lastmm = t_mm(
                                    opsum[:, n * 512:(n + 1) * 512],
                                    yv[:, kd * 512 + si * 128:
                                       kd * 512 + (si + 1) * 128],
                                    wv[:, kd * D + n * 512:
                                       kd * D + (n + 1) * 512],
                                    st_, sp_, waits=w_,
                                    inc="pe" if (sp_ and n == 1) else None)
                    tvh = s_op(lambda h, opsum=opsum: h.copy(vh, opsum),
                               [("pe", lastmm), prev.get("vhfree")])
                    tvl = v_op(lambda h, opsum=opsum: h.tensor_sub(vl, opsum,
                                                                   vh),
                               [("s", tvh), prev.get("vlfree")])
                    bank_free[pb] = ("v", tvl)
                    bank_free[pb + 1] = ("v", tvl)
                    sv1 = sch.dma("gpsimd", s_v1_h[t * 128:(t + 1) * 128, :],
                                  vh, waits=[("s", tvh)], inc="st")
                    sv2 = sch.dma("gpsimd", s_v1_l[t * 128:(t + 1) * 128, :],
                                  vl, waits=[("v", tvl)], inc="st")
                    prev["vhfree"] = [sv1, ("v", tvl)]
                    prev["vlfree"] = sv2
                prev["ymfree"] = [("pe", lastmm)]
                if SAFE_MODE:
                    sch.barrier()
            A.reset()

        def l1a_b(xsrc, xdst):
            A.mark()
            k1h = A.alloc(8 * T, BF16)
            k1l = A.alloc(8 * T, BF16)
            v1h = A.alloc(16 * D, BF16)
            v1l = A.alloc(16 * D, BF16)
            woh = A.alloc(8 * D, BF16)
            wol = A.alloc(8 * D, BF16)
            m_mid = A.alloc(512, F32)
            m_r1 = A.alloc(384, F32)
            m_r15 = A.alloc(384, F32)
            m_r0a = A.alloc(512, F32)
            m_r0b = A.alloc(512, F32)
            qrh = A.alloc(D, BF16)
            qrl = A.alloc(D, BF16)
            xt = A.alloc(D, F32)
            sm32 = A.alloc(T, F32)
            e32 = A.alloc(512, F32)
            phi = A.alloc(512, BF16)
            plo = A.alloc(512, BF16)
            pThi = A.alloc(512, BF16)
            pTlo = A.alloc(512, BF16)
            ao = A.alloc(D, F32)
            aohi = A.alloc(D, BF16)
            aolo = A.alloc(D, BF16)
            aoThi = A.alloc(D, BF16)
            aoTlo = A.alloc(D, BF16)
            ob = ao
            sst = A.alloc(4, F32)
            tk1 = sch.dma("sync", blocks(k1h, 8), kpf(s_k1T_h))
            tk2 = sch.dma("sync", blocks(k1l, 8), kpf(s_k1T_l))
            tv1 = sch.dma("sync", blocks(v1h, 16),
                          s_v1_h.rearrange("(c p) e -> p c e", p=128))
            tv2 = sch.dma("sync", blocks(v1l, 16),
                          s_v1_l.rearrange("(c p) e -> p c e", p=128))
            tw1 = sch.dma("sync", blocks(woh, 8), kpf(s_wT["o"][0]))
            tw2 = sch.dma("sync", blocks(wol, 8), kpf(s_wT["o"][1]))
            wres = [x for x in (tk1, tk2, tv1, tv2, tw1, tw2)]

            tmask = None

            def mk_mask(ap, sels):
                nonlocal tmask
                tmask = g_op(lambda h, ap=ap: h.memset(ap, 0.0))
                for (cm, coef, base, fill) in sels:
                    tmask = g_op(lambda h, ap=ap, cm=cm, coef=coef, base=base,
                                 fill=fill: h.affine_select(
                                     out=ap, in_=ap, compare_op=ALU.is_ge,
                                     fill=fill, base=base,
                                     pattern=[[coef, ap.shape[-1]]],
                                     channel_multiplier=cm))

            mk_mask(m_mid[:, 0:128], [(0, -1, 0, NEG)])
            mk_mask(m_mid[:, 128:256], [(-1, 1, 0, NEG)])
            mk_mask(m_mid[:, 256:384], [])
            mk_mask(m_mid[:, 384:512], [(1, -1, 0, NEG)])
            mk_mask(m_r1[:, 0:128], [(-1, 1, 0, NEG), (0, 1, -1, 0.0)])
            mk_mask(m_r1[:, 128:256], [])
            mk_mask(m_r1[:, 256:384], [(1, -1, 0, NEG)])
            mk_mask(m_r15[:, 0:128], [(0, -1, 0, NEG)])
            mk_mask(m_r15[:, 128:256], [(-1, 1, 0, NEG)])
            mk_mask(m_r15[:, 256:384], [])
            mk_mask(m_r0a[:, 0:128], [])
            mk_mask(m_r0a[:, 128:256], [(1, -1, 0, NEG), (1, 0, -1, 0.0)])
            mk_mask(m_r0a[:, 256:512], [(-1, 0, 0, NEG)])
            mk_mask(m_r0b, [(-1, 0, 0, NEG)])
            wmask = ("g", tmask)

            nmx = sst[:, 0:1]
            sume = sst[:, 1:2]
            rcp = sst[:, 2:3]
            scols = A.alloc(4, F32)
            prev = {}
            for r in range(NT):
                tq1 = sch.dma("gpsimd", blocks(qrh, 8),
                              kpf(s_q1T_h[:, r * 128:(r + 1) * 128]),
                              waits=prev.get("qfree", ()))
                tq2 = sch.dma("gpsimd", blocks(qrl, 8),
                              kpf(s_q1T_l[:, r * 128:(r + 1) * 128]))
                txl = sch.dma("gpsimd", xt, xsrc[r * 128:(r + 1) * 128, :],
                              waits=prev.get("xfree", ()))
                if r == 0:
                    W = 2048
                    mbs = list(range(16))
                    rhs_slices = [(i * 512, 512, i * 512) for i in range(4)]
                elif r == 1:
                    W = 384
                    mbs = [0, 1, 2]
                    rhs_slices = [(0, 384, 0)]
                elif r == 15:
                    W = 384
                    mbs = [0, 14, 15]
                    rhs_slices = [(0, 128, 0), (14 * 128, 256, 128)]
                else:
                    W = 512
                    mbs = [0, r - 1, r, r + 1]
                    rhs_slices = [(0, 128, 0), ((r - 1) * 128, 384, 128)]
                pieces = []
                for pi in range(3):
                    for (mcol, wdt, pcol) in rhs_slices:
                        pieces.append((pi, mcol, wdt, pcol))
                first_by_bank = {}
                last_by_bank = {}
                for idx, (pi, mc2, wd2, pc2) in enumerate(pieces):
                    b = pc2 // 512
                    if b not in first_by_bank:
                        first_by_bank[b] = idx
                    last_by_bank[b] = idx
                lastao = None
                for hd_ in range(NH):
                    c = hd_ // 2
                    rw = slice((hd_ % 2) * 64, (hd_ % 2) * 64 + 64)
                    if r == 0:
                        ps = pbig1[:, 0:2048]
                        pbanks = [0, 1, 2, 3]
                    else:
                        ps = bank(hd_ % 2)[:, 0:W]
                        pbanks = [hd_ % 2]
                    qpairs = ((qrh, k1h), (qrh, k1l), (qrl, k1h))
                    lastmm = None
                    for idx, (pi, mc2, wd2, pc2) in enumerate(pieces):
                        qv, kv = qpairs[pi]
                        b = pc2 // 512
                        st_ = (idx == first_by_bank[b])
                        sp_ = (idx == last_by_bank[b])
                        w_ = ()
                        if idx == 0:
                            w_ = (wres + [tq1, tq2] +
                                  [bank_free[bb] for bb in pbanks])
                        lastmm = t_mm(
                            ps[:, pc2:pc2 + wd2],
                            qv[rw, c * 128:(c + 1) * 128],
                            kv[rw, c * T + mc2: c * T + mc2 + wd2],
                            st_, sp_, waits=w_,
                            inc="pe" if idx == len(pieces) - 1 else None)
                    tsm = None
                    smw = [("pe", lastmm), wmask, prev.get("smfree")]
                    if r == 0:
                        for ck in range(4):
                            mk = m_r0a if ck == 0 else m_r0b
                            tsm = v_op(lambda h, ck=ck, mk=mk:
                                       h.scalar_tensor_tensor(
                                           sm32[:, ck * 512:(ck + 1) * 512],
                                           pbig1[:, ck * 512:(ck + 1) * 512],
                                           0.125, mk, ALU.mult, ALU.add),
                                       smw if ck == 0 else ())
                    else:
                        mk = m_r1 if r == 1 else (m_r15 if r == 15 else m_mid)
                        tsm = v_op(lambda h, ps=ps, mk=mk, W=W:
                                   h.scalar_tensor_tensor(
                                       sm32[:, 0:W], ps, 0.125, mk[:, 0:W],
                                       ALU.mult, ALU.add), smw)
                    for bb in pbanks:
                        bank_free[bb] = ("v", tsm)
                    tnm = v_op(lambda h, W=W: h.reduce_max(
                        nmx, sm32[:, 0:W], axis=AX.X, negate=True),
                        [prev.get("nmxfree")])
                    nch = max(1, W // 512)
                    avb = 4 + (hd_ % 2)
                    avp = bank(avb)[:, 0:64]
                    ppairs = ((pThi, v1h), (pThi, v1l), (pTlo, v1h))
                    lastav = None
                    texp = None
                    for ck in range(nch):
                        cw = min(512, W)
                        texp = s_op(lambda h, ck=ck, cw=cw: h.activation(
                            e32[:, 0:cw], sm32[:, ck * 512:ck * 512 + cw],
                            AF.Exp, bias=nmx,
                            accum_out=scols[:, ck:ck + 1]),
                            [("v", tnm), prev.get("sumefree"),
                             prev.get("e32free")])
                        tph = s_op(lambda h, cw=cw: h.copy(phi[:, 0:cw],
                                                           e32[:, 0:cw]),
                                   [prev.get("phifree")])
                        tpl = v_op(lambda h, cw=cw: h.tensor_sub(
                            plo[:, 0:cw], e32[:, 0:cw], phi[:, 0:cw]),
                            [("s", tph), prev.get("plofree")])
                        prev["e32free"] = ("v", tpl)
                        tt1 = sch.dmat(blocks(pThi[:, 0:cw], cw // 128),
                                       blocks(phi[:, 0:cw], cw // 128),
                                       [("s", tph), prev.get("avdone")])
                        tt2 = sch.dmat(blocks(pTlo[:, 0:cw], cw // 128),
                                       blocks(plo[:, 0:cw], cw // 128),
                                       [("v", tpl), prev.get("avdone")])
                        prev["phifree"] = tt1
                        prev["plofree"] = tt2
                        mbs_ck = mbs[ck * 4:ck * 4 + 4] if r == 0 else mbs
                        for bi, mb in enumerate(mbs_ck):
                            for pi2, (pv, vv) in enumerate(ppairs):
                                st_ = (ck == 0 and bi == 0 and pi2 == 0)
                                sp_ = (ck == nch - 1 and
                                       bi == len(mbs_ck) - 1 and pi2 == 2)
                                cl_ = (bi == len(mbs_ck) - 1 and pi2 == 2)
                                lastav = t_mm(
                                    avp, pv[:, bi * 128:(bi + 1) * 128],
                                    vv[:, mb * D + hd_ * 64:
                                       mb * D + hd_ * 64 + 64],
                                    st_, sp_,
                                    waits=[tt1, tt2] +
                                    ([bank_free[avb]] if st_ else []),
                                    inc="pe" if cl_ else None)
                        prev["avdone"] = ("pe", lastav)
                    prev["smfree"] = ("s", texp)
                    prev["nmxfree"] = ("s", texp)
                    tsum = v_op(lambda h, nch=nch: h.reduce_sum(
                        sume, scols[:, 0:nch], axis=AX.X), [("s", texp)])
                    trcp = v_op(lambda h: h.reciprocal(rcp, sume))
                    prev["sumefree"] = ("v", trcp)
                    aow = [("pe", lastav), ("v", trcp)]
                    if hd_ == 0:
                        aow += [prev.get("aofree1"), prev.get("aofree2"),
                                prev.get("obfree")]
                    tao = v_op(lambda h, hd_=hd_, avp=avp: h.tensor_scalar(
                        ao[:, hd_ * 64:(hd_ + 1) * 64], avp, rcp, None,
                        ALU.mult), aow)
                    bank_free[avb] = ("v", tao)
                    lastao = tao
                prev["qfree"] = [("pe", lastmm)]
                taoh = s_op(lambda h: h.copy(aohi, ao),
                            [("v", lastao), prev.get("t3free")])
                taol = v_op(lambda h: h.tensor_sub(aolo, ao, aohi),
                            [("s", taoh), prev.get("t4free")])
                prev["aofree1"] = ("s", taoh)
                prev["aofree2"] = ("v", taol)
                t3 = sch.dmat(blocks(aoThi, 8), blocks(aohi, 8),
                              [("s", taoh), prev.get("odone")])
                t4 = sch.dmat(blocks(aoTlo, 8), blocks(aolo, 8),
                              [("v", taol), prev.get("odone")])
                prev["t3free"] = t3
                prev["t4free"] = t4
                opsum = bankpair(6)
                opairs = ((aoThi, woh), (aoThi, wol), (aoTlo, woh))
                mmo = None
                for pi, (av, wv) in enumerate(opairs):
                    for kd in range(8):
                        for n in range(2):
                            st_ = (pi == 0 and kd == 0)
                            sp_ = (pi == 2 and kd == 7)
                            w_ = ()
                            if st_:
                                w_ = [t3, t4,
                                      bank_free[6 + n]]
                            mmo = t_mm(
                                opsum[:, n * 512:(n + 1) * 512],
                                av[:, kd * 128:(kd + 1) * 128],
                                wv[:, kd * D + n * 512:
                                   kd * D + (n + 1) * 512],
                                st_, sp_, waits=w_,
                                inc="pe" if (sp_ and n == 1) else None)
                prev["odone"] = ("pe", mmo)
                te = v_op(lambda h: h.scalar_tensor_tensor(
                    ob, opsum, 1.0, xt, ALU.mult, ALU.add),
                    [("pe", mmo), txl, prev.get("obfree")])
                bank_free[6] = ("v", te)
                bank_free[7] = ("v", te)
                tst = sch.dma("gpsimd", xdst[r * 128:(r + 1) * 128, :], ob,
                              waits=[("v", te)], inc="st")
                prev["obfree"] = tst
                prev["xfree"] = [("v", te)]
                if SAFE_MODE:
                    sch.barrier()
            A.reset()

        cur = inp["L0_attn_inputs"]
        np_ = max(0, min(n_phases, N_PHASES))
        done = 0

        def take():
            nonlocal done
            if done < np_:
                done += 1
                return True
            return False

        if take():
            ffn0_a(cur)
            sch.barrier()
        if take():
            ffn0_b(cur, xB)
            cur = xB
            sch.barrier()
        if take():
            l0a_a(cur)
            sch.barrier()
        if take():
            l0a_b()
            sch.barrier()
        if take():
            l0a_c(cur, xA)
            cur = xA
            sch.barrier()
        if take():
            ffn0_a(cur)
            sch.barrier()
        if take():
            ffn0_b(cur, xB)
            cur = xB
            sch.barrier()
        if take():
            l1_ffn(cur, xA)
            cur = xA
            sch.barrier()
        if take():
            l1a_a(cur)
            sch.barrier()
        if take():
            l1a_b(cur, xB)
            cur = xB
            sch.barrier()
        if take():
            l1_ffn(cur, out)
            cur = out
            sch.barrier()
        if cur is not out:
            A.mark()
            tmp = [A.alloc(D, F32) for _ in range(2)]
            ptoks = [None, None]
            for t in range(NT):
                sl = t % 2
                tl = sch.dma("gpsimd", tmp[sl], cur[t * 128:(t + 1) * 128, :],
                             waits=[ptoks[sl]] if ptoks[sl] else (),
                             sem="oc%d" % sl)
                ts_ = sch.dma("gpsimd", out[t * 128:(t + 1) * 128, :],
                              tmp[sl], waits=[tl], sem="os%d" % sl)
                ptoks[sl] = ts_
            A.reset()
        if DEBUG_DUMP:
            scrmap = {k: v for k, v in locals().items()
                      if k.startswith("s_") or k in ("xA", "xB")}
            for nm in DEBUG_DUMP:
                ap = scrmap[nm]
                dbg = nc.dram_tensor("dbg_" + nm, list(ap.shape), ap.dtype,
                                     kind="ExternalOutput").ap()
                sch.barrier()
                sch.dma("gpsimd", dbg, ap, inc="st")
        sch.barrier()
        sch.emit("sync", lambda h: h.nop())

        blk = stk.enter_context(nc.Block())
        sch.run(blk)
    return nc


def kernel(**inputs):
    x = np.ascontiguousarray(np.asarray(inputs["L0_attn_inputs"],
                                        dtype=np.float32))
    base = {k: np.ascontiguousarray(np.asarray(v, dtype=np.float32))
            for k, v in inputs.items() if k != "L0_attn_inputs"}
    nc = build_nc(N_PHASES_RUN)
    in_maps = []
    for i in range(8):
        m = dict(base)
        m["L0_attn_inputs"] = np.ascontiguousarray(x[i])
        in_maps.append(m)
    res = run_bass_kernel_spmd(nc, in_maps, core_ids=list(range(8)))
    return np.stack([np.asarray(r["out"]) for r in res.results], axis=0)


# version 06
# speedup vs baseline: 1.0753x; 1.0753x over previous
import contextlib
import numpy as np
import concourse.bass as bass
import concourse.mybir as mybir
from concourse.bass_utils import run_bass_kernel_spmd

F32 = mybir.dt.float32
BF16 = mybir.dt.bfloat16
I8 = mybir.dt.int8
AF = mybir.ActivationFunctionType
ALU = mybir.AluOpType
AX = mybir.AxisListType

T = 2048
D = 1024
NT = T // 128
HF = 4096
FUV = 4224
E = 2048
NH = 16
NEG = -1.0e30

PHASES = ["p1a", "p1b", "p2a", "p2b", "p2c", "p3a", "p3b", "p4", "p5a", "p5b",
          "p6"]
N_PHASES = len(PHASES)
N_PHASES_RUN = N_PHASES
SAFE_MODE = False
DEBUG_DUMP = []

ENGS = ("sync", "gpsimd", "vector", "scalar", "tensor")


class Sched:
    def __init__(self, nc, stk):
        self.nc = nc
        self.stk = stk
        self.q = {e: [] for e in ENGS}
        self.sems = {}
        self.cnt = {}
        self.barrier_pending = {e: None for e in ENGS}
        self.site_map = {}
        self.pool_base = 0

    def sem(self, name):
        if name not in self.sems:
            self.sems[name] = self.stk.enter_context(self.nc.semaphore(name))
            self.cnt[name] = 0
        return self.sems[name]

    def site_sem(self, key, kind):
        if key not in self.site_map:
            n = sum(1 for k2 in self.site_map if self.site_map[k2][1] == kind)
            name = "q%s%d" % (kind, n)
            assert n < 34, "sem pool exhausted"
            self.site_map[key] = (name, kind)
        return self.site_map[key][0]

    def emit(self, eng, builder, waits=(), inc=None, inc_n=1):
        wmax = {}
        bp = self.barrier_pending[eng]
        if bp is not None:
            for s, v in bp.items():
                if v > 0:
                    wmax[s] = v
            self.barrier_pending[eng] = None
        def add(wv):
            if wv is None:
                return
            if isinstance(wv, list):
                for w2 in wv:
                    add(w2)
                return
            s, v = wv
            if isinstance(v, tuple):
                add(v)
                return
            if v is not None and v > 0:
                wmax[s] = max(wmax.get(s, 0), v)
        for wv in waits:
            add(wv)
        if inc in ("v", "s", "g"):
            self.sem(inc)
            prevcnt = self.cnt[inc]
            if prevcnt > 0:
                wmax[inc] = max(wmax.get(inc, 0), prevcnt)
        for s in wmax:
            self.sem(s)
        tok = None
        if inc is not None:
            self.sem(inc)
            self.cnt[inc] += inc_n
            tok = self.cnt[inc]
        self.q[eng].append((wmax, builder, inc, inc_n))
        return tok

    def dma(self, eng, out, in_, waits=(), inc="ld", slow=False, sem=None):
        def b(h, out=out, in_=in_, slow=slow):
            return h.dma_start(out=out, in_=in_,
                               allow_slow_non_contiguous=slow)
        kind = "g" if eng == "gpsimd" else "s"
        if sem is None:
            import inspect
            sem = self.site_sem(
                "d%d" % inspect.currentframe().f_back.f_lineno, kind)
        else:
            sem = self.site_sem("d_" + sem, kind)
        tok = self.emit(eng, b, waits, sem, 16)
        return (sem, tok)

    def dmat(self, out, in_, waits=(), sem=None):
        def b(h, out=out, in_=in_):
            return h.dma_start_transpose(out=out, in_=in_)
        if sem is None:
            import inspect
            sem = self.site_sem(
                "t%d" % inspect.currentframe().f_back.f_lineno, "t")
        else:
            sem = self.site_sem("t_" + sem, "t")
        return (sem, self.emit("sync", b, waits, sem, 16))

    def barrier(self):
        snap = dict(self.cnt)
        for e in ENGS:
            self.barrier_pending[e] = snap
        self.site_map = {}

    def run(self, blk):
        sch = self

        def mk(eng):
            def body(h):
                hwm = {}
                for wmax, builder, inc, inc_n in sch.q[eng]:
                    for s, v in wmax.items():
                        if hwm.get(s, -1) >= v:
                            continue
                        h.wait_ge(sch.sems[s], v)
                        hwm[s] = v
                    inst = builder(h)
                    if inc is not None:
                        inst.then_inc(sch.sems[inc], inc_n)
            return body

        blk.sync(mk("sync"))
        blk.gpsimd(mk("gpsimd"))
        blk.vector(mk("vector"))
        blk.scalar(mk("scalar"))
        blk.tensor(mk("tensor"))


class Arena:
    def __init__(self, ap, nbytes):
        self.ap = ap
        self.nbytes = nbytes
        self.off = 0
        self.marks = []

    def alloc(self, n_elem, dt, parts=128):
        sz = n_elem * mybir.dt.size(dt)
        off = (self.off + 63) & ~63
        assert off + sz <= self.nbytes, f"arena overflow {off}+{sz}"
        self.off = off + sz
        return self.ap[:parts, off:off + sz].bitcast(dt)

    def mark(self):
        self.marks.append(self.off)

    def reset(self):
        self.off = self.marks.pop()


def blocks(ap, nblk):
    return ap.rearrange("p (k f) -> p k f", k=nblk)


def kpf(dram_ap):
    return dram_ap.rearrange("(k p) f -> p k f", p=128)


def row(dram_ap):
    return dram_ap.rearrange("(one f) -> one f", one=1)


def build_nc(n_phases=N_PHASES):
    nc = bass.Bass("TRN2", target_bir_lowering=False)

    inp = {}
    for name, shp in [
        ("L0_attn_inputs", [T, D]), ("L0_attn_W_uv", [D, FUV]),
        ("L0_attn_b_uv", [FUV]), ("L0_attn_gamma", [2, 128]),
        ("L0_attn_beta", [2, 128]), ("L0_attn_W_out", [E, D]),
        ("L0_attn_b_out", [D]), ("L0_ffn_W_fc", [HF, D]),
        ("L0_ffn_ln_w", [HF]), ("L0_ffn_ln_b", [HF]),
        ("L0_ffn_W_proj", [D, HF]), ("L1_attn_Wq", [D, D]),
        ("L1_attn_Wk", [D, D]), ("L1_attn_Wv", [D, D]),
        ("L1_attn_Wo", [D, D]), ("L1_ffn_weight", [D, D]),
    ]:
        inp[name] = nc.dram_tensor(name, shp, F32, kind="ExternalInput").ap()
    out = nc.dram_tensor("out", [T, D], F32, kind="ExternalOutput").ap()

    def scr(name, shp, dt):
        return nc.dram_tensor(name, shp, dt, kind="Internal").ap()

    xA = scr("xA", [T, D], F32)
    xB = scr("xB", [T, D], F32)
    s_wfcqT = scr("s_wfcqT", [D, HF], BF16)
    s_wprojqT = scr("s_wprojqT", [HF, D], BF16)
    s_wuv_h = scr("s_wuv_h", [D, FUV], BF16)
    s_wuv_l = scr("s_wuv_l", [D, FUV], BF16)
    s_wout_h = scr("s_wout_h", [E, D], BF16)
    s_wout_l = scr("s_wout_l", [E, D], BF16)
    s_wT = {}
    for w in ("q", "k", "v", "o"):
        s_wT[w] = (scr(f"s_w{w}T_h", [D, D], BF16),
                   scr(f"s_w{w}T_l", [D, D], BF16))
    s_wl1qT = scr("s_wl1qT", [D, D], BF16)
    s_hnT_h = scr("s_hnT_h", [HF, T], BF16)
    s_hnT_l = scr("s_hnT_l", [HF, T], BF16)
    s_u_h = scr("s_u_h", [T, E], BF16)
    s_u_l = scr("s_u_l", [T, E], BF16)
    s_v_h = scr("s_v_h", [T, E], BF16)
    s_v_l = scr("s_v_l", [T, E], BF16)
    s_qh = scr("s_qh", [128, T], BF16)
    s_ql = scr("s_ql", [128, T], BF16)
    s_kh = scr("s_kh", [128, T], BF16)
    s_kl = scr("s_kl", [128, T], BF16)
    s_uavT_h = scr("s_uavT_h", [E, T], BF16)
    s_uavT_l = scr("s_uavT_l", [E, T], BF16)
    s_q1T_h = scr("s_q1T_h", [D, T], BF16)
    s_q1T_l = scr("s_q1T_l", [D, T], BF16)
    s_k1T_h = scr("s_k1T_h", [D, T], BF16)
    s_k1T_l = scr("s_k1T_l", [D, T], BF16)
    s_v1_h = scr("s_v1_h", [T, D], BF16)
    s_v1_l = scr("s_v1_l", [T, D], BF16)
    s_dbgA = scr("s_dbgA", [128, 2048], F32)
    s_dbgB = scr("s_dbgB", [128, 2048], F32)
    s_dbgC = scr("s_dbgC", [128, 2048], F32)
    s_pr = scr("s_pr", [128], F32)
    s_scal = scr("s_scal", [8], F32)
    s_bconst = scr("s_bconst", [D], F32)

    stk = contextlib.ExitStack()
    with stk:
        SB_BYTES = 207 * 1024
        arena_t = stk.enter_context(nc.sbuf_tensor("arena", [128, SB_BYTES],
                                                   I8))
        A = Arena(arena_t[:], SB_BYTES)
        pbig1 = stk.enter_context(nc.psum_tensor("pbig1", [128, 2048], F32))
        pbig2 = stk.enter_context(nc.psum_tensor("pbig2", [128, 2048], F32))

        def bank(i):
            pb = pbig1 if i < 4 else pbig2
            return pb[:, (i % 4) * 512:(i % 4 + 1) * 512]

        def bankpair(i):
            pb = pbig1 if i < 4 else pbig2
            return pb[:, (i % 4) * 512:(i % 4 + 2) * 512]

        sch = Sched(nc, stk)
        for s in ("ld", "lg", "tp", "st", "sg", "pe", "v", "s", "g"):
            sch.sem(s)
        bank_free = {i: None for i in range(8)}

        def v_op(builder, waits=(), inc="v"):
            return sch.emit("vector", builder, waits, inc)

        def s_op(builder, waits=(), inc="s"):
            return sch.emit("scalar", builder, waits, inc)

        def g_op(builder, waits=(), inc="g"):
            return sch.emit("gpsimd", builder, waits, inc)

        def t_mm(psum_ap, lhsT, rhs, start, stop, waits=(), inc=None):
            def b(h, o=psum_ap, l=lhsT, r=rhs, st=start, sp=stop):
                return h.matmul(o, l, r, start=st, stop=sp)
            if inc is None:
                return sch.emit("tensor", b, waits)
            sch.emit("tensor", b, waits)
            return sch.emit("tensor", lambda h: h.drain(), (), inc)

        def alloc_ln_slot(with_yT=True):
            d = {
                "xt": A.alloc(D, F32),
                "y32": A.alloc(D, F32),
                "yhi": A.alloc(D, BF16),
                "ylo": A.alloc(D, BF16),
                "st": A.alloc(16, F32),
            }
            if with_yT:
                d["yThi"] = A.alloc(D, BF16)
                d["yTlo"] = A.alloc(D, BF16)
            return d

        def emit_ln_tile(xsrc, t, B, prevtok=None, yT_dsts=None, yT_waits=(),
                         extra_x_waits=(), tag="z"):
            xt, y32, yhi, ylo, stt = (B["xt"], B["y32"], B["yhi"], B["ylo"],
                                      B["st"])
            tok = {}
            slot_free = [w for w in extra_x_waits if w is not None]
            if prevtok is not None:
                slot_free += [("v", prevtok["ylo"]), ("s", prevtok["yhi"])]
            tok["x"] = sch.dma("gpsimd", xt, xsrc[t * 128:(t + 1) * 128, :],
                               waits=slot_free, sem=tag + "x")
            s1n, s2, mn, msq, va, vv, sd, rs, nm = [stt[:, i:i + 1]
                                                    for i in range(9)]
            wl = [tok["x"]] + slot_free
            v_op(lambda h: h.reduce_sum(s1n, xt, axis=AX.X, negate=True), wl)
            tsq = s_op(lambda h: h.activation(y32, xt, AF.Square,
                                              accum_out=s2), wl)
            v_op(lambda h: h.tensor_scalar(mn, s1n, 1.0 / D, None, ALU.mult))
            v_op(lambda h: h.tensor_mul(msq, mn, mn))
            v_op(lambda h: h.tensor_scalar(va, s2, 1.0 / D, 1e-5, ALU.mult,
                                           ALU.add), [("s", tsq)])
            tvv = v_op(lambda h: h.tensor_sub(vv, va, msq))
            tsd = s_op(lambda h: h.activation(sd, vv, AF.Sqrt), [("v", tvv)])
            v_op(lambda h: h.reciprocal(rs, sd), [("s", tsd)])
            tnm = v_op(lambda h: h.tensor_mul(nm, mn, rs))
            whi = [("v", tnm)]
            if prevtok is not None:
                whi.append(prevtok["yThi"])
            tok["yhi"] = s_op(
                lambda h: h.activation(yhi, xt, AF.Identity, bias=nm,
                                       scale=rs), whi)
            tok["y32"] = v_op(
                lambda h: h.tensor_scalar(y32, xt, rs, nm, ALU.mult, ALU.add),
                [("s", tok["yhi"])])
            wlo = []
            if prevtok is not None:
                wlo.append(prevtok["yTlo"])
            tok["ylo"] = v_op(lambda h: h.tensor_sub(ylo, y32, yhi), wlo)
            if yT_dsts is None:
                yT_dsts = (blocks(B["yThi"], 8), blocks(B["yTlo"], 8))
            tok["yThi"] = sch.dmat(yT_dsts[0], blocks(yhi, 8),
                                   [("s", tok["yhi"])] + list(yT_waits),
                                   sem=tag + "h")
            tok["yTlo"] = sch.dmat(yT_dsts[1], blocks(ylo, 8),
                                   [("v", tok["ylo"])] + list(yT_waits),
                                   sem=tag + "l")
            return tok

        A.mark()
        wt = [A.alloc(FUV, F32), A.alloc(FUV, F32)]
        qh = A.alloc(FUV, BF16)
        ql = A.alloc(FUV, BF16)
        qr = A.alloc(FUV, BF16)
        qT = A.alloc(FUV, BF16)
        qT2 = A.alloc(FUV, BF16)
        acc = A.alloc(64, F32)
        prrow = A.alloc(128, F32, parts=1)
        sc1 = A.alloc(4, F32, parts=1)
        thr_rep = A.alloc(1, F32)
        nthr_rep = A.alloc(1, F32)

        def prep_ternary(wsrc, nt, cols, dst, mean_div, use_max, eps,
                         scal_idx):
            last_v = None
            for t in range(nt):
                sl = t % 2
                wld = sch.dma("gpsimd", wt[sl][:, :cols],
                              wsrc[t * 128:(t + 1) * 128, :],
                              waits=[("v", last_v)] if last_v else ())
                last_v = v_op(lambda h, t=t, sl=sl: h.reduce_sum(
                    acc[:, t:t + 1], wt[sl][:, :cols], axis=AX.X,
                    apply_absolute_value=True), [wld])
            tacc = v_op(lambda h: h.reduce_sum(acc[:, 63:64], acc[:, 0:nt],
                                               axis=AX.X), [("v", last_v)])
            tst = sch.dma("gpsimd", s_pr.rearrange("(p one) -> p one", one=1),
                          acc[:, 63:64], waits=[("v", tacc)], inc="st")
            tld = sch.dma("gpsimd", prrow, row(s_pr), waits=[tst])
            v_op(lambda h: h.reduce_sum(sc1[:, 0:1], prrow, axis=AX.X),
                 [tld])
            if use_max:
                v_op(lambda h: h.tensor_scalar(sc1[:, 1:2], sc1[:, 0:1],
                                               mean_div, None, ALU.mult))
                v_op(lambda h: h.tensor_scalar(sc1[:, 1:2], sc1[:, 1:2], eps,
                                               None, ALU.max))
            else:
                v_op(lambda h: h.tensor_scalar(sc1[:, 1:2], sc1[:, 0:1],
                                               mean_div, eps, ALU.mult,
                                               ALU.add))
            tth = v_op(lambda h: h.tensor_scalar(sc1[:, 2:3], sc1[:, 1:2], 0.5,
                                                 None, ALU.mult))
            tsc = sch.dma("gpsimd", row(s_scal[2 * scal_idx:2 * scal_idx + 2]),
                          sc1[:, 1:3], waits=[("v", tth)], inc="st")
            tthr = sch.dma("gpsimd", thr_rep,
                           row(s_scal[2 * scal_idx + 1:2 * scal_idx + 2])
                           .broadcast_to([128, 1]), waits=[tsc])
            tnthr = v_op(lambda h: h.tensor_scalar(nthr_rep, thr_rep, -1.0,
                                                   None, ALU.mult),
                         [tthr])
            nb = cols // 128
            last_q = tnthr
            last_tp = None
            last_st = None
            for t in range(nt):
                sl = t % 2
                wld = sch.dma("gpsimd", wt[sl][:, :cols],
                              wsrc[t * 128:(t + 1) * 128, :],
                              waits=[("v", last_q)])
                v_op(lambda h, sl=sl: h.tensor_scalar(
                    qh[:, :cols], wt[sl][:, :cols], thr_rep, None, ALU.is_gt),
                    [wld])
                v_op(lambda h, sl=sl: h.tensor_scalar(
                    ql[:, :cols], wt[sl][:, :cols], nthr_rep, None,
                    ALU.is_lt))
                tq = v_op(lambda h: h.tensor_sub(qr[:, :cols], qh[:, :cols],
                                                 ql[:, :cols]),
                          [last_tp] if last_tp else ())
                last_q = tq
                ttp = sch.dmat(blocks(qT[:, :cols], nb),
                               blocks(qr[:, :cols], nb),
                               [("v", tq), last_st])
                last_tp = ttp
                last_st = sch.dma("gpsimd", kpf(dst[:, t * 128:(t + 1) * 128]),
                                  blocks(qT[:, :cols], nb),
                                  waits=[ttp], sem="pq%d" % (t % 2))

        def prep_split(wsrc, nt, cols, dsth, dstl, transpose_to=False):
            last_lo = None
            last_hi_cons = None
            last_lo_cons = None
            last_st1 = None
            last_st2 = None
            for t in range(nt):
                sl = t % 2
                wld = sch.dma("gpsimd", wt[sl][:, :cols],
                              wsrc[t * 128:(t + 1) * 128, :],
                              waits=[("v", last_lo)] if last_lo else ())
                thi = s_op(lambda h, sl=sl: h.copy(qh[:, :cols],
                                                   wt[sl][:, :cols]),
                           [wld, last_hi_cons])
                tlo = v_op(lambda h, sl=sl: h.tensor_sub(
                    ql[:, :cols], wt[sl][:, :cols], qh[:, :cols]),
                    [("s", thi), last_lo_cons])
                last_lo = tlo
                nb = cols // 128
                if transpose_to:
                    t1 = sch.dmat(blocks(qT[:, :cols], nb),
                                  blocks(qh[:, :cols], nb),
                                  [("s", thi), last_st1])
                    t2 = sch.dmat(blocks(qT2[:, :cols], nb),
                                  blocks(ql[:, :cols], nb),
                                  [("v", tlo), last_st2])
                    last_hi_cons = t1
                    last_st1 = ("st", sch.dma(
                        "gpsimd", kpf(dsth[:, t * 128:(t + 1) * 128]),
                        blocks(qT[:, :cols], nb), waits=[t1],
                        inc="st"))
                    last_st2 = ("st", sch.dma(
                        "gpsimd", kpf(dstl[:, t * 128:(t + 1) * 128]),
                        blocks(qT2[:, :cols], nb), waits=[t2],
                        inc="st"))
                else:
                    t1 = sch.dma("gpsimd", dsth[t * 128:(t + 1) * 128, :],
                                 qh[:, :cols], waits=[("s", thi)],
                                 sem="ph%d" % (t % 2))
                    last_lo_cons = sch.dma(
                        "gpsimd", dstl[t * 128:(t + 1) * 128, :],
                        ql[:, :cols], waits=[("v", tlo)],
                        sem="pl%d" % (t % 2))
                    last_hi_cons = t1

        prep_ternary(inp["L0_ffn_W_fc"], 32, D, s_wfcqT, 1.0 / (HF * D),
                     False, 1e-6, 0)
        sch.barrier()
        prep_ternary(inp["L0_ffn_W_proj"], 8, HF, s_wprojqT, 1.0 / (HF * D),
                     False, 1e-6, 1)
        sch.barrier()
        prep_split(inp["L0_attn_W_uv"], 8, FUV, s_wuv_h, s_wuv_l)
        sch.barrier()
        prep_split(inp["L0_attn_W_out"], 16, D, s_wout_h, s_wout_l)
        sch.barrier()
        for w, src in (("q", "L1_attn_Wq"), ("k", "L1_attn_Wk"),
                       ("v", "L1_attn_Wv"), ("o", "L1_attn_Wo")):
            prep_split(inp[src], 8, D, s_wT[w][0], s_wT[w][1],
                       transpose_to=True)
            sch.barrier()
        prep_ternary(inp["L1_ffn_weight"], 8, D, s_wl1qT, 1.0 / (D * D),
                     True, 1e-5, 2)
        A.reset()
        sch.barrier()

        def ffn0_a(xsrc):
            A.mark()
            wfcq = A.alloc(8 * HF, BF16)
            lnw_rep = A.alloc(HF, F32)
            lnb = [alloc_ln_slot() for _ in range(2)]
            hbuf = A.alloc(HF, F32)
            hnc = A.alloc(HF, F32)
            hnhi = A.alloc(HF, BF16)
            hnlo = A.alloc(HF, BF16)
            hnThi = A.alloc(HF, BF16)
            hnTlo = A.alloc(HF, BF16)
            hsq = A.alloc(512, F32)
            hst = A.alloc(32, F32)
            twl = sch.dma("sync", blocks(wfcq, 8), kpf(s_wfcqT))
            tlnw = sch.dma("sync", lnw_rep,
                           row(inp["L0_ffn_ln_w"]).broadcast_to([128, HF]))
            prev = {}
            ptok = [None, None]
            for t in range(NT):
                sl = t % 2
                tok = emit_ln_tile(xsrc, t, lnb[sl], prevtok=ptok[sl],
                                   yT_waits=prev.get(("ytfree", sl), ()),
                                   tag="a%d" % sl)
                ptok[sl] = tok
                lastmm = None
                tsqr = None
                for c in range(8):
                    b = c % 2
                    bk = bank(b)
                    w0 = [tok["yThi"], tok["yTlo"],
                          twl, bank_free[b]]
                    for pi, yT in ((0, lnb[sl]["yThi"]), (1, lnb[sl]["yTlo"])):
                        for kd in range(8):
                            st_ = (pi == 0 and kd == 0)
                            sp_ = (pi == 1 and kd == 7)
                            lastmm = t_mm(
                                bk, yT[:, kd * 128:(kd + 1) * 128],
                                wfcq[:, kd * HF + c * 512:
                                     kd * HF + (c + 1) * 512],
                                st_, sp_, waits=w0 if st_ else (),
                                inc="pe" if sp_ else None)
                    gw = [("pe", lastmm)]
                    if c == 0 and ("hfree" in prev):
                        gw.append(prev["hfree"])
                    tg = s_op(lambda h, c=c, bk=bk: h.activation(
                        hbuf[:, c * 512:(c + 1) * 512], bk, AF.Gelu,
                        accum_out=hst[:, c:c + 1]), gw)
                    bank_free[b] = ("s", tg)
                    tsqr = s_op(lambda h, c=c: h.activation(
                        hsq, hbuf[:, c * 512:(c + 1) * 512], AF.Square,
                        accum_out=hst[:, 8 + c:9 + c]))
                sA, sB, mh, mq, vga, vv2, sdh, rsh, nmh = [
                    hst[:, 16 + i:17 + i] for i in range(9)]
                v_op(lambda h: h.reduce_sum(sA, hst[:, 0:8], axis=AX.X,
                                            negate=True), [("s", tsqr)])
                v_op(lambda h: h.reduce_sum(sB, hst[:, 8:16], axis=AX.X))
                v_op(lambda h: h.tensor_scalar(mh, sA, 1.0 / HF, None,
                                               ALU.mult))
                v_op(lambda h: h.tensor_mul(mq, mh, mh))
                v_op(lambda h: h.tensor_scalar(vga, sB, 1.0 / HF, 1e-5,
                                               ALU.mult, ALU.add))
                t2 = v_op(lambda h: h.tensor_sub(vv2, vga, mq))
                t3 = s_op(lambda h: h.activation(sdh, vv2, AF.Sqrt),
                          [("v", t2)])
                v_op(lambda h: h.reciprocal(rsh, sdh), [("s", t3)])
                t4 = v_op(lambda h: h.tensor_mul(nmh, mh, rsh))
                thnc = v_op(lambda h: h.tensor_scalar(hnc, hbuf, rsh, nmh,
                                                      ALU.mult, ALU.add),
                            [("v", t4), ("s", tsqr)])
                tp32 = v_op(lambda h: h.tensor_mul(hbuf, hnc, lnw_rep),
                            [tlnw])
                thh = s_op(lambda h: h.copy(hnhi, hbuf),
                           [("v", tp32), prev.get("tthi")])
                thl = v_op(lambda h: h.tensor_sub(hnlo, hbuf, hnhi),
                           [("s", thh), prev.get("ttlo")])
                prev["hfree"] = ("v", thl)
                tthi = sch.dmat(blocks(hnThi, 32), blocks(hnhi, 32),
                                [("s", thh), prev.get("s1")])
                ttlo = sch.dmat(blocks(hnTlo, 32), blocks(hnlo, 32),
                                [("v", thl), prev.get("s2")])
                prev["tthi"] = tthi
                prev["ttlo"] = ttlo
                s1 = sch.dma("gpsimd", kpf(s_hnT_h[:, t * 128:(t + 1) * 128]),
                             blocks(hnThi, 32), waits=[tthi],
                             inc="st")
                s2_ = sch.dma("gpsimd",
                              kpf(s_hnT_l[:, t * 128:(t + 1) * 128]),
                              blocks(hnTlo, 32), waits=[ttlo],
                              inc="st")
                prev["s1"] = s1
                prev["s2"] = s2_
                prev[("ytfree", sl)] = [("pe", lastmm)]
                if SAFE_MODE:
                    sch.barrier()
            A.reset()

        def ffn0_b(xsrc, xdst):
            A.mark()
            wpq = A.alloc(32 * D, BF16)
            lnbcol = A.alloc(32, F32)
            lnbrh = A.alloc(32 * 128, BF16)
            lnbrl = A.alloc(32 * 128, BF16)
            lnbr32 = A.alloc(32 * 128, F32)
            ones128 = A.alloc(128, F32)
            bc = A.alloc(D, F32, parts=1)
            bcrep = A.alloc(D, F32)
            t1buf = A.alloc(D, F32)
            slots = [{
                "hTh": A.alloc(HF, BF16),
                "hTl": A.alloc(HF, BF16),
                "xt": A.alloc(D, F32),
                "o": A.alloc(D, F32),
            } for _ in range(2)]
            twl = sch.dma("sync", blocks(wpq, 32), kpf(s_wprojqT))
            tlb = sch.dma("sync", lnbcol,
                          inp["L0_ffn_ln_b"].rearrange("(c p) -> p c", p=128),
                          slow=True)
            v_op(lambda h: h.memset(ones128, 1.0))
            t2 = None
            for kc in range(32):
                t2 = v_op(lambda h, kc=kc: h.tensor_scalar(
                    lnbr32[:, kc * 128:(kc + 1) * 128], ones128,
                    lnbcol[:, kc:kc + 1], None, ALU.mult), [tlb])
            th1 = s_op(lambda h: h.copy(lnbrh, lnbr32), [("v", t2)])
            th2 = v_op(lambda h: h.tensor_sub(lnbrl, lnbr32, lnbrh),
                       [("s", th1)])
            mmt = [None, None]
            for n in range(2):
                for pi, lb in ((0, lnbrh), (1, lnbrl)):
                    for kc in range(32):
                        st_ = (pi == 0 and kc == 0)
                        sp_ = (pi == 1 and kc == 31)
                        mmt[n] = t_mm(
                            bank(6 + n), lb[:, kc * 128:(kc + 1) * 128],
                            wpq[:, kc * D + n * 512: kc * D + (n + 1) * 512],
                            st_, sp_,
                            waits=([twl, ("v", th2),
                                    bank_free[6 + n]] if st_ else ()),
                            inc="pe" if sp_ else None)
            tb1 = v_op(lambda h: h.tensor_copy(bc[0:1, 0:512],
                                               bank(6)[0:1, :]),
                       [("pe", mmt[0])])
            tb2 = v_op(lambda h: h.tensor_copy(bc[0:1, 512:1024],
                                               bank(7)[0:1, :]),
                       [("pe", mmt[1])])
            bank_free[6] = ("v", tb1)
            bank_free[7] = ("v", tb2)
            tb3 = sch.dma("gpsimd",
                          row(s_bconst), bc,
                          waits=[("v", tb1), ("v", tb2)], inc="st")
            thb = sch.dma("gpsimd", bcrep,
                          row(s_bconst).broadcast_to([128, D]),
                          waits=[tb3])
            prev = {}
            for t in range(NT):
                sl = t % 2
                S = slots[sl]
                lw = prev.get(("free", sl), ())
                th = sch.dma("gpsimd", blocks(S["hTh"], 32),
                             kpf(s_hnT_h[:, t * 128:(t + 1) * 128]), waits=lw,
                             sem="bh%d" % sl)
                tl = sch.dma("gpsimd", blocks(S["hTl"], 32),
                             kpf(s_hnT_l[:, t * 128:(t + 1) * 128]),
                             sem="bl%d" % sl)
                tx = sch.dma("gpsimd", S["xt"],
                             xsrc[t * 128:(t + 1) * 128, :],
                             sem="bx%d" % sl)
                pb = 0 if sl == 0 else 2
                opsum = bankpair(pb)
                mmtok = None
                for pi in range(2):
                    hT = (S["hTh"], S["hTl"])[pi]
                    for kc in range(32):
                        for n in range(2):
                            st_ = (pi == 0 and kc == 0)
                            sp_ = (pi == 1 and kc == 31)
                            w = ()
                            if st_:
                                w = [th, tl, twl,
                                     bank_free[pb + n]]
                            mmtok = t_mm(
                                opsum[:, n * 512:(n + 1) * 512],
                                hT[:, kc * 128:(kc + 1) * 128],
                                wpq[:, kc * D + n * 512:
                                    kc * D + (n + 1) * 512],
                                st_, sp_, waits=w,
                                inc="pe" if (sp_ and n == 1) else None)
                te1 = v_op(lambda h, S=S, opsum=opsum: h.scalar_tensor_tensor(
                    t1buf, opsum, 0.5, S["xt"], ALU.mult, ALU.add),
                    [("pe", mmtok), tx, thb,
                     prev.get("t1free"), prev.get(("sfree", sl))])
                bank_free[pb] = ("v", te1)
                bank_free[pb + 1] = ("v", te1)
                te1 = v_op(lambda h, S=S: h.scalar_tensor_tensor(
                    S["o"], bcrep, 0.5, t1buf, ALU.mult, ALU.add))
                prev["t1free"] = ("v", te1)
                tst = sch.dma("gpsimd", xdst[t * 128:(t + 1) * 128, :],
                              S["o"], waits=[("v", te1)], sem="bs%d" % sl)
                prev[("free", sl)] = [("pe", mmtok), ("v", te1)]
                prev[("sfree", sl)] = tst
                if SAFE_MODE:
                    sch.barrier()
            A.reset()

        def l0a_a(xsrc):
            A.mark()
            wuvh = A.alloc(8 * FUV, BF16)
            wuvl = A.alloc(8 * FUV, BF16)
            buvcol = A.alloc(33, F32)
            gam = A.alloc(2, F32)
            bet = A.alloc(2, F32)
            lnb = [alloc_ln_slot(with_yT=False)]
            yTh_m = A.alloc(8 * 512, BF16)
            yTl_m = A.alloc(8 * 512, BF16)
            s32 = A.alloc(512, F32)
            shi = A.alloc(512, BF16)
            slo = A.alloc(512, BF16)
            sThi = A.alloc(512, BF16)
            sTlo = A.alloc(512, BF16)
            q32 = A.alloc(512, F32)
            k32 = A.alloc(512, F32)
            qhi = A.alloc(512, BF16)
            qlo = A.alloc(512, BF16)
            khi = A.alloc(512, BF16)
            klo = A.alloc(512, BF16)
            twh = sch.dma("sync", blocks(wuvh, 8), kpf(s_wuv_h))
            twl = sch.dma("sync", blocks(wuvl, 8), kpf(s_wuv_l))
            tbc = sch.dma("sync", buvcol,
                          inp["L0_attn_b_uv"].rearrange("(c p) -> p c", p=128),
                          slow=True)
            tgm = sch.dma("sync", gam,
                          inp["L0_attn_gamma"].rearrange("g p -> p g"),
                          slow=True)
            tbt = sch.dma("sync", bet,
                          inp["L0_attn_beta"].rearrange("g p -> p g"),
                          slow=True)
            prev = {}
            ptok = [None]
            for mc in range(4):
                mtoks = []
                for si in range(4):
                    t = mc * 4 + si
                    tok = emit_ln_tile(
                        xsrc, t, lnb[0], prevtok=ptok[0],
                        yT_dsts=(blocks(yTh_m, 8)
                                 [:, :, si * 128:(si + 1) * 128],
                                 blocks(yTl_m, 8)
                                 [:, :, si * 128:(si + 1) * 128]),
                        yT_waits=prev.get("ymfree", ()), tag="c")
                    ptok[0] = tok
                    mtoks.append(tok)
                wmac = [twh, twl, tbc]
                for tok in mtoks:
                    wmac += [tok["yThi"], tok["yTlo"]]
                lastmm = None
                for ft in range(33):
                    b = ft % 8
                    pairs = ((wuvh, yTh_m), (wuvh, yTl_m), (wuvl, yTh_m))
                    for pi, (wv, yv) in enumerate(pairs):
                        for kd in range(8):
                            st_ = (pi == 0 and kd == 0)
                            sp_ = (pi == 2 and kd == 7)
                            lastmm = t_mm(
                                bank(b),
                                wv[:, kd * FUV + ft * 128:
                                   kd * FUV + (ft + 1) * 128],
                                yv[:, kd * 512:(kd + 1) * 512], st_, sp_,
                                waits=(wmac + [bank_free[b]]) if st_ else (),
                                inc="pe" if sp_ else None)
                    tbs = s_op(lambda h, b=b, ft=ft: h.activation(
                        s32, bank(b), AF.Silu, bias=buvcol[:, ft:ft + 1],
                        scale=1.0),
                        [("pe", lastmm), prev.get("s32free")])
                    bank_free[b] = ("s", tbs)
                    if ft == 32:
                        tq32 = s_op(lambda h: h.activation(
                            q32, s32, AF.Identity, bias=bet[:, 0:1],
                            scale=gam[:, 0:1]),
                            [tgm, tbt,
                             prev.get("q32free")])
                        tk32 = s_op(lambda h: h.activation(
                            k32, s32, AF.Identity, bias=bet[:, 1:2],
                            scale=gam[:, 1:2]), [prev.get("k32free")])
                        prev["s32free"] = ("s", tk32)
                        tqh = s_op(lambda h: h.copy(qhi, q32),
                                   [prev.get("sp_qh")])
                        tql = v_op(lambda h: h.tensor_sub(qlo, q32, qhi),
                                   [("s", tqh), prev.get("sp_ql")])
                        tkh = s_op(lambda h: h.copy(khi, k32),
                                   [prev.get("sp_kh")])
                        tkl = v_op(lambda h: h.tensor_sub(klo, k32, khi),
                                   [("s", tkh), prev.get("sp_kl")])
                        prev["q32free"] = ("v", tql)
                        prev["k32free"] = ("v", tkl)
                        cs = slice(mc * 512, (mc + 1) * 512)
                        for nm_, buf, tk, eng, dst in (
                                ("sp_qh", qhi, tqh, "s", s_qh),
                                ("sp_ql", qlo, tql, "v", s_ql),
                                ("sp_kh", khi, tkh, "s", s_kh),
                                ("sp_kl", klo, tkl, "v", s_kl)):
                            ts_ = sch.dma("gpsimd", dst[:, cs], buf,
                                          waits=[(eng, tk)], sem=nm_)
                            prev[nm_] = ts_
                    else:
                        tsh = s_op(lambda h: h.copy(shi, s32),
                                   [prev.get("shifree")])
                        tsl = v_op(lambda h: h.tensor_sub(slo, s32, shi),
                                   [("s", tsh), prev.get("slofree")])
                        prev["s32free"] = ("v", tsl)
                        tt1 = sch.dmat(blocks(sThi, 4), blocks(shi, 4),
                                       [("s", tsh), prev.get("sthfree")])
                        tt2 = sch.dmat(blocks(sTlo, 4), blocks(slo, 4),
                                       [("v", tsl), prev.get("stlfree")])
                        prev["shifree"] = tt1
                        prev["slofree"] = tt2
                        if ft < 16:
                            dh, dl, col = s_u_h, s_u_l, ft * 128
                        else:
                            dh, dl, col = s_v_h, s_v_l, (ft - 16) * 128
                        rows = slice(mc * 512, (mc + 1) * 512)
                        su1 = sch.dma(
                            "gpsimd",
                            dh[rows, col:col + 128]
                            .rearrange("(b p) f -> p b f", p=128),
                            blocks(sThi, 4), waits=[tt1], inc="st")
                        su2 = sch.dma(
                            "gpsimd",
                            dl[rows, col:col + 128]
                            .rearrange("(b p) f -> p b f", p=128),
                            blocks(sTlo, 4), waits=[tt2], inc="st")
                        prev["sthfree"] = su1
                        prev["stlfree"] = su2
                prev["ymfree"] = [("pe", lastmm)]
                if SAFE_MODE:
                    sch.barrier()
            A.reset()

        def l0a_b():
            A.mark()
            vh_sb = A.alloc(16 * E, BF16)
            vl_sb = A.alloc(16 * E, BF16)
            qh_sb = A.alloc(T, BF16)
            ql_sb = A.alloc(T, BF16)
            kh_sb = A.alloc(T, BF16)
            kl_sb = A.alloc(T, BF16)
            r32 = A.alloc(E, F32)
            uav = A.alloc(E, F32)
            sq32 = uav
            sqhi = A.alloc(E, BF16)
            sqlo = A.alloc(E, BF16)
            sqThi = A.alloc(E, BF16)
            sqTlo = A.alloc(E, BF16)
            uth = A.alloc(E, BF16)
            utl = A.alloc(E, BF16)
            ut32 = r32
            uavhi = A.alloc(E, BF16)
            uavlo = A.alloc(E, BF16)
            uavThi = A.alloc(E, BF16)
            uavTlo = A.alloc(E, BF16)
            tv1 = sch.dma("sync", blocks(vh_sb, 16),
                          s_v_h.rearrange("(c p) e -> p c e", p=128))
            tv2 = sch.dma("sync", blocks(vl_sb, 16),
                          s_v_l.rearrange("(c p) e -> p c e", p=128))
            tq1 = sch.dma("sync", qh_sb, s_qh)
            tq2 = sch.dma("sync", ql_sb, s_ql)
            tk1 = sch.dma("sync", kh_sb, s_kh)
            tk2 = sch.dma("sync", kl_sb, s_kl)
            wload = [x for x in (tv1, tv2, tq1, tq2, tk1, tk2)]
            prev = {}
            for t in range(NT):
                tul1 = sch.dma("gpsimd", uth,
                               s_u_h[t * 128:(t + 1) * 128, :],
                               waits=prev.get("ufree", ()))
                tul2 = sch.dma("gpsimd", utl,
                               s_u_l[t * 128:(t + 1) * 128, :])
                lastmm = None
                qsl = slice(t * 128, (t + 1) * 128)
                pairs = ((qh_sb, kh_sb), (qh_sb, kl_sb), (ql_sb, kh_sb))
                for ms in range(4):
                    for pi, (qv, kv) in enumerate(pairs):
                        st_ = (pi == 0)
                        sp_ = (pi == 2)
                        lastmm = t_mm(
                            bank(ms), qv[:, qsl],
                            kv[:, ms * 512:(ms + 1) * 512], st_, sp_,
                            waits=(wload + [bank_free[ms]]) if st_ else (),
                            inc="pe" if (sp_ and ms == 3) else None)
                tr = None
                for bb in range(4):
                    tr = s_op(lambda h, bb=bb: h.activation(
                        r32[:, bb * 512:(bb + 1) * 512], bank(bb),
                        AF.Relu, scale=1.0 / T),
                        [("pe", lastmm), prev.get("r32free"),
                         prev.get("ut32free")] if bb == 0 else ())

                for ms in range(4):
                    bank_free[ms] = ("s", tr)
                tsq = s_op(lambda h: h.activation(sq32, r32, AF.Square),
                           [prev.get("sqfree")])
                prev["r32free"] = ("s", tsq)
                tsh = s_op(lambda h: h.copy(sqhi, sq32), [prev.get("shfree")])
                tsl = v_op(lambda h: h.tensor_sub(sqlo, sq32, sqhi),
                           [("s", tsh), prev.get("slfree")])
                prev["sqfree"] = ("v", tsl)

                t1 = sch.dmat(blocks(sqThi, 16), blocks(sqhi, 16),
                              [("s", tsh), prev.get("avdone")])
                t2 = sch.dmat(blocks(sqTlo, 16), blocks(sqlo, 16),
                              [("v", tsl), prev.get("avdone")])

                prev["shfree"] = t1
                prev["slfree"] = t2
                lastmm = None
                pairs = ((sqThi, vh_sb), (sqThi, vl_sb), (sqTlo, vh_sb))
                for es in range(4):
                    for pi, (sv, vv) in enumerate(pairs):
                        for kc in range(16):
                            st_ = (pi == 0 and kc == 0)
                            sp_ = (pi == 2 and kc == 15)
                            lastmm = t_mm(
                                bank(4 + es), sv[:, kc * 128:(kc + 1) * 128],
                                vv[:, kc * E + es * 512:
                                   kc * E + (es + 1) * 512], st_, sp_,
                                waits=([t1, t2,
                                        bank_free[4 + es]] if st_ else ()),
                                inc="pe" if (sp_ and es == 3) else None)
                prev["avdone"] = ("pe", lastmm)
                tadd = v_op(lambda h: h.tensor_add(ut32, uth, utl),
                            [tul1, tul2, ("s", tsq)])
                tua = None
                for bb in range(4):
                    tua = v_op(lambda h, bb=bb: h.tensor_mul(
                        uav[:, bb * 512:(bb + 1) * 512],
                        ut32[:, bb * 512:(bb + 1) * 512], bank(4 + bb)),
                        [("pe", lastmm), prev.get("uavfree"),
                         prev.get("tuhtok"), ("s", tsh), ("v", tadd)]
                        if bb == 0 else ())
                for es in range(4):
                    bank_free[4 + es] = ("v", tua)
                prev["ufree"] = [("v", tadd)]
                prev["ut32free"] = ("v", tua)
                tuh = s_op(lambda h: h.copy(uavhi, uav),
                           [("v", tua), prev.get("uhfree")])
                prev["tuhtok"] = ("s", tuh)
                tul2 = v_op(lambda h: h.tensor_sub(uavlo, uav, uavhi),
                            [("s", tuh), prev.get("ulfree")])
                prev["uavfree"] = ("v", tul2)
                prev["sqfree"] = [("v", tul2), ("s", tuh)]
                t3 = sch.dmat(blocks(uavThi, 16), blocks(uavhi, 16),
                              [("s", tuh), prev.get("uths")])
                t4 = sch.dmat(blocks(uavTlo, 16), blocks(uavlo, 16),
                              [("v", tul2), prev.get("utls")])
                prev["uhfree"] = t3
                prev["ulfree"] = t4
                su1 = sch.dma("gpsimd",
                              kpf(s_uavT_h[:, t * 128:(t + 1) * 128]),
                              blocks(uavThi, 16), waits=[t3],
                              inc="st")
                su2 = sch.dma("gpsimd",
                              kpf(s_uavT_l[:, t * 128:(t + 1) * 128]),
                              blocks(uavTlo, 16), waits=[t4],
                              inc="st")
                prev["uths"] = su1
                prev["utls"] = su2
                if SAFE_MODE:
                    sch.barrier()
            A.reset()

        def l0a_c(xsrc, xdst):
            A.mark()
            wouth = A.alloc(16 * D, BF16)
            woutl = A.alloc(16 * D, BF16)
            borep = A.alloc(D, F32)
            t1b = A.alloc(D, F32)
            slots = [{
                "uTh": A.alloc(E, BF16),
                "uTl": A.alloc(E, BF16),
                "xt": A.alloc(D, F32),
                "o": A.alloc(D, F32),
            } for _ in range(2)]
            tw1 = sch.dma("sync", blocks(wouth, 16), kpf(s_wout_h))
            tw2 = sch.dma("sync", blocks(woutl, 16), kpf(s_wout_l))
            tb0 = sch.dma("sync", borep,
                          row(inp["L0_attn_b_out"]).broadcast_to([128, D]))
            prev = {}
            for t in range(NT):
                sl = t % 2
                S = slots[sl]
                lw = prev.get(("free", sl), ())
                th = sch.dma("gpsimd", blocks(S["uTh"], 16),
                             kpf(s_uavT_h[:, t * 128:(t + 1) * 128]),
                             waits=lw, sem="eh%d" % sl)
                tl = sch.dma("gpsimd", blocks(S["uTl"], 16),
                             kpf(s_uavT_l[:, t * 128:(t + 1) * 128]),
                             sem="el%d" % sl)
                tx = sch.dma("gpsimd", S["xt"],
                             xsrc[t * 128:(t + 1) * 128, :],
                             sem="ex%d" % sl)
                pb = 0 if sl == 0 else 2
                opsum = bankpair(pb)
                mmtok = None
                pairs = ((S["uTh"], wouth), (S["uTh"], woutl),
                         (S["uTl"], wouth))
                for pi, (uv, wv) in enumerate(pairs):
                    for kc in range(16):
                        for n in range(2):
                            st_ = (pi == 0 and kc == 0)
                            sp_ = (pi == 2 and kc == 15)
                            w = ()
                            if st_:
                                w = [th, tl, tw1,
                                     tw2, bank_free[pb + n]]
                            mmtok = t_mm(
                                opsum[:, n * 512:(n + 1) * 512],
                                uv[:, kc * 128:(kc + 1) * 128],
                                wv[:, kc * D + n * 512:
                                   kc * D + (n + 1) * 512],
                                st_, sp_, waits=w,
                                inc="pe" if (sp_ and n == 1) else None)
                te0 = v_op(lambda h, S=S, opsum=opsum: h.scalar_tensor_tensor(
                    t1b, opsum, 1.0, S["xt"], ALU.mult, ALU.add),
                    [("pe", mmtok), tx, tb0,
                     prev.get("t1bfree"), prev.get(("sfree", sl))])
                bank_free[pb] = ("v", te0)
                bank_free[pb + 1] = ("v", te0)
                te1 = v_op(lambda h, S=S: h.tensor_add(S["o"], t1b, borep))
                prev["t1bfree"] = ("v", te1)
                tst = sch.dma("gpsimd", xdst[t * 128:(t + 1) * 128, :],
                              S["o"], waits=[("v", te1)], sem="bs%d" % sl)
                prev[("free", sl)] = [("pe", mmtok), ("v", te1)]
                prev[("sfree", sl)] = tst
                if SAFE_MODE:
                    sch.barrier()
            A.reset()

        def l1_ffn(xsrc, xdst):
            A.mark()
            wq = A.alloc(8 * D, BF16)
            hsc = A.alloc(1, F32)
            lnb = [alloc_ln_slot() for _ in range(2)]
            obuf = [A.alloc(D, F32) for _ in range(2)]
            twl = sch.dma("sync", blocks(wq, 8), kpf(s_wl1qT))
            thsc = sch.dma("sync", hsc,
                           row(s_scal[5:6]).broadcast_to([128, 1]))
            prev = {}
            ptok = [None, None]
            for t in range(NT):
                sl = t % 2
                tok = emit_ln_tile(xsrc, t, lnb[sl], prevtok=ptok[sl],
                                   yT_waits=prev.get(("ytfree", sl), ()),
                                   extra_x_waits=[prev.get(("xlate", sl))],
                                   tag="f%d" % sl)
                ptok[sl] = tok
                pb = 0 if sl == 0 else 2
                opsum = bankpair(pb)
                mmtok = None
                for pi in range(2):
                    yT = (lnb[sl]["yThi"], lnb[sl]["yTlo"])[pi]
                    for kd in range(8):
                        for n in range(2):
                            st_ = (pi == 0 and kd == 0)
                            sp_ = (pi == 1 and kd == 7)
                            w = ()
                            if st_:
                                w = [tok["yThi"], tok["yTlo"],
                                     twl, bank_free[pb + n]]
                            mmtok = t_mm(
                                opsum[:, n * 512:(n + 1) * 512],
                                yT[:, kd * 128:(kd + 1) * 128],
                                wq[:, kd * D + n * 512:
                                   kd * D + (n + 1) * 512],
                                st_, sp_, waits=w,
                                inc="pe" if (sp_ and n == 1) else None)
                te1 = v_op(lambda h, sl=sl, opsum=opsum:
                           h.scalar_tensor_tensor(obuf[sl], opsum, hsc,
                                                  lnb[sl]["xt"], ALU.mult,
                                                  ALU.add),
                           [("pe", mmtok), thsc,
                            prev.get(("sfree", sl))])
                bank_free[pb] = ("v", te1)
                bank_free[pb + 1] = ("v", te1)
                tst = sch.dma("gpsimd", xdst[t * 128:(t + 1) * 128, :],
                              obuf[sl], waits=[("v", te1)], sem="fs%d" % sl)
                prev[("ytfree", sl)] = [("pe", mmtok)]
                prev[("sfree", sl)] = tst
                prev[("xlate", sl)] = ("v", te1)
                if SAFE_MODE:
                    sch.barrier()
            A.reset()

        def l1a_a(xsrc):
            A.mark()
            wq = {}
            for w in ("q", "k", "v"):
                wq[w] = (A.alloc(8 * D, BF16), A.alloc(8 * D, BF16))
            lnb = [alloc_ln_slot(with_yT=False) for _ in range(2)]
            yTh_m = A.alloc(8 * 512, BF16)
            yTl_m = A.alloc(8 * 512, BF16)
            eh = A.alloc(512, BF16)
            el = A.alloc(512, BF16)
            vh = A.alloc(D, BF16)
            vl = A.alloc(D, BF16)
            tld = {}
            for w in ("q", "k", "v"):
                tld[w] = (sch.dma("sync", blocks(wq[w][0], 8),
                                  kpf(s_wT[w][0]), sem="w%sh" % w),
                          sch.dma("sync", blocks(wq[w][1], 8),
                                  kpf(s_wT[w][1]), sem="w%sl" % w))
            dstmap = {"q": (s_q1T_h, s_q1T_l), "k": (s_k1T_h, s_k1T_l)}
            prev = {}
            ptok = [None, None]
            for mc in range(4):
                mtoks = []
                for si in range(4):
                    t = mc * 4 + si
                    sl = t % 2
                    tok = emit_ln_tile(
                        xsrc, t, lnb[sl], prevtok=ptok[sl],
                        yT_dsts=(blocks(yTh_m, 8)
                                 [:, :, si * 128:(si + 1) * 128],
                                 blocks(yTl_m, 8)
                                 [:, :, si * 128:(si + 1) * 128]),
                        yT_waits=prev.get("ymfree", ()), tag="g%d" % sl)
                    ptok[sl] = tok
                    mtoks.append(tok)
                wmac = []
                for tok in mtoks:
                    wmac += [tok["yThi"], tok["yTlo"]]
                lastmm = None
                for w in ("q", "k"):
                    wh, wl_ = wq[w]
                    dh, dl = dstmap[w]
                    for mt in range(8):
                        b = mt % 4
                        pairs = ((wh, yTh_m), (wh, yTl_m), (wl_, yTh_m))
                        for pi, (wv, yv) in enumerate(pairs):
                            for kd in range(8):
                                st_ = (pi == 0 and kd == 0)
                                sp_ = (pi == 2 and kd == 7)
                                lastmm = t_mm(
                                    bank(b),
                                    wv[:, kd * D + mt * 128:
                                       kd * D + (mt + 1) * 128],
                                    yv[:, kd * 512:(kd + 1) * 512], st_, sp_,
                                    waits=(wmac + [tld[w][0],
                                                   tld[w][1],
                                                   bank_free[b]])
                                    if st_ else (),
                                    inc="pe" if sp_ else None)
                        teh = s_op(lambda h, b=b: h.copy(eh, bank(b)),
                                   [("pe", lastmm), prev.get("ehfree")])
                        tel = v_op(lambda h, b=b: h.tensor_sub(el, bank(b),
                                                               eh),
                                   [("s", teh), prev.get("elfree")])
                        bank_free[b] = ("v", tel)
                        se1 = sch.dma("gpsimd",
                                      dh[mt * 128:(mt + 1) * 128,
                                         mc * 512:(mc + 1) * 512], eh,
                                      waits=[("s", teh)], inc="st")
                        se2 = sch.dma("gpsimd",
                                      dl[mt * 128:(mt + 1) * 128,
                                         mc * 512:(mc + 1) * 512], el,
                                      waits=[("v", tel)], inc="st")
                        prev["ehfree"] = [se1, ("v", tel)]
                        prev["elfree"] = se2
                wh, wl_ = wq["v"]
                for si in range(4):
                    t = mc * 4 + si
                    pb = 4 if si % 2 == 0 else 6
                    opsum = bankpair(pb)
                    lastmm = None
                    pairs = ((yTh_m, wh), (yTh_m, wl_), (yTl_m, wh))
                    for pi, (yv, wv) in enumerate(pairs):
                        for kd in range(8):
                            for n in range(2):
                                st_ = (pi == 0 and kd == 0)
                                sp_ = (pi == 2 and kd == 7)
                                w_ = ()
                                if st_:
                                    w_ = wmac + [tld["v"][0],
                                                 tld["v"][1],
                                                 bank_free[pb + n]]
                                lastmm = t_mm(
                                    opsum[:, n * 512:(n + 1) * 512],
                                    yv[:, kd * 512 + si * 128:
                                       kd * 512 + (si + 1) * 128],
                                    wv[:, kd * D + n * 512:
                                       kd * D + (n + 1) * 512],
                                    st_, sp_, waits=w_,
                                    inc="pe" if (sp_ and n == 1) else None)
                    tvh = s_op(lambda h, opsum=opsum: h.copy(vh, opsum),
                               [("pe", lastmm), prev.get("vhfree")])
                    tvl = v_op(lambda h, opsum=opsum: h.tensor_sub(vl, opsum,
                                                                   vh),
                               [("s", tvh), prev.get("vlfree")])
                    bank_free[pb] = ("v", tvl)
                    bank_free[pb + 1] = ("v", tvl)
                    sv1 = sch.dma("gpsimd", s_v1_h[t * 128:(t + 1) * 128, :],
                                  vh, waits=[("s", tvh)], inc="st")
                    sv2 = sch.dma("gpsimd", s_v1_l[t * 128:(t + 1) * 128, :],
                                  vl, waits=[("v", tvl)], inc="st")
                    prev["vhfree"] = [sv1, ("v", tvl)]
                    prev["vlfree"] = sv2
                prev["ymfree"] = [("pe", lastmm)]
                if SAFE_MODE:
                    sch.barrier()
            A.reset()

        def l1a_b(xsrc, xdst):
            A.mark()
            k1h = A.alloc(8 * T, BF16)
            k1l = A.alloc(8 * T, BF16)
            v1h = A.alloc(16 * D, BF16)
            v1l = A.alloc(16 * D, BF16)
            woh = A.alloc(8 * D, BF16)
            wol = A.alloc(8 * D, BF16)
            m_mid = A.alloc(512, F32)
            m_r1 = A.alloc(384, F32)
            m_r15 = A.alloc(384, F32)
            m_r0a = A.alloc(512, F32)
            m_r0b = A.alloc(512, F32)
            qrh = A.alloc(D, BF16)
            qrl = A.alloc(D, BF16)
            xt = A.alloc(D, F32)
            sm32 = A.alloc(T, F32)
            e32 = A.alloc(512, F32)
            phi = A.alloc(512, BF16)
            plo = A.alloc(512, BF16)
            pThi = A.alloc(512, BF16)
            pTlo = A.alloc(512, BF16)
            ao = A.alloc(D, F32)
            aohi = A.alloc(D, BF16)
            aolo = A.alloc(D, BF16)
            aoThi = A.alloc(D, BF16)
            aoTlo = A.alloc(D, BF16)
            ob = ao
            sst = A.alloc(4, F32)
            tk1 = sch.dma("sync", blocks(k1h, 8), kpf(s_k1T_h))
            tk2 = sch.dma("sync", blocks(k1l, 8), kpf(s_k1T_l))
            tv1 = sch.dma("sync", blocks(v1h, 16),
                          s_v1_h.rearrange("(c p) e -> p c e", p=128))
            tv2 = sch.dma("sync", blocks(v1l, 16),
                          s_v1_l.rearrange("(c p) e -> p c e", p=128))
            tw1 = sch.dma("sync", blocks(woh, 8), kpf(s_wT["o"][0]))
            tw2 = sch.dma("sync", blocks(wol, 8), kpf(s_wT["o"][1]))
            wres = [x for x in (tk1, tk2, tv1, tv2, tw1, tw2)]

            tmask = None

            def mk_mask(ap, sels):
                nonlocal tmask
                tmask = g_op(lambda h, ap=ap: h.memset(ap, 0.0))
                for (cm, coef, base, fill) in sels:
                    tmask = g_op(lambda h, ap=ap, cm=cm, coef=coef, base=base,
                                 fill=fill: h.affine_select(
                                     out=ap, in_=ap, compare_op=ALU.is_ge,
                                     fill=fill, base=base,
                                     pattern=[[coef, ap.shape[-1]]],
                                     channel_multiplier=cm))

            mk_mask(m_mid[:, 0:128], [(0, -1, 0, NEG)])
            mk_mask(m_mid[:, 128:256], [(-1, 1, 0, NEG)])
            mk_mask(m_mid[:, 256:384], [])
            mk_mask(m_mid[:, 384:512], [(1, -1, 0, NEG)])
            mk_mask(m_r1[:, 0:128], [(-1, 1, 0, NEG), (0, 1, -1, 0.0)])
            mk_mask(m_r1[:, 128:256], [])
            mk_mask(m_r1[:, 256:384], [(1, -1, 0, NEG)])
            mk_mask(m_r15[:, 0:128], [(0, -1, 0, NEG)])
            mk_mask(m_r15[:, 128:256], [(-1, 1, 0, NEG)])
            mk_mask(m_r15[:, 256:384], [])
            mk_mask(m_r0a[:, 0:128], [])
            mk_mask(m_r0a[:, 128:256], [(1, -1, 0, NEG), (1, 0, -1, 0.0)])
            mk_mask(m_r0a[:, 256:512], [(-1, 0, 0, NEG)])
            mk_mask(m_r0b, [(-1, 0, 0, NEG)])
            wmask = ("g", tmask)

            nmx = sst[:, 0:1]
            sume = sst[:, 1:2]
            rcp = sst[:, 2:3]
            scols = A.alloc(4, F32)
            prev = {}
            for r in range(NT):
                tq1 = sch.dma("gpsimd", blocks(qrh, 8),
                              kpf(s_q1T_h[:, r * 128:(r + 1) * 128]),
                              waits=prev.get("qfree", ()))
                tq2 = sch.dma("gpsimd", blocks(qrl, 8),
                              kpf(s_q1T_l[:, r * 128:(r + 1) * 128]))
                txl = sch.dma("gpsimd", xt, xsrc[r * 128:(r + 1) * 128, :],
                              waits=prev.get("xfree", ()))
                if r == 0:
                    W = 2048
                    mbs = list(range(16))
                    rhs_slices = [(i * 512, 512, i * 512) for i in range(4)]
                elif r == 1:
                    W = 384
                    mbs = [0, 1, 2]
                    rhs_slices = [(0, 384, 0)]
                elif r == 15:
                    W = 384
                    mbs = [0, 14, 15]
                    rhs_slices = [(0, 128, 0), (14 * 128, 256, 128)]
                else:
                    W = 512
                    mbs = [0, r - 1, r, r + 1]
                    rhs_slices = [(0, 128, 0), ((r - 1) * 128, 384, 128)]
                pieces = []
                for pi in range(3):
                    for (mcol, wdt, pcol) in rhs_slices:
                        pieces.append((pi, mcol, wdt, pcol))
                first_by_bank = {}
                last_by_bank = {}
                for idx, (pi, mc2, wd2, pc2) in enumerate(pieces):
                    b = pc2 // 512
                    if b not in first_by_bank:
                        first_by_bank[b] = idx
                    last_by_bank[b] = idx
                lastao = None
                for hd_ in range(NH):
                    c = hd_ // 2
                    rw = slice((hd_ % 2) * 64, (hd_ % 2) * 64 + 64)
                    if r == 0:
                        ps = pbig1[:, 0:2048]
                        pbanks = [0, 1, 2, 3]
                    else:
                        ps = bank(hd_ % 2)[:, 0:W]
                        pbanks = [hd_ % 2]
                    qpairs = ((qrh, k1h), (qrh, k1l), (qrl, k1h))
                    lastmm = None
                    for idx, (pi, mc2, wd2, pc2) in enumerate(pieces):
                        qv, kv = qpairs[pi]
                        b = pc2 // 512
                        st_ = (idx == first_by_bank[b])
                        sp_ = (idx == last_by_bank[b])
                        w_ = ()
                        if idx == 0:
                            w_ = (wres + [tq1, tq2] +
                                  [bank_free[bb] for bb in pbanks])
                        lastmm = t_mm(
                            ps[:, pc2:pc2 + wd2],
                            qv[rw, c * 128:(c + 1) * 128],
                            kv[rw, c * T + mc2: c * T + mc2 + wd2],
                            st_, sp_, waits=w_,
                            inc="pe" if idx == len(pieces) - 1 else None)
                    tsm = None
                    smw = [("pe", lastmm), wmask, prev.get("smfree")]
                    if r == 0:
                        for ck in range(4):
                            mk = m_r0a if ck == 0 else m_r0b
                            tsm = v_op(lambda h, ck=ck, mk=mk:
                                       h.scalar_tensor_tensor(
                                           sm32[:, ck * 512:(ck + 1) * 512],
                                           pbig1[:, ck * 512:(ck + 1) * 512],
                                           0.125, mk, ALU.mult, ALU.add),
                                       smw if ck == 0 else ())
                    else:
                        mk = m_r1 if r == 1 else (m_r15 if r == 15 else m_mid)
                        tsm = v_op(lambda h, ps=ps, mk=mk, W=W:
                                   h.scalar_tensor_tensor(
                                       sm32[:, 0:W], ps, 0.125, mk[:, 0:W],
                                       ALU.mult, ALU.add), smw)
                    for bb in pbanks:
                        bank_free[bb] = ("v", tsm)
                    tnm = v_op(lambda h, W=W: h.reduce_max(
                        nmx, sm32[:, 0:W], axis=AX.X, negate=True),
                        [prev.get("nmxfree")])
                    nch = max(1, W // 512)
                    avb = 4 + (hd_ % 2)
                    avp = bank(avb)[:, 0:64]
                    ppairs = ((pThi, v1h), (pThi, v1l), (pTlo, v1h))
                    lastav = None
                    texp = None
                    for ck in range(nch):
                        cw = min(512, W)
                        texp = s_op(lambda h, ck=ck, cw=cw: h.activation(
                            e32[:, 0:cw], sm32[:, ck * 512:ck * 512 + cw],
                            AF.Exp, bias=nmx,
                            accum_out=scols[:, ck:ck + 1]),
                            [("v", tnm), prev.get("sumefree"),
                             prev.get("e32free")])
                        tph = s_op(lambda h, cw=cw: h.copy(phi[:, 0:cw],
                                                           e32[:, 0:cw]),
                                   [prev.get("phifree")])
                        tpl = v_op(lambda h, cw=cw: h.tensor_sub(
                            plo[:, 0:cw], e32[:, 0:cw], phi[:, 0:cw]),
                            [("s", tph), prev.get("plofree")])
                        prev["e32free"] = ("v", tpl)
                        tt1 = sch.dmat(blocks(pThi[:, 0:cw], cw // 128),
                                       blocks(phi[:, 0:cw], cw // 128),
                                       [("s", tph), prev.get("avdone")])
                        tt2 = sch.dmat(blocks(pTlo[:, 0:cw], cw // 128),
                                       blocks(plo[:, 0:cw], cw // 128),
                                       [("v", tpl), prev.get("avdone")])
                        prev["phifree"] = tt1
                        prev["plofree"] = tt2
                        mbs_ck = mbs[ck * 4:ck * 4 + 4] if r == 0 else mbs
                        for bi, mb in enumerate(mbs_ck):
                            for pi2, (pv, vv) in enumerate(ppairs):
                                st_ = (ck == 0 and bi == 0 and pi2 == 0)
                                sp_ = (ck == nch - 1 and
                                       bi == len(mbs_ck) - 1 and pi2 == 2)
                                cl_ = (bi == len(mbs_ck) - 1 and pi2 == 2)
                                lastav = t_mm(
                                    avp, pv[:, bi * 128:(bi + 1) * 128],
                                    vv[:, mb * D + hd_ * 64:
                                       mb * D + hd_ * 64 + 64],
                                    st_, sp_,
                                    waits=[tt1, tt2] +
                                    ([bank_free[avb]] if st_ else []),
                                    inc="pe" if cl_ else None)
                        prev["avdone"] = ("pe", lastav)
                    prev["smfree"] = ("s", texp)
                    prev["nmxfree"] = ("s", texp)
                    tsum = v_op(lambda h, nch=nch: h.reduce_sum(
                        sume, scols[:, 0:nch], axis=AX.X), [("s", texp)])
                    trcp = v_op(lambda h: h.reciprocal(rcp, sume))
                    prev["sumefree"] = ("v", trcp)
                    aow = [("pe", lastav), ("v", trcp)]
                    if hd_ == 0:
                        aow += [prev.get("aofree1"), prev.get("aofree2"),
                                prev.get("obfree")]
                    tao = v_op(lambda h, hd_=hd_, avp=avp: h.tensor_scalar(
                        ao[:, hd_ * 64:(hd_ + 1) * 64], avp, rcp, None,
                        ALU.mult), aow)
                    bank_free[avb] = ("v", tao)
                    lastao = tao
                prev["qfree"] = [("pe", lastmm)]
                taoh = s_op(lambda h: h.copy(aohi, ao),
                            [("v", lastao), prev.get("t3free")])
                taol = v_op(lambda h: h.tensor_sub(aolo, ao, aohi),
                            [("s", taoh), prev.get("t4free")])
                prev["aofree1"] = ("s", taoh)
                prev["aofree2"] = ("v", taol)
                t3 = sch.dmat(blocks(aoThi, 8), blocks(aohi, 8),
                              [("s", taoh), prev.get("odone")])
                t4 = sch.dmat(blocks(aoTlo, 8), blocks(aolo, 8),
                              [("v", taol), prev.get("odone")])
                prev["t3free"] = t3
                prev["t4free"] = t4
                opsum = bankpair(6)
                opairs = ((aoThi, woh), (aoThi, wol), (aoTlo, woh))
                mmo = None
                for pi, (av, wv) in enumerate(opairs):
                    for kd in range(8):
                        for n in range(2):
                            st_ = (pi == 0 and kd == 0)
                            sp_ = (pi == 2 and kd == 7)
                            w_ = ()
                            if st_:
                                w_ = [t3, t4,
                                      bank_free[6 + n]]
                            mmo = t_mm(
                                opsum[:, n * 512:(n + 1) * 512],
                                av[:, kd * 128:(kd + 1) * 128],
                                wv[:, kd * D + n * 512:
                                   kd * D + (n + 1) * 512],
                                st_, sp_, waits=w_,
                                inc="pe" if (sp_ and n == 1) else None)
                prev["odone"] = ("pe", mmo)
                te = v_op(lambda h: h.scalar_tensor_tensor(
                    ob, opsum, 1.0, xt, ALU.mult, ALU.add),
                    [("pe", mmo), txl, prev.get("obfree")])
                bank_free[6] = ("v", te)
                bank_free[7] = ("v", te)
                tst = sch.dma("gpsimd", xdst[r * 128:(r + 1) * 128, :], ob,
                              waits=[("v", te)], inc="st")
                prev["obfree"] = tst
                prev["xfree"] = [("v", te)]
                if SAFE_MODE:
                    sch.barrier()
            A.reset()

        cur = inp["L0_attn_inputs"]
        np_ = max(0, min(n_phases, N_PHASES))
        done = 0

        def take():
            nonlocal done
            if done < np_:
                done += 1
                return True
            return False

        if take():
            ffn0_a(cur)
            sch.barrier()
        if take():
            ffn0_b(cur, xB)
            cur = xB
            sch.barrier()
        if take():
            l0a_a(cur)
            sch.barrier()
        if take():
            l0a_b()
            sch.barrier()
        if take():
            l0a_c(cur, xA)
            cur = xA
            sch.barrier()
        if take():
            ffn0_a(cur)
            sch.barrier()
        if take():
            ffn0_b(cur, xB)
            cur = xB
            sch.barrier()
        if take():
            l1_ffn(cur, xA)
            cur = xA
            sch.barrier()
        if take():
            l1a_a(cur)
            sch.barrier()
        if take():
            l1a_b(cur, xB)
            cur = xB
            sch.barrier()
        if take():
            l1_ffn(cur, out)
            cur = out
            sch.barrier()
        if cur is not out:
            A.mark()
            tmp = [A.alloc(D, F32) for _ in range(2)]
            ptoks = [None, None]
            for t in range(NT):
                sl = t % 2
                tl = sch.dma("gpsimd", tmp[sl], cur[t * 128:(t + 1) * 128, :],
                             waits=[ptoks[sl]] if ptoks[sl] else (),
                             sem="oc%d" % sl)
                ts_ = sch.dma("gpsimd", out[t * 128:(t + 1) * 128, :],
                              tmp[sl], waits=[tl], sem="os%d" % sl)
                ptoks[sl] = ts_
            A.reset()
        if DEBUG_DUMP:
            scrmap = {k: v for k, v in locals().items()
                      if k.startswith("s_") or k in ("xA", "xB")}
            for nm in DEBUG_DUMP:
                ap = scrmap[nm]
                dbg = nc.dram_tensor("dbg_" + nm, list(ap.shape), ap.dtype,
                                     kind="ExternalOutput").ap()
                sch.barrier()
                sch.dma("gpsimd", dbg, ap, inc="st")
        sch.barrier()
        sch.emit("sync", lambda h: h.nop())

        blk = stk.enter_context(nc.Block())
        sch.run(blk)
    return nc


def kernel(**inputs):
    x = np.ascontiguousarray(np.asarray(inputs["L0_attn_inputs"],
                                        dtype=np.float32))
    base = {k: np.ascontiguousarray(np.asarray(v, dtype=np.float32))
            for k, v in inputs.items() if k != "L0_attn_inputs"}
    nc = build_nc(N_PHASES_RUN)
    in_maps = []
    for i in range(8):
        m = dict(base)
        m["L0_attn_inputs"] = np.ascontiguousarray(x[i])
        in_maps.append(m)
    res = run_bass_kernel_spmd(nc, in_maps, core_ids=list(range(8)))
    return np.stack([np.asarray(r["out"]) for r in res.results], axis=0)


# version 07
# speedup vs baseline: 1.0863x; 1.0103x over previous
import contextlib
import numpy as np
import concourse.bass as bass
import concourse.mybir as mybir
from concourse.bass_utils import run_bass_kernel_spmd

F32 = mybir.dt.float32
BF16 = mybir.dt.bfloat16
I8 = mybir.dt.int8
AF = mybir.ActivationFunctionType
ALU = mybir.AluOpType
AX = mybir.AxisListType

T = 2048
D = 1024
NT = T // 128
HF = 4096
FUV = 4224
E = 2048
NH = 16
NEG = -1.0e30

PHASES = ["p1a", "p1b", "p2a", "p2b", "p2c", "p3a", "p3b", "p4", "p5a", "p5b",
          "p6"]
N_PHASES = len(PHASES)
N_PHASES_RUN = N_PHASES
SAFE_MODE = False
DEBUG_DUMP = []

ENGS = ("sync", "gpsimd", "vector", "scalar", "tensor")


class Sched:
    def __init__(self, nc, stk):
        self.nc = nc
        self.stk = stk
        self.q = {e: [] for e in ENGS}
        self.sems = {}
        self.cnt = {}
        self.barrier_pending = {e: None for e in ENGS}
        self.site_map = {}
        self.pool_base = 0

    def sem(self, name):
        if name not in self.sems:
            self.sems[name] = self.stk.enter_context(self.nc.semaphore(name))
            self.cnt[name] = 0
        return self.sems[name]

    def site_sem(self, key, kind):
        if key not in self.site_map:
            n = sum(1 for k2 in self.site_map if self.site_map[k2][1] == kind)
            name = "q%s%d" % (kind, n)
            assert n < 34, "sem pool exhausted"
            self.site_map[key] = (name, kind)
        return self.site_map[key][0]

    def emit(self, eng, builder, waits=(), inc=None, inc_n=1):
        wmax = {}
        bp = self.barrier_pending[eng]
        if bp is not None:
            for s, v in bp.items():
                if v > 0:
                    wmax[s] = v
            self.barrier_pending[eng] = None
        def add(wv):
            if wv is None:
                return
            if isinstance(wv, list):
                for w2 in wv:
                    add(w2)
                return
            s, v = wv
            if isinstance(v, tuple):
                add(v)
                return
            if v is not None and v > 0:
                wmax[s] = max(wmax.get(s, 0), v)
        for wv in waits:
            add(wv)
        if inc in ("v", "s", "g"):
            self.sem(inc)
            prevcnt = self.cnt[inc]
            if prevcnt > 0:
                wmax[inc] = max(wmax.get(inc, 0), prevcnt)
        for s in wmax:
            self.sem(s)
        tok = None
        if inc is not None:
            self.sem(inc)
            self.cnt[inc] += inc_n
            tok = self.cnt[inc]
        self.q[eng].append((wmax, builder, inc, inc_n))
        return tok

    def dma(self, eng, out, in_, waits=(), inc="ld", slow=False, sem=None):
        def b(h, out=out, in_=in_, slow=slow):
            return h.dma_start(out=out, in_=in_,
                               allow_slow_non_contiguous=slow)
        kind = "g" if eng == "gpsimd" else "s"
        if sem is None:
            import inspect
            sem = self.site_sem(
                "d%d" % inspect.currentframe().f_back.f_lineno, kind)
        else:
            sem = self.site_sem("d_" + sem, kind)
        tok = self.emit(eng, b, waits, sem, 16)
        return (sem, tok)

    def dmat(self, out, in_, waits=(), sem=None, q="sync"):
        def b(h, out=out, in_=in_):
            return h.dma_start_transpose(out=out, in_=in_)
        if sem is None:
            import inspect
            sem = self.site_sem(
                "t%d" % inspect.currentframe().f_back.f_lineno, "t")
        else:
            sem = self.site_sem("t_" + sem, "t")
        return (sem, self.emit(q, b, waits, sem, 16))

    def barrier(self):
        snap = dict(self.cnt)
        for e in ENGS:
            self.barrier_pending[e] = snap
        self.site_map = {}

    def run(self, blk):
        sch = self

        def mk(eng):
            def body(h):
                hwm = {}
                for wmax, builder, inc, inc_n in sch.q[eng]:
                    for s, v in wmax.items():
                        if hwm.get(s, -1) >= v:
                            continue
                        h.wait_ge(sch.sems[s], v)
                        hwm[s] = v
                    inst = builder(h)
                    if inc is not None:
                        inst.then_inc(sch.sems[inc], inc_n)
            return body

        blk.sync(mk("sync"))
        blk.gpsimd(mk("gpsimd"))
        blk.vector(mk("vector"))
        blk.scalar(mk("scalar"))
        blk.tensor(mk("tensor"))


class Arena:
    def __init__(self, ap, nbytes):
        self.ap = ap
        self.nbytes = nbytes
        self.off = 0
        self.marks = []

    def alloc(self, n_elem, dt, parts=128):
        sz = n_elem * mybir.dt.size(dt)
        off = (self.off + 63) & ~63
        assert off + sz <= self.nbytes, f"arena overflow {off}+{sz}"
        self.off = off + sz
        return self.ap[:parts, off:off + sz].bitcast(dt)

    def mark(self):
        self.marks.append(self.off)

    def reset(self):
        self.off = self.marks.pop()


def blocks(ap, nblk):
    return ap.rearrange("p (k f) -> p k f", k=nblk)


def kpf(dram_ap):
    return dram_ap.rearrange("(k p) f -> p k f", p=128)


def row(dram_ap):
    return dram_ap.rearrange("(one f) -> one f", one=1)


def build_nc(n_phases=N_PHASES):
    nc = bass.Bass("TRN2", target_bir_lowering=False)

    inp = {}
    for name, shp in [
        ("L0_attn_inputs", [T, D]), ("L0_attn_W_uv", [D, FUV]),
        ("L0_attn_b_uv", [FUV]), ("L0_attn_gamma", [2, 128]),
        ("L0_attn_beta", [2, 128]), ("L0_attn_W_out", [E, D]),
        ("L0_attn_b_out", [D]), ("L0_ffn_W_fc", [HF, D]),
        ("L0_ffn_ln_w", [HF]), ("L0_ffn_ln_b", [HF]),
        ("L0_ffn_W_proj", [D, HF]), ("L1_attn_Wq", [D, D]),
        ("L1_attn_Wk", [D, D]), ("L1_attn_Wv", [D, D]),
        ("L1_attn_Wo", [D, D]), ("L1_ffn_weight", [D, D]),
    ]:
        inp[name] = nc.dram_tensor(name, shp, F32, kind="ExternalInput").ap()
    out = nc.dram_tensor("out", [T, D], F32, kind="ExternalOutput").ap()

    def scr(name, shp, dt):
        return nc.dram_tensor(name, shp, dt, kind="Internal").ap()

    xA = scr("xA", [T, D], F32)
    xB = scr("xB", [T, D], F32)
    s_wfcqT = scr("s_wfcqT", [D, HF], BF16)
    s_wprojqT = scr("s_wprojqT", [HF, D], BF16)
    s_wuv_h = scr("s_wuv_h", [D, FUV], BF16)
    s_wuv_l = scr("s_wuv_l", [D, FUV], BF16)
    s_wout_h = scr("s_wout_h", [E, D], BF16)
    s_wout_l = scr("s_wout_l", [E, D], BF16)
    s_wT = {}
    for w in ("q", "k", "v", "o"):
        s_wT[w] = (scr(f"s_w{w}T_h", [D, D], BF16),
                   scr(f"s_w{w}T_l", [D, D], BF16))
    s_wl1qT = scr("s_wl1qT", [D, D], BF16)
    s_hnT_h = scr("s_hnT_h", [HF, T], BF16)
    s_hnT_l = scr("s_hnT_l", [HF, T], BF16)
    s_u_h = scr("s_u_h", [T, E], BF16)
    s_u_l = scr("s_u_l", [T, E], BF16)
    s_v_h = scr("s_v_h", [T, E], BF16)
    s_v_l = scr("s_v_l", [T, E], BF16)
    s_qh = scr("s_qh", [128, T], BF16)
    s_ql = scr("s_ql", [128, T], BF16)
    s_kh = scr("s_kh", [128, T], BF16)
    s_kl = scr("s_kl", [128, T], BF16)
    s_uavT_h = scr("s_uavT_h", [E, T], BF16)
    s_uavT_l = scr("s_uavT_l", [E, T], BF16)
    s_q1T_h = scr("s_q1T_h", [D, T], BF16)
    s_q1T_l = scr("s_q1T_l", [D, T], BF16)
    s_k1T_h = scr("s_k1T_h", [D, T], BF16)
    s_k1T_l = scr("s_k1T_l", [D, T], BF16)
    s_v1_h = scr("s_v1_h", [T, D], BF16)
    s_v1_l = scr("s_v1_l", [T, D], BF16)
    s_dbgA = scr("s_dbgA", [128, 2048], F32)
    s_dbgB = scr("s_dbgB", [128, 2048], F32)
    s_dbgC = scr("s_dbgC", [128, 2048], F32)
    s_pr = scr("s_pr", [128], F32)
    s_scal = scr("s_scal", [8], F32)
    s_bconst = scr("s_bconst", [D], F32)

    stk = contextlib.ExitStack()
    with stk:
        SB_BYTES = 207 * 1024
        arena_t = stk.enter_context(nc.sbuf_tensor("arena", [128, SB_BYTES],
                                                   I8))
        A = Arena(arena_t[:], SB_BYTES)
        pbig1 = stk.enter_context(nc.psum_tensor("pbig1", [128, 2048], F32))
        pbig2 = stk.enter_context(nc.psum_tensor("pbig2", [128, 2048], F32))

        def bank(i):
            pb = pbig1 if i < 4 else pbig2
            return pb[:, (i % 4) * 512:(i % 4 + 1) * 512]

        def bankpair(i):
            pb = pbig1 if i < 4 else pbig2
            return pb[:, (i % 4) * 512:(i % 4 + 2) * 512]

        sch = Sched(nc, stk)
        for s in ("ld", "lg", "tp", "st", "sg", "pe", "v", "s", "g"):
            sch.sem(s)
        bank_free = {i: None for i in range(8)}

        def v_op(builder, waits=(), inc="v"):
            return sch.emit("vector", builder, waits, inc)

        def s_op(builder, waits=(), inc="s"):
            return sch.emit("scalar", builder, waits, inc)

        def g_op(builder, waits=(), inc="g"):
            return sch.emit("gpsimd", builder, waits, inc)

        def t_mm(psum_ap, lhsT, rhs, start, stop, waits=(), inc=None):
            def b(h, o=psum_ap, l=lhsT, r=rhs, st=start, sp=stop):
                return h.matmul(o, l, r, start=st, stop=sp)
            if inc is None:
                return sch.emit("tensor", b, waits)
            sch.emit("tensor", b, waits)
            return sch.emit("tensor", lambda h: h.drain(), (), inc)

        def alloc_ln_slot(with_yT=True):
            d = {
                "xt": A.alloc(D, F32),
                "y32": A.alloc(D, F32),
                "yhi": A.alloc(D, BF16),
                "ylo": A.alloc(D, BF16),
                "st": A.alloc(16, F32),
            }
            if with_yT:
                d["yThi"] = A.alloc(D, BF16)
                d["yTlo"] = A.alloc(D, BF16)
            return d

        def emit_ln_tile(xsrc, t, B, prevtok=None, yT_dsts=None, yT_waits=(),
                         extra_x_waits=(), tag="z"):
            xt, y32, yhi, ylo, stt = (B["xt"], B["y32"], B["yhi"], B["ylo"],
                                      B["st"])
            tok = {}
            slot_free = [w for w in extra_x_waits if w is not None]
            if prevtok is not None:
                slot_free += [("v", prevtok["ylo"]), ("s", prevtok["yhi"])]
            tok["x"] = sch.dma("gpsimd", xt, xsrc[t * 128:(t + 1) * 128, :],
                               waits=slot_free, sem=tag + "x")
            s1n, s2, mn, msq, va, vv, sd, rs, nm = [stt[:, i:i + 1]
                                                    for i in range(9)]
            wl = [tok["x"]] + slot_free
            v_op(lambda h: h.reduce_sum(s1n, xt, axis=AX.X, negate=True), wl)
            tsq = s_op(lambda h: h.activation(y32, xt, AF.Square,
                                              accum_out=s2), wl)
            v_op(lambda h: h.tensor_scalar(mn, s1n, 1.0 / D, None, ALU.mult))
            v_op(lambda h: h.tensor_mul(msq, mn, mn))
            v_op(lambda h: h.tensor_scalar(va, s2, 1.0 / D, 1e-5, ALU.mult,
                                           ALU.add), [("s", tsq)])
            tvv = v_op(lambda h: h.tensor_sub(vv, va, msq))
            tsd = s_op(lambda h: h.activation(sd, vv, AF.Sqrt), [("v", tvv)])
            v_op(lambda h: h.reciprocal(rs, sd), [("s", tsd)])
            tnm = v_op(lambda h: h.tensor_mul(nm, mn, rs))
            whi = [("v", tnm)]
            if prevtok is not None:
                whi.append(prevtok["yThi"])
            tok["yhi"] = s_op(
                lambda h: h.activation(yhi, xt, AF.Identity, bias=nm,
                                       scale=rs), whi)
            tok["y32"] = v_op(
                lambda h: h.tensor_scalar(y32, xt, rs, nm, ALU.mult, ALU.add),
                [("s", tok["yhi"])])
            wlo = []
            if prevtok is not None:
                wlo.append(prevtok["yTlo"])
            tok["ylo"] = v_op(lambda h: h.tensor_sub(ylo, y32, yhi), wlo)
            if yT_dsts is None:
                yT_dsts = (blocks(B["yThi"], 8), blocks(B["yTlo"], 8))
            tok["yThi"] = sch.dmat(yT_dsts[0], blocks(yhi, 8),
                                   [("s", tok["yhi"])] + list(yT_waits),
                                   sem=tag + "h")
            tok["yTlo"] = sch.dmat(yT_dsts[1], blocks(ylo, 8),
                                   [("v", tok["ylo"])] + list(yT_waits),
                                   sem=tag + "l")
            return tok

        A.mark()
        wt = [A.alloc(FUV, F32), A.alloc(FUV, F32)]
        qh = A.alloc(FUV, BF16)
        ql = A.alloc(FUV, BF16)
        qr = A.alloc(FUV, BF16)
        qT = A.alloc(FUV, BF16)
        qT2 = A.alloc(FUV, BF16)
        acc = A.alloc(64, F32)
        prrow = A.alloc(128, F32, parts=1)
        sc1 = A.alloc(4, F32, parts=1)
        thr_rep = A.alloc(1, F32)
        nthr_rep = A.alloc(1, F32)

        def prep_ternary(wsrc, nt, cols, dst, mean_div, use_max, eps,
                         scal_idx):
            last_v = None
            for t in range(nt):
                sl = t % 2
                wld = sch.dma("gpsimd", wt[sl][:, :cols],
                              wsrc[t * 128:(t + 1) * 128, :],
                              waits=[("v", last_v)] if last_v else ())
                last_v = v_op(lambda h, t=t, sl=sl: h.reduce_sum(
                    acc[:, t:t + 1], wt[sl][:, :cols], axis=AX.X,
                    apply_absolute_value=True), [wld])
            tacc = v_op(lambda h: h.reduce_sum(acc[:, 63:64], acc[:, 0:nt],
                                               axis=AX.X), [("v", last_v)])
            tst = sch.dma("gpsimd", s_pr.rearrange("(p one) -> p one", one=1),
                          acc[:, 63:64], waits=[("v", tacc)], inc="st")
            tld = sch.dma("gpsimd", prrow, row(s_pr), waits=[tst])
            v_op(lambda h: h.reduce_sum(sc1[:, 0:1], prrow, axis=AX.X),
                 [tld])
            if use_max:
                v_op(lambda h: h.tensor_scalar(sc1[:, 1:2], sc1[:, 0:1],
                                               mean_div, None, ALU.mult))
                v_op(lambda h: h.tensor_scalar(sc1[:, 1:2], sc1[:, 1:2], eps,
                                               None, ALU.max))
            else:
                v_op(lambda h: h.tensor_scalar(sc1[:, 1:2], sc1[:, 0:1],
                                               mean_div, eps, ALU.mult,
                                               ALU.add))
            tth = v_op(lambda h: h.tensor_scalar(sc1[:, 2:3], sc1[:, 1:2], 0.5,
                                                 None, ALU.mult))
            tsc = sch.dma("gpsimd", row(s_scal[2 * scal_idx:2 * scal_idx + 2]),
                          sc1[:, 1:3], waits=[("v", tth)], inc="st")
            tthr = sch.dma("gpsimd", thr_rep,
                           row(s_scal[2 * scal_idx + 1:2 * scal_idx + 2])
                           .broadcast_to([128, 1]), waits=[tsc])
            tnthr = v_op(lambda h: h.tensor_scalar(nthr_rep, thr_rep, -1.0,
                                                   None, ALU.mult),
                         [tthr])
            nb = cols // 128
            last_q = tnthr
            last_tp = None
            last_st = None
            for t in range(nt):
                sl = t % 2
                wld = sch.dma("gpsimd", wt[sl][:, :cols],
                              wsrc[t * 128:(t + 1) * 128, :],
                              waits=[("v", last_q)])
                v_op(lambda h, sl=sl: h.tensor_scalar(
                    qh[:, :cols], wt[sl][:, :cols], thr_rep, None, ALU.is_gt),
                    [wld])
                v_op(lambda h, sl=sl: h.tensor_scalar(
                    ql[:, :cols], wt[sl][:, :cols], nthr_rep, None,
                    ALU.is_lt))
                tq = v_op(lambda h: h.tensor_sub(qr[:, :cols], qh[:, :cols],
                                                 ql[:, :cols]),
                          [last_tp] if last_tp else ())
                last_q = tq
                ttp = sch.dmat(blocks(qT[:, :cols], nb),
                               blocks(qr[:, :cols], nb),
                               [("v", tq), last_st])
                last_tp = ttp
                last_st = sch.dma("gpsimd", kpf(dst[:, t * 128:(t + 1) * 128]),
                                  blocks(qT[:, :cols], nb),
                                  waits=[ttp], sem="pq%d" % (t % 2))

        def prep_split(wsrc, nt, cols, dsth, dstl, transpose_to=False):
            last_lo = None
            last_hi_cons = None
            last_lo_cons = None
            last_st1 = None
            last_st2 = None
            for t in range(nt):
                sl = t % 2
                wld = sch.dma("gpsimd", wt[sl][:, :cols],
                              wsrc[t * 128:(t + 1) * 128, :],
                              waits=[("v", last_lo)] if last_lo else ())
                thi = s_op(lambda h, sl=sl: h.copy(qh[:, :cols],
                                                   wt[sl][:, :cols]),
                           [wld, last_hi_cons])
                tlo = v_op(lambda h, sl=sl: h.tensor_sub(
                    ql[:, :cols], wt[sl][:, :cols], qh[:, :cols]),
                    [("s", thi), last_lo_cons])
                last_lo = tlo
                nb = cols // 128
                if transpose_to:
                    t1 = sch.dmat(blocks(qT[:, :cols], nb),
                                  blocks(qh[:, :cols], nb),
                                  [("s", thi), last_st1])
                    t2 = sch.dmat(blocks(qT2[:, :cols], nb),
                                  blocks(ql[:, :cols], nb),
                                  [("v", tlo), last_st2])
                    last_hi_cons = t1
                    last_st1 = ("st", sch.dma(
                        "gpsimd", kpf(dsth[:, t * 128:(t + 1) * 128]),
                        blocks(qT[:, :cols], nb), waits=[t1],
                        inc="st"))
                    last_st2 = ("st", sch.dma(
                        "gpsimd", kpf(dstl[:, t * 128:(t + 1) * 128]),
                        blocks(qT2[:, :cols], nb), waits=[t2],
                        inc="st"))
                else:
                    t1 = sch.dma("gpsimd", dsth[t * 128:(t + 1) * 128, :],
                                 qh[:, :cols], waits=[("s", thi)],
                                 sem="ph%d" % (t % 2))
                    last_lo_cons = sch.dma(
                        "gpsimd", dstl[t * 128:(t + 1) * 128, :],
                        ql[:, :cols], waits=[("v", tlo)],
                        sem="pl%d" % (t % 2))
                    last_hi_cons = t1

        prep_ternary(inp["L0_ffn_W_fc"], 32, D, s_wfcqT, 1.0 / (HF * D),
                     False, 1e-6, 0)
        sch.barrier()
        prep_ternary(inp["L0_ffn_W_proj"], 8, HF, s_wprojqT, 1.0 / (HF * D),
                     False, 1e-6, 1)
        sch.barrier()
        prep_split(inp["L0_attn_W_uv"], 8, FUV, s_wuv_h, s_wuv_l)
        sch.barrier()
        prep_split(inp["L0_attn_W_out"], 16, D, s_wout_h, s_wout_l)
        sch.barrier()
        for w, src in (("q", "L1_attn_Wq"), ("k", "L1_attn_Wk"),
                       ("v", "L1_attn_Wv"), ("o", "L1_attn_Wo")):
            prep_split(inp[src], 8, D, s_wT[w][0], s_wT[w][1],
                       transpose_to=True)
            sch.barrier()
        prep_ternary(inp["L1_ffn_weight"], 8, D, s_wl1qT, 1.0 / (D * D),
                     True, 1e-5, 2)
        A.reset()
        sch.barrier()

        def ffn0_a(xsrc):
            A.mark()
            wfcq = A.alloc(8 * HF, BF16)
            lnw_rep = A.alloc(HF, F32)
            lnb = [alloc_ln_slot() for _ in range(2)]
            hbuf = A.alloc(HF, F32)
            hnc = A.alloc(HF, F32)
            hnhi = A.alloc(HF, BF16)
            hnlo = A.alloc(HF, BF16)
            hnThi = A.alloc(HF, BF16)
            hnTlo = A.alloc(HF, BF16)
            hsq = A.alloc(512, F32)
            hst = A.alloc(32, F32)
            twl = sch.dma("sync", blocks(wfcq, 8), kpf(s_wfcqT))
            tlnw = sch.dma("sync", lnw_rep,
                           row(inp["L0_ffn_ln_w"]).broadcast_to([128, HF]))
            prev = {}
            ptok = [None, None]
            for t in range(NT):
                sl = t % 2
                tok = emit_ln_tile(xsrc, t, lnb[sl], prevtok=ptok[sl],
                                   yT_waits=prev.get(("ytfree", sl), ()),
                                   tag="a%d" % sl)
                ptok[sl] = tok
                lastmm = None
                tsqr = None
                for c in range(8):
                    b = c % 2
                    bk = bank(b)
                    w0 = [tok["yThi"], tok["yTlo"],
                          twl, bank_free[b]]
                    for pi, yT in ((0, lnb[sl]["yThi"]), (1, lnb[sl]["yTlo"])):
                        for kd in range(8):
                            st_ = (pi == 0 and kd == 0)
                            sp_ = (pi == 1 and kd == 7)
                            lastmm = t_mm(
                                bk, yT[:, kd * 128:(kd + 1) * 128],
                                wfcq[:, kd * HF + c * 512:
                                     kd * HF + (c + 1) * 512],
                                st_, sp_, waits=w0 if st_ else (),
                                inc="pe" if sp_ else None)
                    gw = [("pe", lastmm)]
                    if c == 0 and ("hfree" in prev):
                        gw.append(prev["hfree"])
                    tg = s_op(lambda h, c=c, bk=bk: h.activation(
                        hbuf[:, c * 512:(c + 1) * 512], bk, AF.Gelu,
                        accum_out=hst[:, c:c + 1]), gw)
                    bank_free[b] = ("s", tg)
                    tsqr = s_op(lambda h, c=c: h.activation(
                        hsq, hbuf[:, c * 512:(c + 1) * 512], AF.Square,
                        accum_out=hst[:, 8 + c:9 + c]))
                sA, sB, mh, mq, vga, vv2, sdh, rsh, nmh = [
                    hst[:, 16 + i:17 + i] for i in range(9)]
                v_op(lambda h: h.reduce_sum(sA, hst[:, 0:8], axis=AX.X,
                                            negate=True), [("s", tsqr)])
                v_op(lambda h: h.reduce_sum(sB, hst[:, 8:16], axis=AX.X))
                v_op(lambda h: h.tensor_scalar(mh, sA, 1.0 / HF, None,
                                               ALU.mult))
                v_op(lambda h: h.tensor_mul(mq, mh, mh))
                v_op(lambda h: h.tensor_scalar(vga, sB, 1.0 / HF, 1e-5,
                                               ALU.mult, ALU.add))
                t2 = v_op(lambda h: h.tensor_sub(vv2, vga, mq))
                t3 = s_op(lambda h: h.activation(sdh, vv2, AF.Sqrt),
                          [("v", t2)])
                v_op(lambda h: h.reciprocal(rsh, sdh), [("s", t3)])
                t4 = v_op(lambda h: h.tensor_mul(nmh, mh, rsh))
                thnc = v_op(lambda h: h.tensor_scalar(hnc, hbuf, rsh, nmh,
                                                      ALU.mult, ALU.add),
                            [("v", t4), ("s", tsqr)])
                tp32 = v_op(lambda h: h.tensor_mul(hbuf, hnc, lnw_rep),
                            [tlnw])
                thh = s_op(lambda h: h.copy(hnhi, hbuf),
                           [("v", tp32), prev.get("tthi")])
                thl = v_op(lambda h: h.tensor_sub(hnlo, hbuf, hnhi),
                           [("s", thh), prev.get("ttlo")])
                prev["hfree"] = ("v", thl)
                tthi = sch.dmat(blocks(hnThi, 32), blocks(hnhi, 32),
                                [("s", thh), prev.get("s1")], q="scalar")
                ttlo = sch.dmat(blocks(hnTlo, 32), blocks(hnlo, 32),
                                [("v", thl), prev.get("s2")], q="scalar")
                prev["tthi"] = tthi
                prev["ttlo"] = ttlo
                s1 = sch.dma("gpsimd", kpf(s_hnT_h[:, t * 128:(t + 1) * 128]),
                             blocks(hnThi, 32), waits=[tthi],
                             inc="st")
                s2_ = sch.dma("gpsimd",
                              kpf(s_hnT_l[:, t * 128:(t + 1) * 128]),
                              blocks(hnTlo, 32), waits=[ttlo],
                              inc="st")
                prev["s1"] = s1
                prev["s2"] = s2_
                prev[("ytfree", sl)] = [("pe", lastmm)]
                if SAFE_MODE:
                    sch.barrier()
            A.reset()

        def ffn0_b(xsrc, xdst):
            A.mark()
            wpq = A.alloc(32 * D, BF16)
            lnbcol = A.alloc(32, F32)
            lnbrh = A.alloc(32 * 128, BF16)
            lnbrl = A.alloc(32 * 128, BF16)
            lnbr32 = A.alloc(32 * 128, F32)
            ones128 = A.alloc(128, F32)
            bc = A.alloc(D, F32, parts=1)
            bcrep = A.alloc(D, F32)
            t1buf = A.alloc(D, F32)
            slots = [{
                "hTh": A.alloc(HF, BF16),
                "hTl": A.alloc(HF, BF16),
                "xt": A.alloc(D, F32),
                "o": A.alloc(D, F32),
            } for _ in range(2)]
            twl = sch.dma("sync", blocks(wpq, 32), kpf(s_wprojqT))
            tlb = sch.dma("sync", lnbcol,
                          inp["L0_ffn_ln_b"].rearrange("(c p) -> p c", p=128),
                          slow=True)
            v_op(lambda h: h.memset(ones128, 1.0))
            t2 = None
            for kc in range(32):
                t2 = v_op(lambda h, kc=kc: h.tensor_scalar(
                    lnbr32[:, kc * 128:(kc + 1) * 128], ones128,
                    lnbcol[:, kc:kc + 1], None, ALU.mult), [tlb])
            th1 = s_op(lambda h: h.copy(lnbrh, lnbr32), [("v", t2)])
            th2 = v_op(lambda h: h.tensor_sub(lnbrl, lnbr32, lnbrh),
                       [("s", th1)])
            mmt = [None, None]
            for n in range(2):
                for pi, lb in ((0, lnbrh), (1, lnbrl)):
                    for kc in range(32):
                        st_ = (pi == 0 and kc == 0)
                        sp_ = (pi == 1 and kc == 31)
                        mmt[n] = t_mm(
                            bank(6 + n), lb[:, kc * 128:(kc + 1) * 128],
                            wpq[:, kc * D + n * 512: kc * D + (n + 1) * 512],
                            st_, sp_,
                            waits=([twl, ("v", th2),
                                    bank_free[6 + n]] if st_ else ()),
                            inc="pe" if sp_ else None)
            tb1 = v_op(lambda h: h.tensor_copy(bc[0:1, 0:512],
                                               bank(6)[0:1, :]),
                       [("pe", mmt[0])])
            tb2 = v_op(lambda h: h.tensor_copy(bc[0:1, 512:1024],
                                               bank(7)[0:1, :]),
                       [("pe", mmt[1])])
            bank_free[6] = ("v", tb1)
            bank_free[7] = ("v", tb2)
            tb3 = sch.dma("gpsimd",
                          row(s_bconst), bc,
                          waits=[("v", tb1), ("v", tb2)], inc="st")
            thb = sch.dma("gpsimd", bcrep,
                          row(s_bconst).broadcast_to([128, D]),
                          waits=[tb3])
            prev = {}
            for t in range(NT):
                sl = t % 2
                S = slots[sl]
                lw = prev.get(("free", sl), ())
                th = sch.dma("gpsimd", blocks(S["hTh"], 32),
                             kpf(s_hnT_h[:, t * 128:(t + 1) * 128]), waits=lw,
                             sem="bh%d" % sl)
                tl = sch.dma("gpsimd", blocks(S["hTl"], 32),
                             kpf(s_hnT_l[:, t * 128:(t + 1) * 128]),
                             sem="bl%d" % sl)
                tx = sch.dma("gpsimd", S["xt"],
                             xsrc[t * 128:(t + 1) * 128, :],
                             sem="bx%d" % sl)
                pb = 0 if sl == 0 else 2
                opsum = bankpair(pb)
                mmtok = None
                for pi in range(2):
                    hT = (S["hTh"], S["hTl"])[pi]
                    for kc in range(32):
                        for n in range(2):
                            st_ = (pi == 0 and kc == 0)
                            sp_ = (pi == 1 and kc == 31)
                            w = ()
                            if st_:
                                w = [th, tl, twl,
                                     bank_free[pb + n]]
                            mmtok = t_mm(
                                opsum[:, n * 512:(n + 1) * 512],
                                hT[:, kc * 128:(kc + 1) * 128],
                                wpq[:, kc * D + n * 512:
                                    kc * D + (n + 1) * 512],
                                st_, sp_, waits=w,
                                inc="pe" if (sp_ and n == 1) else None)
                te1 = v_op(lambda h, S=S, opsum=opsum: h.scalar_tensor_tensor(
                    t1buf, opsum, 0.5, S["xt"], ALU.mult, ALU.add),
                    [("pe", mmtok), tx, thb,
                     prev.get("t1free"), prev.get(("sfree", sl))])
                bank_free[pb] = ("v", te1)
                bank_free[pb + 1] = ("v", te1)
                te1 = v_op(lambda h, S=S: h.scalar_tensor_tensor(
                    S["o"], bcrep, 0.5, t1buf, ALU.mult, ALU.add))
                prev["t1free"] = ("v", te1)
                tst = sch.dma("gpsimd", xdst[t * 128:(t + 1) * 128, :],
                              S["o"], waits=[("v", te1)], sem="bs%d" % sl)
                prev[("free", sl)] = [("pe", mmtok), ("v", te1)]
                prev[("sfree", sl)] = tst
                if SAFE_MODE:
                    sch.barrier()
            A.reset()

        def l0a_a(xsrc):
            A.mark()
            wuvh = A.alloc(8 * FUV, BF16)
            wuvl = A.alloc(8 * FUV, BF16)
            buvcol = A.alloc(33, F32)
            gam = A.alloc(2, F32)
            bet = A.alloc(2, F32)
            lnb = [alloc_ln_slot(with_yT=False)]
            yTh_m = A.alloc(8 * 512, BF16)
            yTl_m = A.alloc(8 * 512, BF16)
            s32 = A.alloc(512, F32)
            shi = A.alloc(512, BF16)
            slo = A.alloc(512, BF16)
            sThi = A.alloc(512, BF16)
            sTlo = A.alloc(512, BF16)
            q32 = A.alloc(512, F32)
            k32 = A.alloc(512, F32)
            qhi = A.alloc(512, BF16)
            qlo = A.alloc(512, BF16)
            khi = A.alloc(512, BF16)
            klo = A.alloc(512, BF16)
            twh = sch.dma("sync", blocks(wuvh, 8), kpf(s_wuv_h))
            twl = sch.dma("sync", blocks(wuvl, 8), kpf(s_wuv_l))
            tbc = sch.dma("sync", buvcol,
                          inp["L0_attn_b_uv"].rearrange("(c p) -> p c", p=128),
                          slow=True)
            tgm = sch.dma("sync", gam,
                          inp["L0_attn_gamma"].rearrange("g p -> p g"),
                          slow=True)
            tbt = sch.dma("sync", bet,
                          inp["L0_attn_beta"].rearrange("g p -> p g"),
                          slow=True)
            prev = {}
            ptok = [None]
            for mc in range(4):
                mtoks = []
                for si in range(4):
                    t = mc * 4 + si
                    tok = emit_ln_tile(
                        xsrc, t, lnb[0], prevtok=ptok[0],
                        yT_dsts=(blocks(yTh_m, 8)
                                 [:, :, si * 128:(si + 1) * 128],
                                 blocks(yTl_m, 8)
                                 [:, :, si * 128:(si + 1) * 128]),
                        yT_waits=prev.get("ymfree", ()), tag="c")
                    ptok[0] = tok
                    mtoks.append(tok)
                wmac = [twh, twl, tbc]
                for tok in mtoks:
                    wmac += [tok["yThi"], tok["yTlo"]]
                lastmm = None
                for ft in range(33):
                    b = ft % 8
                    pairs = ((wuvh, yTh_m), (wuvh, yTl_m), (wuvl, yTh_m))
                    for pi, (wv, yv) in enumerate(pairs):
                        for kd in range(8):
                            st_ = (pi == 0 and kd == 0)
                            sp_ = (pi == 2 and kd == 7)
                            lastmm = t_mm(
                                bank(b),
                                wv[:, kd * FUV + ft * 128:
                                   kd * FUV + (ft + 1) * 128],
                                yv[:, kd * 512:(kd + 1) * 512], st_, sp_,
                                waits=(wmac + [bank_free[b]]) if st_ else (),
                                inc="pe" if sp_ else None)
                    tbs = s_op(lambda h, b=b, ft=ft: h.activation(
                        s32, bank(b), AF.Silu, bias=buvcol[:, ft:ft + 1],
                        scale=1.0),
                        [("pe", lastmm), prev.get("s32free")])
                    bank_free[b] = ("s", tbs)
                    if ft == 32:
                        tq32 = s_op(lambda h: h.activation(
                            q32, s32, AF.Identity, bias=bet[:, 0:1],
                            scale=gam[:, 0:1]),
                            [tgm, tbt,
                             prev.get("q32free")])
                        tk32 = s_op(lambda h: h.activation(
                            k32, s32, AF.Identity, bias=bet[:, 1:2],
                            scale=gam[:, 1:2]), [prev.get("k32free")])
                        prev["s32free"] = ("s", tk32)
                        tqh = s_op(lambda h: h.copy(qhi, q32),
                                   [prev.get("sp_qh")])
                        tql = v_op(lambda h: h.tensor_sub(qlo, q32, qhi),
                                   [("s", tqh), prev.get("sp_ql")])
                        tkh = s_op(lambda h: h.copy(khi, k32),
                                   [prev.get("sp_kh")])
                        tkl = v_op(lambda h: h.tensor_sub(klo, k32, khi),
                                   [("s", tkh), prev.get("sp_kl")])
                        prev["q32free"] = ("v", tql)
                        prev["k32free"] = ("v", tkl)
                        cs = slice(mc * 512, (mc + 1) * 512)
                        for nm_, buf, tk, eng, dst in (
                                ("sp_qh", qhi, tqh, "s", s_qh),
                                ("sp_ql", qlo, tql, "v", s_ql),
                                ("sp_kh", khi, tkh, "s", s_kh),
                                ("sp_kl", klo, tkl, "v", s_kl)):
                            ts_ = sch.dma("gpsimd", dst[:, cs], buf,
                                          waits=[(eng, tk)], sem=nm_)
                            prev[nm_] = ts_
                    else:
                        tsh = s_op(lambda h: h.copy(shi, s32),
                                   [prev.get("shifree")])
                        tsl = v_op(lambda h: h.tensor_sub(slo, s32, shi),
                                   [("s", tsh), prev.get("slofree")])
                        prev["s32free"] = ("v", tsl)
                        tt1 = sch.dmat(blocks(sThi, 4), blocks(shi, 4),
                                       [("s", tsh), prev.get("sthfree")],
                                       q="scalar")
                        tt2 = sch.dmat(blocks(sTlo, 4), blocks(slo, 4),
                                       [("v", tsl), prev.get("stlfree")],
                                       q="scalar")
                        prev["shifree"] = tt1
                        prev["slofree"] = tt2
                        if ft < 16:
                            dh, dl, col = s_u_h, s_u_l, ft * 128
                        else:
                            dh, dl, col = s_v_h, s_v_l, (ft - 16) * 128
                        rows = slice(mc * 512, (mc + 1) * 512)
                        su1 = sch.dma(
                            "gpsimd",
                            dh[rows, col:col + 128]
                            .rearrange("(b p) f -> p b f", p=128),
                            blocks(sThi, 4), waits=[tt1], inc="st")
                        su2 = sch.dma(
                            "gpsimd",
                            dl[rows, col:col + 128]
                            .rearrange("(b p) f -> p b f", p=128),
                            blocks(sTlo, 4), waits=[tt2], inc="st")
                        prev["sthfree"] = su1
                        prev["stlfree"] = su2
                prev["ymfree"] = [("pe", lastmm)]
                if SAFE_MODE:
                    sch.barrier()
            A.reset()

        def l0a_b():
            A.mark()
            vh_sb = A.alloc(16 * E, BF16)
            vl_sb = A.alloc(16 * E, BF16)
            qh_sb = A.alloc(T, BF16)
            ql_sb = A.alloc(T, BF16)
            kh_sb = A.alloc(T, BF16)
            kl_sb = A.alloc(T, BF16)
            r32 = A.alloc(E, F32)
            uav = A.alloc(E, F32)
            sq32 = uav
            sqhi = A.alloc(E, BF16)
            sqlo = A.alloc(E, BF16)
            sqThi = A.alloc(E, BF16)
            sqTlo = A.alloc(E, BF16)
            uth = A.alloc(E, BF16)
            utl = A.alloc(E, BF16)
            ut32 = r32
            uavhi = A.alloc(E, BF16)
            uavlo = A.alloc(E, BF16)
            uavThi = A.alloc(E, BF16)
            uavTlo = A.alloc(E, BF16)
            tv1 = sch.dma("sync", blocks(vh_sb, 16),
                          s_v_h.rearrange("(c p) e -> p c e", p=128))
            tv2 = sch.dma("sync", blocks(vl_sb, 16),
                          s_v_l.rearrange("(c p) e -> p c e", p=128))
            tq1 = sch.dma("sync", qh_sb, s_qh)
            tq2 = sch.dma("sync", ql_sb, s_ql)
            tk1 = sch.dma("sync", kh_sb, s_kh)
            tk2 = sch.dma("sync", kl_sb, s_kl)
            wload = [x for x in (tv1, tv2, tq1, tq2, tk1, tk2)]
            prev = {}
            for t in range(NT):
                tul1 = sch.dma("gpsimd", uth,
                               s_u_h[t * 128:(t + 1) * 128, :],
                               waits=prev.get("ufree", ()))
                tul2 = sch.dma("gpsimd", utl,
                               s_u_l[t * 128:(t + 1) * 128, :])
                lastmm = None
                qsl = slice(t * 128, (t + 1) * 128)
                pairs = ((qh_sb, kh_sb), (qh_sb, kl_sb), (ql_sb, kh_sb))
                for ms in range(4):
                    for pi, (qv, kv) in enumerate(pairs):
                        st_ = (pi == 0)
                        sp_ = (pi == 2)
                        lastmm = t_mm(
                            bank(ms), qv[:, qsl],
                            kv[:, ms * 512:(ms + 1) * 512], st_, sp_,
                            waits=(wload + [bank_free[ms]]) if st_ else (),
                            inc="pe" if (sp_ and ms == 3) else None)
                tr = None
                for bb in range(4):
                    tr = s_op(lambda h, bb=bb: h.activation(
                        r32[:, bb * 512:(bb + 1) * 512], bank(bb),
                        AF.Relu, scale=1.0 / T),
                        [("pe", lastmm), prev.get("r32free"),
                         prev.get("ut32free")] if bb == 0 else ())

                for ms in range(4):
                    bank_free[ms] = ("s", tr)
                tsq = s_op(lambda h: h.activation(sq32, r32, AF.Square),
                           [prev.get("sqfree")])
                prev["r32free"] = ("s", tsq)
                tsh = s_op(lambda h: h.copy(sqhi, sq32), [prev.get("shfree")])
                tsl = v_op(lambda h: h.tensor_sub(sqlo, sq32, sqhi),
                           [("s", tsh), prev.get("slfree")])
                prev["sqfree"] = ("v", tsl)

                t1 = sch.dmat(blocks(sqThi, 16), blocks(sqhi, 16),
                              [("s", tsh), prev.get("avdone")], q="scalar")
                t2 = sch.dmat(blocks(sqTlo, 16), blocks(sqlo, 16),
                              [("v", tsl), prev.get("avdone")], q="scalar")

                prev["shfree"] = t1
                prev["slfree"] = t2
                lastmm = None
                pairs = ((sqThi, vh_sb), (sqThi, vl_sb), (sqTlo, vh_sb))
                for es in range(4):
                    for pi, (sv, vv) in enumerate(pairs):
                        for kc in range(16):
                            st_ = (pi == 0 and kc == 0)
                            sp_ = (pi == 2 and kc == 15)
                            lastmm = t_mm(
                                bank(4 + es), sv[:, kc * 128:(kc + 1) * 128],
                                vv[:, kc * E + es * 512:
                                   kc * E + (es + 1) * 512], st_, sp_,
                                waits=([t1, t2,
                                        bank_free[4 + es]] if st_ else ()),
                                inc="pe" if (sp_ and es == 3) else None)
                prev["avdone"] = ("pe", lastmm)
                tadd = v_op(lambda h: h.tensor_add(ut32, uth, utl),
                            [tul1, tul2, ("s", tsq)])
                tua = None
                for bb in range(4):
                    tua = v_op(lambda h, bb=bb: h.tensor_mul(
                        uav[:, bb * 512:(bb + 1) * 512],
                        ut32[:, bb * 512:(bb + 1) * 512], bank(4 + bb)),
                        [("pe", lastmm), prev.get("uavfree"),
                         prev.get("tuhtok"), ("s", tsh), ("v", tadd)]
                        if bb == 0 else ())
                for es in range(4):
                    bank_free[4 + es] = ("v", tua)
                prev["ufree"] = [("v", tadd)]
                prev["ut32free"] = ("v", tua)
                tuh = s_op(lambda h: h.copy(uavhi, uav),
                           [("v", tua), prev.get("uhfree")])
                prev["tuhtok"] = ("s", tuh)
                tul2 = v_op(lambda h: h.tensor_sub(uavlo, uav, uavhi),
                            [("s", tuh), prev.get("ulfree")])
                prev["uavfree"] = ("v", tul2)
                prev["sqfree"] = [("v", tul2), ("s", tuh)]
                t3 = sch.dmat(blocks(uavThi, 16), blocks(uavhi, 16),
                              [("s", tuh), prev.get("uths")], q="scalar")
                t4 = sch.dmat(blocks(uavTlo, 16), blocks(uavlo, 16),
                              [("v", tul2), prev.get("utls")], q="scalar")
                prev["uhfree"] = t3
                prev["ulfree"] = t4
                su1 = sch.dma("gpsimd",
                              kpf(s_uavT_h[:, t * 128:(t + 1) * 128]),
                              blocks(uavThi, 16), waits=[t3],
                              inc="st")
                su2 = sch.dma("gpsimd",
                              kpf(s_uavT_l[:, t * 128:(t + 1) * 128]),
                              blocks(uavTlo, 16), waits=[t4],
                              inc="st")
                prev["uths"] = su1
                prev["utls"] = su2
                if SAFE_MODE:
                    sch.barrier()
            A.reset()

        def l0a_c(xsrc, xdst):
            A.mark()
            wouth = A.alloc(16 * D, BF16)
            woutl = A.alloc(16 * D, BF16)
            borep = A.alloc(D, F32)
            t1b = A.alloc(D, F32)
            slots = [{
                "uTh": A.alloc(E, BF16),
                "uTl": A.alloc(E, BF16),
                "xt": A.alloc(D, F32),
                "o": A.alloc(D, F32),
            } for _ in range(2)]
            tw1 = sch.dma("sync", blocks(wouth, 16), kpf(s_wout_h))
            tw2 = sch.dma("sync", blocks(woutl, 16), kpf(s_wout_l))
            tb0 = sch.dma("sync", borep,
                          row(inp["L0_attn_b_out"]).broadcast_to([128, D]))
            prev = {}
            for t in range(NT):
                sl = t % 2
                S = slots[sl]
                lw = prev.get(("free", sl), ())
                th = sch.dma("gpsimd", blocks(S["uTh"], 16),
                             kpf(s_uavT_h[:, t * 128:(t + 1) * 128]),
                             waits=lw, sem="eh%d" % sl)
                tl = sch.dma("gpsimd", blocks(S["uTl"], 16),
                             kpf(s_uavT_l[:, t * 128:(t + 1) * 128]),
                             sem="el%d" % sl)
                tx = sch.dma("gpsimd", S["xt"],
                             xsrc[t * 128:(t + 1) * 128, :],
                             sem="ex%d" % sl)
                pb = 0 if sl == 0 else 2
                opsum = bankpair(pb)
                mmtok = None
                pairs = ((S["uTh"], wouth), (S["uTh"], woutl),
                         (S["uTl"], wouth))
                for pi, (uv, wv) in enumerate(pairs):
                    for kc in range(16):
                        for n in range(2):
                            st_ = (pi == 0 and kc == 0)
                            sp_ = (pi == 2 and kc == 15)
                            w = ()
                            if st_:
                                w = [th, tl, tw1,
                                     tw2, bank_free[pb + n]]
                            mmtok = t_mm(
                                opsum[:, n * 512:(n + 1) * 512],
                                uv[:, kc * 128:(kc + 1) * 128],
                                wv[:, kc * D + n * 512:
                                   kc * D + (n + 1) * 512],
                                st_, sp_, waits=w,
                                inc="pe" if (sp_ and n == 1) else None)
                te0 = v_op(lambda h, S=S, opsum=opsum: h.scalar_tensor_tensor(
                    t1b, opsum, 1.0, S["xt"], ALU.mult, ALU.add),
                    [("pe", mmtok), tx, tb0,
                     prev.get("t1bfree"), prev.get(("sfree", sl))])
                bank_free[pb] = ("v", te0)
                bank_free[pb + 1] = ("v", te0)
                te1 = v_op(lambda h, S=S: h.tensor_add(S["o"], t1b, borep))
                prev["t1bfree"] = ("v", te1)
                tst = sch.dma("gpsimd", xdst[t * 128:(t + 1) * 128, :],
                              S["o"], waits=[("v", te1)], sem="bs%d" % sl)
                prev[("free", sl)] = [("pe", mmtok), ("v", te1)]
                prev[("sfree", sl)] = tst
                if SAFE_MODE:
                    sch.barrier()
            A.reset()

        def l1_ffn(xsrc, xdst):
            A.mark()
            wq = A.alloc(8 * D, BF16)
            hsc = A.alloc(1, F32)
            lnb = [alloc_ln_slot() for _ in range(2)]
            obuf = [A.alloc(D, F32) for _ in range(2)]
            twl = sch.dma("sync", blocks(wq, 8), kpf(s_wl1qT))
            thsc = sch.dma("sync", hsc,
                           row(s_scal[5:6]).broadcast_to([128, 1]))
            prev = {}
            ptok = [None, None]
            for t in range(NT):
                sl = t % 2
                tok = emit_ln_tile(xsrc, t, lnb[sl], prevtok=ptok[sl],
                                   yT_waits=prev.get(("ytfree", sl), ()),
                                   extra_x_waits=[prev.get(("xlate", sl))],
                                   tag="f%d" % sl)
                ptok[sl] = tok
                pb = 0 if sl == 0 else 2
                opsum = bankpair(pb)
                mmtok = None
                for pi in range(2):
                    yT = (lnb[sl]["yThi"], lnb[sl]["yTlo"])[pi]
                    for kd in range(8):
                        for n in range(2):
                            st_ = (pi == 0 and kd == 0)
                            sp_ = (pi == 1 and kd == 7)
                            w = ()
                            if st_:
                                w = [tok["yThi"], tok["yTlo"],
                                     twl, bank_free[pb + n]]
                            mmtok = t_mm(
                                opsum[:, n * 512:(n + 1) * 512],
                                yT[:, kd * 128:(kd + 1) * 128],
                                wq[:, kd * D + n * 512:
                                   kd * D + (n + 1) * 512],
                                st_, sp_, waits=w,
                                inc="pe" if (sp_ and n == 1) else None)
                te1 = v_op(lambda h, sl=sl, opsum=opsum:
                           h.scalar_tensor_tensor(obuf[sl], opsum, hsc,
                                                  lnb[sl]["xt"], ALU.mult,
                                                  ALU.add),
                           [("pe", mmtok), thsc,
                            prev.get(("sfree", sl))])
                bank_free[pb] = ("v", te1)
                bank_free[pb + 1] = ("v", te1)
                tst = sch.dma("gpsimd", xdst[t * 128:(t + 1) * 128, :],
                              obuf[sl], waits=[("v", te1)], sem="fs%d" % sl)
                prev[("ytfree", sl)] = [("pe", mmtok)]
                prev[("sfree", sl)] = tst
                prev[("xlate", sl)] = ("v", te1)
                if SAFE_MODE:
                    sch.barrier()
            A.reset()

        def l1a_a(xsrc):
            A.mark()
            wq = {}
            for w in ("q", "k", "v"):
                wq[w] = (A.alloc(8 * D, BF16), A.alloc(8 * D, BF16))
            lnb = [alloc_ln_slot(with_yT=False) for _ in range(2)]
            yTh_m = A.alloc(8 * 512, BF16)
            yTl_m = A.alloc(8 * 512, BF16)
            eh = A.alloc(512, BF16)
            el = A.alloc(512, BF16)
            vh = A.alloc(D, BF16)
            vl = A.alloc(D, BF16)
            tld = {}
            for w in ("q", "k", "v"):
                tld[w] = (sch.dma("sync", blocks(wq[w][0], 8),
                                  kpf(s_wT[w][0]), sem="w%sh" % w),
                          sch.dma("sync", blocks(wq[w][1], 8),
                                  kpf(s_wT[w][1]), sem="w%sl" % w))
            dstmap = {"q": (s_q1T_h, s_q1T_l), "k": (s_k1T_h, s_k1T_l)}
            prev = {}
            ptok = [None, None]
            for mc in range(4):
                mtoks = []
                for si in range(4):
                    t = mc * 4 + si
                    sl = t % 2
                    tok = emit_ln_tile(
                        xsrc, t, lnb[sl], prevtok=ptok[sl],
                        yT_dsts=(blocks(yTh_m, 8)
                                 [:, :, si * 128:(si + 1) * 128],
                                 blocks(yTl_m, 8)
                                 [:, :, si * 128:(si + 1) * 128]),
                        yT_waits=prev.get("ymfree", ()), tag="g%d" % sl)
                    ptok[sl] = tok
                    mtoks.append(tok)
                wmac = []
                for tok in mtoks:
                    wmac += [tok["yThi"], tok["yTlo"]]
                lastmm = None
                for w in ("q", "k"):
                    wh, wl_ = wq[w]
                    dh, dl = dstmap[w]
                    for mt in range(8):
                        b = mt % 4
                        pairs = ((wh, yTh_m), (wh, yTl_m), (wl_, yTh_m))
                        for pi, (wv, yv) in enumerate(pairs):
                            for kd in range(8):
                                st_ = (pi == 0 and kd == 0)
                                sp_ = (pi == 2 and kd == 7)
                                lastmm = t_mm(
                                    bank(b),
                                    wv[:, kd * D + mt * 128:
                                       kd * D + (mt + 1) * 128],
                                    yv[:, kd * 512:(kd + 1) * 512], st_, sp_,
                                    waits=(wmac + [tld[w][0],
                                                   tld[w][1],
                                                   bank_free[b]])
                                    if st_ else (),
                                    inc="pe" if sp_ else None)
                        teh = s_op(lambda h, b=b: h.copy(eh, bank(b)),
                                   [("pe", lastmm), prev.get("ehfree")])
                        tel = v_op(lambda h, b=b: h.tensor_sub(el, bank(b),
                                                               eh),
                                   [("s", teh), prev.get("elfree")])
                        bank_free[b] = ("v", tel)
                        se1 = sch.dma("gpsimd",
                                      dh[mt * 128:(mt + 1) * 128,
                                         mc * 512:(mc + 1) * 512], eh,
                                      waits=[("s", teh)], inc="st")
                        se2 = sch.dma("gpsimd",
                                      dl[mt * 128:(mt + 1) * 128,
                                         mc * 512:(mc + 1) * 512], el,
                                      waits=[("v", tel)], inc="st")
                        prev["ehfree"] = [se1, ("v", tel)]
                        prev["elfree"] = se2
                wh, wl_ = wq["v"]
                for si in range(4):
                    t = mc * 4 + si
                    pb = 4 if si % 2 == 0 else 6
                    opsum = bankpair(pb)
                    lastmm = None
                    pairs = ((yTh_m, wh), (yTh_m, wl_), (yTl_m, wh))
                    for pi, (yv, wv) in enumerate(pairs):
                        for kd in range(8):
                            for n in range(2):
                                st_ = (pi == 0 and kd == 0)
                                sp_ = (pi == 2 and kd == 7)
                                w_ = ()
                                if st_:
                                    w_ = wmac + [tld["v"][0],
                                                 tld["v"][1],
                                                 bank_free[pb + n]]
                                lastmm = t_mm(
                                    opsum[:, n * 512:(n + 1) * 512],
                                    yv[:, kd * 512 + si * 128:
                                       kd * 512 + (si + 1) * 128],
                                    wv[:, kd * D + n * 512:
                                       kd * D + (n + 1) * 512],
                                    st_, sp_, waits=w_,
                                    inc="pe" if (sp_ and n == 1) else None)
                    tvh = s_op(lambda h, opsum=opsum: h.copy(vh, opsum),
                               [("pe", lastmm), prev.get("vhfree")])
                    tvl = v_op(lambda h, opsum=opsum: h.tensor_sub(vl, opsum,
                                                                   vh),
                               [("s", tvh), prev.get("vlfree")])
                    bank_free[pb] = ("v", tvl)
                    bank_free[pb + 1] = ("v", tvl)
                    sv1 = sch.dma("gpsimd", s_v1_h[t * 128:(t + 1) * 128, :],
                                  vh, waits=[("s", tvh)], inc="st")
                    sv2 = sch.dma("gpsimd", s_v1_l[t * 128:(t + 1) * 128, :],
                                  vl, waits=[("v", tvl)], inc="st")
                    prev["vhfree"] = [sv1, ("v", tvl)]
                    prev["vlfree"] = sv2
                prev["ymfree"] = [("pe", lastmm)]
                if SAFE_MODE:
                    sch.barrier()
            A.reset()

        def l1a_b(xsrc, xdst):
            A.mark()
            k1h = A.alloc(8 * T, BF16)
            k1l = A.alloc(8 * T, BF16)
            v1h = A.alloc(16 * D, BF16)
            v1l = A.alloc(16 * D, BF16)
            woh = A.alloc(8 * D, BF16)
            wol = A.alloc(8 * D, BF16)
            m_mid = A.alloc(512, F32)
            m_r1 = A.alloc(384, F32)
            m_r15 = A.alloc(384, F32)
            m_r0a = A.alloc(512, F32)
            m_r0b = A.alloc(512, F32)
            qrh = A.alloc(D, BF16)
            qrl = A.alloc(D, BF16)
            xt = A.alloc(D, F32)
            sm32 = A.alloc(T, F32)
            e32 = A.alloc(512, F32)
            phi = A.alloc(512, BF16)
            plo = A.alloc(512, BF16)
            pThi = A.alloc(512, BF16)
            pTlo = A.alloc(512, BF16)
            ao = A.alloc(D, F32)
            aohi = A.alloc(D, BF16)
            aolo = A.alloc(D, BF16)
            aoThi = A.alloc(D, BF16)
            aoTlo = A.alloc(D, BF16)
            ob = ao
            sst = A.alloc(4, F32)
            tk1 = sch.dma("sync", blocks(k1h, 8), kpf(s_k1T_h))
            tk2 = sch.dma("sync", blocks(k1l, 8), kpf(s_k1T_l))
            tv1 = sch.dma("sync", blocks(v1h, 16),
                          s_v1_h.rearrange("(c p) e -> p c e", p=128))
            tv2 = sch.dma("sync", blocks(v1l, 16),
                          s_v1_l.rearrange("(c p) e -> p c e", p=128))
            tw1 = sch.dma("sync", blocks(woh, 8), kpf(s_wT["o"][0]))
            tw2 = sch.dma("sync", blocks(wol, 8), kpf(s_wT["o"][1]))
            wres = [x for x in (tk1, tk2, tv1, tv2, tw1, tw2)]

            tmask = None

            def mk_mask(ap, sels):
                nonlocal tmask
                tmask = g_op(lambda h, ap=ap: h.memset(ap, 0.0))
                for (cm, coef, base, fill) in sels:
                    tmask = g_op(lambda h, ap=ap, cm=cm, coef=coef, base=base,
                                 fill=fill: h.affine_select(
                                     out=ap, in_=ap, compare_op=ALU.is_ge,
                                     fill=fill, base=base,
                                     pattern=[[coef, ap.shape[-1]]],
                                     channel_multiplier=cm))

            mk_mask(m_mid[:, 0:128], [(0, -1, 0, NEG)])
            mk_mask(m_mid[:, 128:256], [(-1, 1, 0, NEG)])
            mk_mask(m_mid[:, 256:384], [])
            mk_mask(m_mid[:, 384:512], [(1, -1, 0, NEG)])
            mk_mask(m_r1[:, 0:128], [(-1, 1, 0, NEG), (0, 1, -1, 0.0)])
            mk_mask(m_r1[:, 128:256], [])
            mk_mask(m_r1[:, 256:384], [(1, -1, 0, NEG)])
            mk_mask(m_r15[:, 0:128], [(0, -1, 0, NEG)])
            mk_mask(m_r15[:, 128:256], [(-1, 1, 0, NEG)])
            mk_mask(m_r15[:, 256:384], [])
            mk_mask(m_r0a[:, 0:128], [])
            mk_mask(m_r0a[:, 128:256], [(1, -1, 0, NEG), (1, 0, -1, 0.0)])
            mk_mask(m_r0a[:, 256:512], [(-1, 0, 0, NEG)])
            mk_mask(m_r0b, [(-1, 0, 0, NEG)])
            wmask = ("g", tmask)

            nmx = sst[:, 0:1]
            sume = sst[:, 1:2]
            rcp = sst[:, 2:3]
            scols = A.alloc(4, F32)
            prev = {}
            for r in range(NT):
                tq1 = sch.dma("gpsimd", blocks(qrh, 8),
                              kpf(s_q1T_h[:, r * 128:(r + 1) * 128]),
                              waits=prev.get("qfree", ()))
                tq2 = sch.dma("gpsimd", blocks(qrl, 8),
                              kpf(s_q1T_l[:, r * 128:(r + 1) * 128]))
                txl = sch.dma("gpsimd", xt, xsrc[r * 128:(r + 1) * 128, :],
                              waits=prev.get("xfree", ()))
                if r == 0:
                    W = 2048
                    mbs = list(range(16))
                    rhs_slices = [(i * 512, 512, i * 512) for i in range(4)]
                elif r == 1:
                    W = 384
                    mbs = [0, 1, 2]
                    rhs_slices = [(0, 384, 0)]
                elif r == 15:
                    W = 384
                    mbs = [0, 14, 15]
                    rhs_slices = [(0, 128, 0), (14 * 128, 256, 128)]
                else:
                    W = 512
                    mbs = [0, r - 1, r, r + 1]
                    rhs_slices = [(0, 128, 0), ((r - 1) * 128, 384, 128)]
                pieces = []
                for pi in range(3):
                    for (mcol, wdt, pcol) in rhs_slices:
                        pieces.append((pi, mcol, wdt, pcol))
                first_by_bank = {}
                last_by_bank = {}
                for idx, (pi, mc2, wd2, pc2) in enumerate(pieces):
                    b = pc2 // 512
                    if b not in first_by_bank:
                        first_by_bank[b] = idx
                    last_by_bank[b] = idx
                lastao = None
                for hd_ in range(NH):
                    c = hd_ // 2
                    rw = slice((hd_ % 2) * 64, (hd_ % 2) * 64 + 64)
                    if r == 0:
                        ps = pbig1[:, 0:2048]
                        pbanks = [0, 1, 2, 3]
                    else:
                        ps = bank(hd_ % 2)[:, 0:W]
                        pbanks = [hd_ % 2]
                    qpairs = ((qrh, k1h), (qrh, k1l), (qrl, k1h))
                    lastmm = None
                    for idx, (pi, mc2, wd2, pc2) in enumerate(pieces):
                        qv, kv = qpairs[pi]
                        b = pc2 // 512
                        st_ = (idx == first_by_bank[b])
                        sp_ = (idx == last_by_bank[b])
                        w_ = ()
                        if idx == 0:
                            w_ = (wres + [tq1, tq2] +
                                  [bank_free[bb] for bb in pbanks])
                        lastmm = t_mm(
                            ps[:, pc2:pc2 + wd2],
                            qv[rw, c * 128:(c + 1) * 128],
                            kv[rw, c * T + mc2: c * T + mc2 + wd2],
                            st_, sp_, waits=w_,
                            inc="pe" if idx == len(pieces) - 1 else None)
                    tsm = None
                    smw = [("pe", lastmm), wmask, prev.get("smfree")]
                    if r == 0:
                        for ck in range(4):
                            mk = m_r0a if ck == 0 else m_r0b
                            tsm = v_op(lambda h, ck=ck, mk=mk:
                                       h.scalar_tensor_tensor(
                                           sm32[:, ck * 512:(ck + 1) * 512],
                                           pbig1[:, ck * 512:(ck + 1) * 512],
                                           0.125, mk, ALU.mult, ALU.add),
                                       smw if ck == 0 else ())
                    else:
                        mk = m_r1 if r == 1 else (m_r15 if r == 15 else m_mid)
                        tsm = v_op(lambda h, ps=ps, mk=mk, W=W:
                                   h.scalar_tensor_tensor(
                                       sm32[:, 0:W], ps, 0.125, mk[:, 0:W],
                                       ALU.mult, ALU.add), smw)
                    for bb in pbanks:
                        bank_free[bb] = ("v", tsm)
                    tnm = v_op(lambda h, W=W: h.reduce_max(
                        nmx, sm32[:, 0:W], axis=AX.X, negate=True),
                        [prev.get("nmxfree")])
                    nch = max(1, W // 512)
                    avb = 4 + (hd_ % 2)
                    avp = bank(avb)[:, 0:64]
                    ppairs = ((pThi, v1h), (pThi, v1l), (pTlo, v1h))
                    lastav = None
                    texp = None
                    for ck in range(nch):
                        cw = min(512, W)
                        texp = s_op(lambda h, ck=ck, cw=cw: h.activation(
                            e32[:, 0:cw], sm32[:, ck * 512:ck * 512 + cw],
                            AF.Exp, bias=nmx,
                            accum_out=scols[:, ck:ck + 1]),
                            [("v", tnm), prev.get("sumefree"),
                             prev.get("e32free")])
                        tph = s_op(lambda h, cw=cw: h.copy(phi[:, 0:cw],
                                                           e32[:, 0:cw]),
                                   [prev.get("phifree")])
                        tpl = v_op(lambda h, cw=cw: h.tensor_sub(
                            plo[:, 0:cw], e32[:, 0:cw], phi[:, 0:cw]),
                            [("s", tph), prev.get("plofree")])
                        prev["e32free"] = ("v", tpl)
                        tt1 = sch.dmat(blocks(pThi[:, 0:cw], cw // 128),
                                       blocks(phi[:, 0:cw], cw // 128),
                                       [("s", tph), prev.get("avdone")],
                                       q="scalar")
                        tt2 = sch.dmat(blocks(pTlo[:, 0:cw], cw // 128),
                                       blocks(plo[:, 0:cw], cw // 128),
                                       [("v", tpl), prev.get("avdone")],
                                       q="scalar")
                        prev["phifree"] = tt1
                        prev["plofree"] = tt2
                        mbs_ck = mbs[ck * 4:ck * 4 + 4] if r == 0 else mbs
                        for bi, mb in enumerate(mbs_ck):
                            for pi2, (pv, vv) in enumerate(ppairs):
                                st_ = (ck == 0 and bi == 0 and pi2 == 0)
                                sp_ = (ck == nch - 1 and
                                       bi == len(mbs_ck) - 1 and pi2 == 2)
                                cl_ = (bi == len(mbs_ck) - 1 and pi2 == 2)
                                lastav = t_mm(
                                    avp, pv[:, bi * 128:(bi + 1) * 128],
                                    vv[:, mb * D + hd_ * 64:
                                       mb * D + hd_ * 64 + 64],
                                    st_, sp_,
                                    waits=[tt1, tt2] +
                                    ([bank_free[avb]] if st_ else []),
                                    inc="pe" if cl_ else None)
                        prev["avdone"] = ("pe", lastav)
                    prev["smfree"] = ("s", texp)
                    prev["nmxfree"] = ("s", texp)
                    tsum = v_op(lambda h, nch=nch: h.reduce_sum(
                        sume, scols[:, 0:nch], axis=AX.X), [("s", texp)])
                    trcp = v_op(lambda h: h.reciprocal(rcp, sume))
                    prev["sumefree"] = ("v", trcp)
                    aow = [("pe", lastav), ("v", trcp)]
                    if hd_ == 0:
                        aow += [prev.get("aofree1"), prev.get("aofree2"),
                                prev.get("obfree")]
                    tao = v_op(lambda h, hd_=hd_, avp=avp: h.tensor_scalar(
                        ao[:, hd_ * 64:(hd_ + 1) * 64], avp, rcp, None,
                        ALU.mult), aow)
                    bank_free[avb] = ("v", tao)
                    lastao = tao
                prev["qfree"] = [("pe", lastmm)]
                taoh = s_op(lambda h: h.copy(aohi, ao),
                            [("v", lastao), prev.get("t3free")])
                taol = v_op(lambda h: h.tensor_sub(aolo, ao, aohi),
                            [("s", taoh), prev.get("t4free")])
                prev["aofree1"] = ("s", taoh)
                prev["aofree2"] = ("v", taol)
                t3 = sch.dmat(blocks(aoThi, 8), blocks(aohi, 8),
                              [("s", taoh), prev.get("odone")])
                t4 = sch.dmat(blocks(aoTlo, 8), blocks(aolo, 8),
                              [("v", taol), prev.get("odone")])
                prev["t3free"] = t3
                prev["t4free"] = t4
                opsum = bankpair(6)
                opairs = ((aoThi, woh), (aoThi, wol), (aoTlo, woh))
                mmo = None
                for pi, (av, wv) in enumerate(opairs):
                    for kd in range(8):
                        for n in range(2):
                            st_ = (pi == 0 and kd == 0)
                            sp_ = (pi == 2 and kd == 7)
                            w_ = ()
                            if st_:
                                w_ = [t3, t4,
                                      bank_free[6 + n]]
                            mmo = t_mm(
                                opsum[:, n * 512:(n + 1) * 512],
                                av[:, kd * 128:(kd + 1) * 128],
                                wv[:, kd * D + n * 512:
                                   kd * D + (n + 1) * 512],
                                st_, sp_, waits=w_,
                                inc="pe" if (sp_ and n == 1) else None)
                prev["odone"] = ("pe", mmo)
                te = v_op(lambda h: h.scalar_tensor_tensor(
                    ob, opsum, 1.0, xt, ALU.mult, ALU.add),
                    [("pe", mmo), txl, prev.get("obfree")])
                bank_free[6] = ("v", te)
                bank_free[7] = ("v", te)
                tst = sch.dma("gpsimd", xdst[r * 128:(r + 1) * 128, :], ob,
                              waits=[("v", te)], inc="st")
                prev["obfree"] = tst
                prev["xfree"] = [("v", te)]
                if SAFE_MODE:
                    sch.barrier()
            A.reset()

        cur = inp["L0_attn_inputs"]
        np_ = max(0, min(n_phases, N_PHASES))
        done = 0

        def take():
            nonlocal done
            if done < np_:
                done += 1
                return True
            return False

        if take():
            ffn0_a(cur)
            sch.barrier()
        if take():
            ffn0_b(cur, xB)
            cur = xB
            sch.barrier()
        if take():
            l0a_a(cur)
            sch.barrier()
        if take():
            l0a_b()
            sch.barrier()
        if take():
            l0a_c(cur, xA)
            cur = xA
            sch.barrier()
        if take():
            ffn0_a(cur)
            sch.barrier()
        if take():
            ffn0_b(cur, xB)
            cur = xB
            sch.barrier()
        if take():
            l1_ffn(cur, xA)
            cur = xA
            sch.barrier()
        if take():
            l1a_a(cur)
            sch.barrier()
        if take():
            l1a_b(cur, xB)
            cur = xB
            sch.barrier()
        if take():
            l1_ffn(cur, out)
            cur = out
            sch.barrier()
        if cur is not out:
            A.mark()
            tmp = [A.alloc(D, F32) for _ in range(2)]
            ptoks = [None, None]
            for t in range(NT):
                sl = t % 2
                tl = sch.dma("gpsimd", tmp[sl], cur[t * 128:(t + 1) * 128, :],
                             waits=[ptoks[sl]] if ptoks[sl] else (),
                             sem="oc%d" % sl)
                ts_ = sch.dma("gpsimd", out[t * 128:(t + 1) * 128, :],
                              tmp[sl], waits=[tl], sem="os%d" % sl)
                ptoks[sl] = ts_
            A.reset()
        if DEBUG_DUMP:
            scrmap = {k: v for k, v in locals().items()
                      if k.startswith("s_") or k in ("xA", "xB")}
            for nm in DEBUG_DUMP:
                ap = scrmap[nm]
                dbg = nc.dram_tensor("dbg_" + nm, list(ap.shape), ap.dtype,
                                     kind="ExternalOutput").ap()
                sch.barrier()
                sch.dma("gpsimd", dbg, ap, inc="st")
        sch.barrier()
        sch.emit("sync", lambda h: h.nop())

        blk = stk.enter_context(nc.Block())
        sch.run(blk)
    return nc


def kernel(**inputs):
    x = np.ascontiguousarray(np.asarray(inputs["L0_attn_inputs"],
                                        dtype=np.float32))
    base = {k: np.ascontiguousarray(np.asarray(v, dtype=np.float32))
            for k, v in inputs.items() if k != "L0_attn_inputs"}
    nc = build_nc(N_PHASES_RUN)
    in_maps = []
    for i in range(8):
        m = dict(base)
        m["L0_attn_inputs"] = np.ascontiguousarray(x[i])
        in_maps.append(m)
    res = run_bass_kernel_spmd(nc, in_maps, core_ids=list(range(8)))
    return np.stack([np.asarray(r["out"]) for r in res.results], axis=0)


# version 08
# speedup vs baseline: 1.0917x; 1.0049x over previous
import contextlib
import numpy as np
import concourse.bass as bass
import concourse.mybir as mybir
from concourse.bass_utils import run_bass_kernel_spmd

F32 = mybir.dt.float32
BF16 = mybir.dt.bfloat16
I8 = mybir.dt.int8
AF = mybir.ActivationFunctionType
ALU = mybir.AluOpType
AX = mybir.AxisListType

T = 2048
D = 1024
NT = T // 128
HF = 4096
FUV = 4224
E = 2048
NH = 16
NEG = -1.0e30

PHASES = ["p1a", "p1b", "p2a", "p2b", "p2c", "p3a", "p3b", "p4", "p5a", "p5b",
          "p6"]
N_PHASES = len(PHASES)
N_PHASES_RUN = N_PHASES
SAFE_MODE = False
DEBUG_DUMP = []

ENGS = ("sync", "gpsimd", "vector", "scalar", "tensor")


class Sched:
    def __init__(self, nc, stk):
        self.nc = nc
        self.stk = stk
        self.q = {e: [] for e in ENGS}
        self.sems = {}
        self.cnt = {}
        self.barrier_pending = {e: None for e in ENGS}
        self.site_map = {}
        self.pool_base = 0

    def sem(self, name):
        if name not in self.sems:
            self.sems[name] = self.stk.enter_context(self.nc.semaphore(name))
            self.cnt[name] = 0
        return self.sems[name]

    def site_sem(self, key, kind):
        if key not in self.site_map:
            n = sum(1 for k2 in self.site_map if self.site_map[k2][1] == kind)
            name = "q%s%d" % (kind, n)
            assert n < 34, "sem pool exhausted"
            self.site_map[key] = (name, kind)
        return self.site_map[key][0]

    def emit(self, eng, builder, waits=(), inc=None, inc_n=1):
        wmax = {}
        bp = self.barrier_pending[eng]
        if bp is not None:
            for s, v in bp.items():
                if v > 0:
                    wmax[s] = v
            self.barrier_pending[eng] = None
        def add(wv):
            if wv is None:
                return
            if isinstance(wv, list):
                for w2 in wv:
                    add(w2)
                return
            s, v = wv
            if isinstance(v, tuple):
                add(v)
                return
            if v is not None and v > 0:
                wmax[s] = max(wmax.get(s, 0), v)
        for wv in waits:
            add(wv)
        if inc in ("v", "s", "g"):
            self.sem(inc)
            prevcnt = self.cnt[inc]
            if prevcnt > 0:
                wmax[inc] = max(wmax.get(inc, 0), prevcnt)
        for s in wmax:
            self.sem(s)
        tok = None
        if inc is not None:
            self.sem(inc)
            self.cnt[inc] += inc_n
            tok = self.cnt[inc]
        self.q[eng].append((wmax, builder, inc, inc_n))
        return tok

    def dma(self, eng, out, in_, waits=(), inc="ld", slow=False, sem=None):
        def b(h, out=out, in_=in_, slow=slow):
            return h.dma_start(out=out, in_=in_,
                               allow_slow_non_contiguous=slow)
        kind = "g" if eng == "gpsimd" else "s"
        if sem is None:
            import inspect
            sem = self.site_sem(
                "d%d" % inspect.currentframe().f_back.f_lineno, kind)
        else:
            sem = self.site_sem("d_" + sem, kind)
        tok = self.emit(eng, b, waits, sem, 16)
        return (sem, tok)

    def dmat(self, out, in_, waits=(), sem=None, q="sync"):
        def b(h, out=out, in_=in_):
            return h.dma_start_transpose(out=out, in_=in_)
        if sem is None:
            import inspect
            sem = self.site_sem(
                "t%d" % inspect.currentframe().f_back.f_lineno, "t")
        else:
            sem = self.site_sem("t_" + sem, "t")
        return (sem, self.emit(q, b, waits, sem, 16))

    def barrier(self):
        snap = dict(self.cnt)
        for e in ENGS:
            self.barrier_pending[e] = snap
        self.site_map = {}

    def run(self, blk):
        sch = self

        def mk(eng):
            def body(h):
                hwm = {}
                for wmax, builder, inc, inc_n in sch.q[eng]:
                    for s, v in wmax.items():
                        if hwm.get(s, -1) >= v:
                            continue
                        h.wait_ge(sch.sems[s], v)
                        hwm[s] = v
                    inst = builder(h)
                    if inc is not None:
                        inst.then_inc(sch.sems[inc], inc_n)
            return body

        blk.sync(mk("sync"))
        blk.gpsimd(mk("gpsimd"))
        blk.vector(mk("vector"))
        blk.scalar(mk("scalar"))
        blk.tensor(mk("tensor"))


class Arena:
    def __init__(self, ap, nbytes):
        self.ap = ap
        self.nbytes = nbytes
        self.off = 0
        self.marks = []

    def alloc(self, n_elem, dt, parts=128):
        sz = n_elem * mybir.dt.size(dt)
        off = (self.off + 63) & ~63
        assert off + sz <= self.nbytes, f"arena overflow {off}+{sz}"
        self.off = off + sz
        return self.ap[:parts, off:off + sz].bitcast(dt)

    def mark(self):
        self.marks.append(self.off)

    def reset(self):
        self.off = self.marks.pop()


def blocks(ap, nblk):
    return ap.rearrange("p (k f) -> p k f", k=nblk)


def kpf(dram_ap):
    return dram_ap.rearrange("(k p) f -> p k f", p=128)


def row(dram_ap):
    return dram_ap.rearrange("(one f) -> one f", one=1)


def build_nc(n_phases=N_PHASES):
    nc = bass.Bass("TRN2", target_bir_lowering=False)

    inp = {}
    for name, shp in [
        ("L0_attn_inputs", [T, D]), ("L0_attn_W_uv", [D, FUV]),
        ("L0_attn_b_uv", [FUV]), ("L0_attn_gamma", [2, 128]),
        ("L0_attn_beta", [2, 128]), ("L0_attn_W_out", [E, D]),
        ("L0_attn_b_out", [D]), ("L0_ffn_W_fc", [HF, D]),
        ("L0_ffn_ln_w", [HF]), ("L0_ffn_ln_b", [HF]),
        ("L0_ffn_W_proj", [D, HF]), ("L1_attn_Wq", [D, D]),
        ("L1_attn_Wk", [D, D]), ("L1_attn_Wv", [D, D]),
        ("L1_attn_Wo", [D, D]), ("L1_ffn_weight", [D, D]),
    ]:
        inp[name] = nc.dram_tensor(name, shp, F32, kind="ExternalInput").ap()
    out = nc.dram_tensor("out", [T, D], F32, kind="ExternalOutput").ap()

    def scr(name, shp, dt):
        return nc.dram_tensor(name, shp, dt, kind="Internal").ap()

    xA = scr("xA", [T, D], F32)
    xB = scr("xB", [T, D], F32)
    s_wfcqT = scr("s_wfcqT", [D, HF], BF16)
    s_wprojqT = scr("s_wprojqT", [HF, D], BF16)
    s_wuv_h = scr("s_wuv_h", [D, FUV], BF16)
    s_wuv_l = scr("s_wuv_l", [D, FUV], BF16)
    s_wout_h = scr("s_wout_h", [E, D], BF16)
    s_wout_l = scr("s_wout_l", [E, D], BF16)
    s_wT = {}
    for w in ("q", "k", "v", "o"):
        s_wT[w] = (scr(f"s_w{w}T_h", [D, D], BF16),
                   scr(f"s_w{w}T_l", [D, D], BF16))
    s_wl1qT = scr("s_wl1qT", [D, D], BF16)
    s_hnT_h = scr("s_hnT_h", [HF, T], BF16)
    s_hnT_l = scr("s_hnT_l", [HF, T], BF16)
    s_u_h = scr("s_u_h", [T, E], BF16)
    s_u_l = scr("s_u_l", [T, E], BF16)
    s_v_h = scr("s_v_h", [T, E], BF16)
    s_v_l = scr("s_v_l", [T, E], BF16)
    s_qh = scr("s_qh", [128, T], BF16)
    s_ql = scr("s_ql", [128, T], BF16)
    s_kh = scr("s_kh", [128, T], BF16)
    s_kl = scr("s_kl", [128, T], BF16)
    s_uavT_h = scr("s_uavT_h", [E, T], BF16)
    s_uavT_l = scr("s_uavT_l", [E, T], BF16)
    s_q1T_h = scr("s_q1T_h", [D, T], BF16)
    s_q1T_l = scr("s_q1T_l", [D, T], BF16)
    s_k1T_h = scr("s_k1T_h", [D, T], BF16)
    s_k1T_l = scr("s_k1T_l", [D, T], BF16)
    s_v1_h = scr("s_v1_h", [T, D], BF16)
    s_v1_l = scr("s_v1_l", [T, D], BF16)
    s_dbgA = scr("s_dbgA", [128, 2048], F32)
    s_dbgB = scr("s_dbgB", [128, 2048], F32)
    s_dbgC = scr("s_dbgC", [128, 2048], F32)
    s_pr = scr("s_pr", [128], F32)
    s_scal = scr("s_scal", [8], F32)
    s_bconst = scr("s_bconst", [D], F32)

    stk = contextlib.ExitStack()
    with stk:
        SB_BYTES = 207 * 1024
        arena_t = stk.enter_context(nc.sbuf_tensor("arena", [128, SB_BYTES],
                                                   I8))
        A = Arena(arena_t[:], SB_BYTES)
        pbig1 = stk.enter_context(nc.psum_tensor("pbig1", [128, 2048], F32))
        pbig2 = stk.enter_context(nc.psum_tensor("pbig2", [128, 2048], F32))

        def bank(i):
            pb = pbig1 if i < 4 else pbig2
            return pb[:, (i % 4) * 512:(i % 4 + 1) * 512]

        def bankpair(i):
            pb = pbig1 if i < 4 else pbig2
            return pb[:, (i % 4) * 512:(i % 4 + 2) * 512]

        sch = Sched(nc, stk)
        for s in ("ld", "lg", "tp", "st", "sg", "pe", "v", "s", "g"):
            sch.sem(s)
        bank_free = {i: None for i in range(8)}

        def v_op(builder, waits=(), inc="v"):
            return sch.emit("vector", builder, waits, inc)

        def s_op(builder, waits=(), inc="s"):
            return sch.emit("scalar", builder, waits, inc)

        def g_op(builder, waits=(), inc="g"):
            return sch.emit("gpsimd", builder, waits, inc)

        def t_mm(psum_ap, lhsT, rhs, start, stop, waits=(), inc=None):
            def b(h, o=psum_ap, l=lhsT, r=rhs, st=start, sp=stop):
                return h.matmul(o, l, r, start=st, stop=sp)
            if inc is None:
                return sch.emit("tensor", b, waits)
            sch.emit("tensor", b, waits)
            return sch.emit("tensor", lambda h: h.drain(), (), inc)

        def alloc_ln_slot(with_yT=True):
            d = {
                "xt": A.alloc(D, F32),
                "y32": A.alloc(D, F32),
                "yhi": A.alloc(D, BF16),
                "ylo": A.alloc(D, BF16),
                "st": A.alloc(16, F32),
            }
            if with_yT:
                d["yThi"] = A.alloc(D, BF16)
                d["yTlo"] = A.alloc(D, BF16)
            return d

        def emit_ln_tile(xsrc, t, B, prevtok=None, yT_dsts=None, yT_waits=(),
                         extra_x_waits=(), tag="z"):
            xt, y32, yhi, ylo, stt = (B["xt"], B["y32"], B["yhi"], B["ylo"],
                                      B["st"])
            tok = {}
            slot_free = [w for w in extra_x_waits if w is not None]
            if prevtok is not None:
                slot_free += [("v", prevtok["ylo"]), ("s", prevtok["yhi"])]
            tok["x"] = sch.dma("gpsimd", xt, xsrc[t * 128:(t + 1) * 128, :],
                               waits=slot_free, sem=tag + "x")
            s1n, s2, mn, msq, va, vv, sd, rs, nm = [stt[:, i:i + 1]
                                                    for i in range(9)]
            wl = [tok["x"]] + slot_free
            v_op(lambda h: h.reduce_sum(s1n, xt, axis=AX.X, negate=True), wl)
            tsq = s_op(lambda h: h.activation(y32, xt, AF.Square,
                                              accum_out=s2), wl)
            v_op(lambda h: h.tensor_scalar(mn, s1n, 1.0 / D, None, ALU.mult))
            v_op(lambda h: h.tensor_mul(msq, mn, mn))
            v_op(lambda h: h.tensor_scalar(va, s2, 1.0 / D, 1e-5, ALU.mult,
                                           ALU.add), [("s", tsq)])
            tvv = v_op(lambda h: h.tensor_sub(vv, va, msq))
            tsd = s_op(lambda h: h.activation(sd, vv, AF.Sqrt), [("v", tvv)])
            v_op(lambda h: h.reciprocal(rs, sd), [("s", tsd)])
            tnm = v_op(lambda h: h.tensor_mul(nm, mn, rs))
            whi = [("v", tnm)]
            if prevtok is not None:
                whi.append(prevtok["yThi"])
            tok["yhi"] = s_op(
                lambda h: h.activation(yhi, xt, AF.Identity, bias=nm,
                                       scale=rs), whi)
            tok["y32"] = v_op(
                lambda h: h.tensor_scalar(y32, xt, rs, nm, ALU.mult, ALU.add),
                [("s", tok["yhi"])])
            wlo = []
            if prevtok is not None:
                wlo.append(prevtok["yTlo"])
            tok["ylo"] = v_op(lambda h: h.tensor_sub(ylo, y32, yhi), wlo)
            if yT_dsts is None:
                yT_dsts = (blocks(B["yThi"], 8), blocks(B["yTlo"], 8))
            tok["yThi"] = sch.dmat(yT_dsts[0], blocks(yhi, 8),
                                   [("s", tok["yhi"])] + list(yT_waits),
                                   sem=tag + "h")
            tok["yTlo"] = sch.dmat(yT_dsts[1], blocks(ylo, 8),
                                   [("v", tok["ylo"])] + list(yT_waits),
                                   sem=tag + "l")
            return tok

        A.mark()
        wt = [A.alloc(FUV, F32), A.alloc(FUV, F32)]
        qh = A.alloc(FUV, BF16)
        ql = A.alloc(FUV, BF16)
        qr = A.alloc(FUV, BF16)
        qT = A.alloc(FUV, BF16)
        qT2 = A.alloc(FUV, BF16)
        acc = A.alloc(64, F32)
        prrow = A.alloc(128, F32, parts=1)
        sc1 = A.alloc(4, F32, parts=1)
        thr_rep = A.alloc(1, F32)
        nthr_rep = A.alloc(1, F32)

        def prep_ternary(wsrc, nt, cols, dst, mean_div, use_max, eps,
                         scal_idx):
            last_v = None
            for t in range(nt):
                sl = t % 2
                wld = sch.dma("gpsimd", wt[sl][:, :cols],
                              wsrc[t * 128:(t + 1) * 128, :],
                              waits=[("v", last_v)] if last_v else ())
                last_v = v_op(lambda h, t=t, sl=sl: h.reduce_sum(
                    acc[:, t:t + 1], wt[sl][:, :cols], axis=AX.X,
                    apply_absolute_value=True), [wld])
            tacc = v_op(lambda h: h.reduce_sum(acc[:, 63:64], acc[:, 0:nt],
                                               axis=AX.X), [("v", last_v)])
            tst = sch.dma("gpsimd", s_pr.rearrange("(p one) -> p one", one=1),
                          acc[:, 63:64], waits=[("v", tacc)], inc="st")
            tld = sch.dma("gpsimd", prrow, row(s_pr), waits=[tst])
            v_op(lambda h: h.reduce_sum(sc1[:, 0:1], prrow, axis=AX.X),
                 [tld])
            if use_max:
                v_op(lambda h: h.tensor_scalar(sc1[:, 1:2], sc1[:, 0:1],
                                               mean_div, None, ALU.mult))
                v_op(lambda h: h.tensor_scalar(sc1[:, 1:2], sc1[:, 1:2], eps,
                                               None, ALU.max))
            else:
                v_op(lambda h: h.tensor_scalar(sc1[:, 1:2], sc1[:, 0:1],
                                               mean_div, eps, ALU.mult,
                                               ALU.add))
            tth = v_op(lambda h: h.tensor_scalar(sc1[:, 2:3], sc1[:, 1:2], 0.5,
                                                 None, ALU.mult))
            tsc = sch.dma("gpsimd", row(s_scal[2 * scal_idx:2 * scal_idx + 2]),
                          sc1[:, 1:3], waits=[("v", tth)], inc="st")
            tthr = sch.dma("gpsimd", thr_rep,
                           row(s_scal[2 * scal_idx + 1:2 * scal_idx + 2])
                           .broadcast_to([128, 1]), waits=[tsc])
            tnthr = v_op(lambda h: h.tensor_scalar(nthr_rep, thr_rep, -1.0,
                                                   None, ALU.mult),
                         [tthr])
            nb = cols // 128
            last_q = tnthr
            last_tp = None
            last_st = None
            for t in range(nt):
                sl = t % 2
                wld = sch.dma("gpsimd", wt[sl][:, :cols],
                              wsrc[t * 128:(t + 1) * 128, :],
                              waits=[("v", last_q)])
                v_op(lambda h, sl=sl: h.tensor_scalar(
                    qh[:, :cols], wt[sl][:, :cols], thr_rep, None, ALU.is_gt),
                    [wld])
                v_op(lambda h, sl=sl: h.tensor_scalar(
                    ql[:, :cols], wt[sl][:, :cols], nthr_rep, None,
                    ALU.is_lt))
                tq = v_op(lambda h: h.tensor_sub(qr[:, :cols], qh[:, :cols],
                                                 ql[:, :cols]),
                          [last_tp] if last_tp else ())
                last_q = tq
                ttp = sch.dmat(blocks(qT[:, :cols], nb),
                               blocks(qr[:, :cols], nb),
                               [("v", tq), last_st])
                last_tp = ttp
                last_st = sch.dma("gpsimd", kpf(dst[:, t * 128:(t + 1) * 128]),
                                  blocks(qT[:, :cols], nb),
                                  waits=[ttp], sem="pq%d" % (t % 2))

        def prep_split(wsrc, nt, cols, dsth, dstl, transpose_to=False):
            last_lo = None
            last_hi_cons = None
            last_lo_cons = None
            last_st1 = None
            last_st2 = None
            for t in range(nt):
                sl = t % 2
                wld = sch.dma("gpsimd", wt[sl][:, :cols],
                              wsrc[t * 128:(t + 1) * 128, :],
                              waits=[("v", last_lo)] if last_lo else ())
                thi = s_op(lambda h, sl=sl: h.copy(qh[:, :cols],
                                                   wt[sl][:, :cols]),
                           [wld, last_hi_cons])
                tlo = v_op(lambda h, sl=sl: h.tensor_sub(
                    ql[:, :cols], wt[sl][:, :cols], qh[:, :cols]),
                    [("s", thi), last_lo_cons])
                last_lo = tlo
                nb = cols // 128
                if transpose_to:
                    t1 = sch.dmat(blocks(qT[:, :cols], nb),
                                  blocks(qh[:, :cols], nb),
                                  [("s", thi), last_st1])
                    t2 = sch.dmat(blocks(qT2[:, :cols], nb),
                                  blocks(ql[:, :cols], nb),
                                  [("v", tlo), last_st2])
                    last_hi_cons = t1
                    last_st1 = ("st", sch.dma(
                        "gpsimd", kpf(dsth[:, t * 128:(t + 1) * 128]),
                        blocks(qT[:, :cols], nb), waits=[t1],
                        inc="st"))
                    last_st2 = ("st", sch.dma(
                        "gpsimd", kpf(dstl[:, t * 128:(t + 1) * 128]),
                        blocks(qT2[:, :cols], nb), waits=[t2],
                        inc="st"))
                else:
                    t1 = sch.dma("gpsimd", dsth[t * 128:(t + 1) * 128, :],
                                 qh[:, :cols], waits=[("s", thi)],
                                 sem="ph%d" % (t % 2))
                    last_lo_cons = sch.dma(
                        "gpsimd", dstl[t * 128:(t + 1) * 128, :],
                        ql[:, :cols], waits=[("v", tlo)],
                        sem="pl%d" % (t % 2))
                    last_hi_cons = t1

        prep_ternary(inp["L0_ffn_W_fc"], 32, D, s_wfcqT, 1.0 / (HF * D),
                     False, 1e-6, 0)
        sch.barrier()
        prep_ternary(inp["L0_ffn_W_proj"], 8, HF, s_wprojqT, 1.0 / (HF * D),
                     False, 1e-6, 1)
        sch.barrier()
        prep_split(inp["L0_attn_W_uv"], 8, FUV, s_wuv_h, s_wuv_l)
        sch.barrier()
        prep_split(inp["L0_attn_W_out"], 16, D, s_wout_h, s_wout_l)
        sch.barrier()
        for w, src in (("q", "L1_attn_Wq"), ("k", "L1_attn_Wk"),
                       ("v", "L1_attn_Wv"), ("o", "L1_attn_Wo")):
            prep_split(inp[src], 8, D, s_wT[w][0], s_wT[w][1],
                       transpose_to=True)
            sch.barrier()
        prep_ternary(inp["L1_ffn_weight"], 8, D, s_wl1qT, 1.0 / (D * D),
                     True, 1e-5, 2)
        A.reset()
        sch.barrier()

        def ffn0_a(xsrc):
            A.mark()
            wfcq = A.alloc(8 * HF, BF16)
            lnw_rep = A.alloc(HF, F32)
            lnb = [alloc_ln_slot() for _ in range(2)]
            hbuf = A.alloc(HF, F32)
            hnc = A.alloc(HF, F32)
            hnhi = A.alloc(HF, BF16)
            hnlo = A.alloc(HF, BF16)
            hnThi = A.alloc(HF, BF16)
            hnTlo = A.alloc(HF, BF16)
            hsq = A.alloc(512, F32)
            hst = A.alloc(32, F32)
            twl = sch.dma("sync", blocks(wfcq, 8), kpf(s_wfcqT))
            tlnw = sch.dma("sync", lnw_rep,
                           row(inp["L0_ffn_ln_w"]).broadcast_to([128, HF]))
            prev = {}
            ptok = [None, None]
            for t in range(NT):
                sl = t % 2
                tok = emit_ln_tile(xsrc, t, lnb[sl], prevtok=ptok[sl],
                                   yT_waits=prev.get(("ytfree", sl), ()),
                                   tag="a%d" % sl)
                ptok[sl] = tok
                lastmm = None
                tsqr = None
                for c in range(8):
                    b = c % 2
                    bk = bank(b)
                    w0 = [tok["yThi"], tok["yTlo"],
                          twl, bank_free[b]]
                    for pi, yT in ((0, lnb[sl]["yThi"]), (1, lnb[sl]["yTlo"])):
                        for kd in range(8):
                            st_ = (pi == 0 and kd == 0)
                            sp_ = (pi == 1 and kd == 7)
                            lastmm = t_mm(
                                bk, yT[:, kd * 128:(kd + 1) * 128],
                                wfcq[:, kd * HF + c * 512:
                                     kd * HF + (c + 1) * 512],
                                st_, sp_, waits=w0 if st_ else (),
                                inc="pe" if sp_ else None)
                    gw = [("pe", lastmm)]
                    if c == 0 and ("hfree" in prev):
                        gw.append(prev["hfree"])
                    tg = s_op(lambda h, c=c, bk=bk: h.activation(
                        hbuf[:, c * 512:(c + 1) * 512], bk, AF.Gelu,
                        accum_out=hst[:, c:c + 1]), gw)
                    bank_free[b] = ("s", tg)
                    tsqr = s_op(lambda h, c=c: h.activation(
                        hsq, hbuf[:, c * 512:(c + 1) * 512], AF.Square,
                        accum_out=hst[:, 8 + c:9 + c]))
                sA, sB, mh, mq, vga, vv2, sdh, rsh, nmh = [
                    hst[:, 16 + i:17 + i] for i in range(9)]
                v_op(lambda h: h.reduce_sum(sA, hst[:, 0:8], axis=AX.X,
                                            negate=True), [("s", tsqr)])
                v_op(lambda h: h.reduce_sum(sB, hst[:, 8:16], axis=AX.X))
                v_op(lambda h: h.tensor_scalar(mh, sA, 1.0 / HF, None,
                                               ALU.mult))
                v_op(lambda h: h.tensor_mul(mq, mh, mh))
                v_op(lambda h: h.tensor_scalar(vga, sB, 1.0 / HF, 1e-5,
                                               ALU.mult, ALU.add))
                t2 = v_op(lambda h: h.tensor_sub(vv2, vga, mq))
                t3 = s_op(lambda h: h.activation(sdh, vv2, AF.Sqrt),
                          [("v", t2)])
                v_op(lambda h: h.reciprocal(rsh, sdh), [("s", t3)])
                t4 = v_op(lambda h: h.tensor_mul(nmh, mh, rsh))
                thnc = v_op(lambda h: h.tensor_scalar(hnc, hbuf, rsh, nmh,
                                                      ALU.mult, ALU.add),
                            [("v", t4), ("s", tsqr)])
                tp32 = v_op(lambda h: h.tensor_mul(hbuf, hnc, lnw_rep),
                            [tlnw])
                thh = s_op(lambda h: h.copy(hnhi, hbuf),
                           [("v", tp32), prev.get("tthi")])
                thl = v_op(lambda h: h.tensor_sub(hnlo, hbuf, hnhi),
                           [("s", thh), prev.get("ttlo")])
                prev["hfree"] = ("v", thl)
                tthi = sch.dmat(blocks(hnThi, 32), blocks(hnhi, 32),
                                [("s", thh), prev.get("s1")], q="scalar")
                ttlo = sch.dmat(blocks(hnTlo, 32), blocks(hnlo, 32),
                                [("v", thl), prev.get("s2")], q="scalar")
                prev["tthi"] = tthi
                prev["ttlo"] = ttlo
                s1 = sch.dma("gpsimd", kpf(s_hnT_h[:, t * 128:(t + 1) * 128]),
                             blocks(hnThi, 32), waits=[tthi],
                             inc="st")
                s2_ = sch.dma("gpsimd",
                              kpf(s_hnT_l[:, t * 128:(t + 1) * 128]),
                              blocks(hnTlo, 32), waits=[ttlo],
                              inc="st")
                prev["s1"] = s1
                prev["s2"] = s2_
                prev[("ytfree", sl)] = [("pe", lastmm)]
                if SAFE_MODE:
                    sch.barrier()
            A.reset()

        def ffn0_b(xsrc, xdst):
            A.mark()
            wpq = A.alloc(32 * D, BF16)
            lnbcol = A.alloc(32, F32)
            lnbrh = A.alloc(32 * 128, BF16)
            lnbrl = A.alloc(32 * 128, BF16)
            lnbr32 = A.alloc(32 * 128, F32)
            ones128 = A.alloc(128, F32)
            bc = A.alloc(D, F32, parts=1)
            bcrep = A.alloc(D, F32)
            t1buf = A.alloc(D, F32)
            slots = [{
                "hTh": A.alloc(HF, BF16),
                "hTl": A.alloc(HF, BF16),
                "xt": A.alloc(D, F32),
                "o": A.alloc(D, F32),
            } for _ in range(2)]
            twl = sch.dma("sync", blocks(wpq, 32), kpf(s_wprojqT))
            tlb = sch.dma("sync", lnbcol,
                          inp["L0_ffn_ln_b"].rearrange("(c p) -> p c", p=128),
                          slow=True)
            v_op(lambda h: h.memset(ones128, 1.0))
            t2 = None
            for kc in range(32):
                t2 = v_op(lambda h, kc=kc: h.tensor_scalar(
                    lnbr32[:, kc * 128:(kc + 1) * 128], ones128,
                    lnbcol[:, kc:kc + 1], None, ALU.mult), [tlb])
            th1 = s_op(lambda h: h.copy(lnbrh, lnbr32), [("v", t2)])
            th2 = v_op(lambda h: h.tensor_sub(lnbrl, lnbr32, lnbrh),
                       [("s", th1)])
            mmt = [None, None]
            for n in range(2):
                for pi, lb in ((0, lnbrh), (1, lnbrl)):
                    for kc in range(32):
                        st_ = (pi == 0 and kc == 0)
                        sp_ = (pi == 1 and kc == 31)
                        mmt[n] = t_mm(
                            bank(6 + n), lb[:, kc * 128:(kc + 1) * 128],
                            wpq[:, kc * D + n * 512: kc * D + (n + 1) * 512],
                            st_, sp_,
                            waits=([twl, ("v", th2),
                                    bank_free[6 + n]] if st_ else ()),
                            inc="pe" if sp_ else None)
            tb1 = v_op(lambda h: h.tensor_copy(bc[0:1, 0:512],
                                               bank(6)[0:1, :]),
                       [("pe", mmt[0])])
            tb2 = v_op(lambda h: h.tensor_copy(bc[0:1, 512:1024],
                                               bank(7)[0:1, :]),
                       [("pe", mmt[1])])
            bank_free[6] = ("v", tb1)
            bank_free[7] = ("v", tb2)
            tb3 = sch.dma("gpsimd",
                          row(s_bconst), bc,
                          waits=[("v", tb1), ("v", tb2)], inc="st")
            thb = sch.dma("gpsimd", bcrep,
                          row(s_bconst).broadcast_to([128, D]),
                          waits=[tb3])
            prev = {}
            for t in range(NT):
                sl = t % 2
                S = slots[sl]
                lw = prev.get(("free", sl), ())
                th = sch.dma("gpsimd", blocks(S["hTh"], 32),
                             kpf(s_hnT_h[:, t * 128:(t + 1) * 128]), waits=lw,
                             sem="bh%d" % sl)
                tl = sch.dma("gpsimd", blocks(S["hTl"], 32),
                             kpf(s_hnT_l[:, t * 128:(t + 1) * 128]),
                             sem="bl%d" % sl)
                tx = sch.dma("gpsimd", S["xt"],
                             xsrc[t * 128:(t + 1) * 128, :],
                             sem="bx%d" % sl)
                pb = 0 if sl == 0 else 2
                opsum = bankpair(pb)
                mmtok = None
                for pi in range(2):
                    hT = (S["hTh"], S["hTl"])[pi]
                    for kc in range(32):
                        for n in range(2):
                            st_ = (pi == 0 and kc == 0)
                            sp_ = (pi == 1 and kc == 31)
                            w = ()
                            if st_:
                                w = [th, tl, twl,
                                     bank_free[pb + n]]
                            mmtok = t_mm(
                                opsum[:, n * 512:(n + 1) * 512],
                                hT[:, kc * 128:(kc + 1) * 128],
                                wpq[:, kc * D + n * 512:
                                    kc * D + (n + 1) * 512],
                                st_, sp_, waits=w,
                                inc="pe" if (sp_ and n == 1) else None)
                te1 = v_op(lambda h, S=S, opsum=opsum: h.scalar_tensor_tensor(
                    t1buf, opsum, 0.5, S["xt"], ALU.mult, ALU.add),
                    [("pe", mmtok), tx, thb,
                     prev.get("t1free"), prev.get(("sfree", sl))])
                bank_free[pb] = ("v", te1)
                bank_free[pb + 1] = ("v", te1)
                te1 = v_op(lambda h, S=S: h.scalar_tensor_tensor(
                    S["o"], bcrep, 0.5, t1buf, ALU.mult, ALU.add))
                prev["t1free"] = ("v", te1)
                tst = sch.dma("gpsimd", xdst[t * 128:(t + 1) * 128, :],
                              S["o"], waits=[("v", te1)], sem="bs%d" % sl)
                prev[("free", sl)] = [("pe", mmtok), ("v", te1)]
                prev[("sfree", sl)] = tst
                if SAFE_MODE:
                    sch.barrier()
            A.reset()

        def l0a_a(xsrc):
            A.mark()
            wuvh = A.alloc(8 * FUV, BF16)
            wuvl = A.alloc(8 * FUV, BF16)
            buvcol = A.alloc(33, F32)
            gam = A.alloc(2, F32)
            bet = A.alloc(2, F32)
            lnb = [alloc_ln_slot(with_yT=False)]
            yTh_m = A.alloc(8 * 512, BF16)
            yTl_m = A.alloc(8 * 512, BF16)
            s32 = A.alloc(512, F32)
            shi = A.alloc(512, BF16)
            slo = A.alloc(512, BF16)
            sThi = A.alloc(512, BF16)
            sTlo = A.alloc(512, BF16)
            q32 = A.alloc(512, F32)
            k32 = A.alloc(512, F32)
            qhi = A.alloc(512, BF16)
            qlo = A.alloc(512, BF16)
            khi = A.alloc(512, BF16)
            klo = A.alloc(512, BF16)
            twh = sch.dma("sync", blocks(wuvh, 8), kpf(s_wuv_h))
            twl = sch.dma("sync", blocks(wuvl, 8), kpf(s_wuv_l))
            tbc = sch.dma("sync", buvcol,
                          inp["L0_attn_b_uv"].rearrange("(c p) -> p c", p=128),
                          slow=True)
            tgm = sch.dma("sync", gam,
                          inp["L0_attn_gamma"].rearrange("g p -> p g"),
                          slow=True)
            tbt = sch.dma("sync", bet,
                          inp["L0_attn_beta"].rearrange("g p -> p g"),
                          slow=True)
            prev = {}
            ptok = [None]
            for mc in range(4):
                mtoks = []
                for si in range(4):
                    t = mc * 4 + si
                    tok = emit_ln_tile(
                        xsrc, t, lnb[0], prevtok=ptok[0],
                        yT_dsts=(blocks(yTh_m, 8)
                                 [:, :, si * 128:(si + 1) * 128],
                                 blocks(yTl_m, 8)
                                 [:, :, si * 128:(si + 1) * 128]),
                        yT_waits=prev.get("ymfree", ()), tag="c")
                    ptok[0] = tok
                    mtoks.append(tok)
                wmac = [twh, twl, tbc]
                for tok in mtoks:
                    wmac += [tok["yThi"], tok["yTlo"]]
                lastmm = None
                for ft in range(33):
                    b = ft % 8
                    pairs = ((wuvh, yTh_m), (wuvh, yTl_m), (wuvl, yTh_m))
                    for pi, (wv, yv) in enumerate(pairs):
                        for kd in range(8):
                            st_ = (pi == 0 and kd == 0)
                            sp_ = (pi == 2 and kd == 7)
                            lastmm = t_mm(
                                bank(b),
                                wv[:, kd * FUV + ft * 128:
                                   kd * FUV + (ft + 1) * 128],
                                yv[:, kd * 512:(kd + 1) * 512], st_, sp_,
                                waits=(wmac + [bank_free[b]]) if st_ else (),
                                inc="pe" if sp_ else None)
                    tbs = s_op(lambda h, b=b, ft=ft: h.activation(
                        s32, bank(b), AF.Silu, bias=buvcol[:, ft:ft + 1],
                        scale=1.0),
                        [("pe", lastmm), prev.get("s32free")])
                    bank_free[b] = ("s", tbs)
                    if ft == 32:
                        tq32 = s_op(lambda h: h.activation(
                            q32, s32, AF.Identity, bias=bet[:, 0:1],
                            scale=gam[:, 0:1]),
                            [tgm, tbt,
                             prev.get("q32free")])
                        tk32 = s_op(lambda h: h.activation(
                            k32, s32, AF.Identity, bias=bet[:, 1:2],
                            scale=gam[:, 1:2]), [prev.get("k32free")])
                        prev["s32free"] = ("s", tk32)
                        tqh = s_op(lambda h: h.copy(qhi, q32),
                                   [prev.get("sp_qh")])
                        tql = v_op(lambda h: h.tensor_sub(qlo, q32, qhi),
                                   [("s", tqh), prev.get("sp_ql")])
                        tkh = s_op(lambda h: h.copy(khi, k32),
                                   [prev.get("sp_kh")])
                        tkl = v_op(lambda h: h.tensor_sub(klo, k32, khi),
                                   [("s", tkh), prev.get("sp_kl")])
                        prev["q32free"] = ("v", tql)
                        prev["k32free"] = ("v", tkl)
                        cs = slice(mc * 512, (mc + 1) * 512)
                        for nm_, buf, tk, eng, dst in (
                                ("sp_qh", qhi, tqh, "s", s_qh),
                                ("sp_ql", qlo, tql, "v", s_ql),
                                ("sp_kh", khi, tkh, "s", s_kh),
                                ("sp_kl", klo, tkl, "v", s_kl)):
                            ts_ = sch.dma("gpsimd", dst[:, cs], buf,
                                          waits=[(eng, tk)], sem=nm_)
                            prev[nm_] = ts_
                    else:
                        tsh = s_op(lambda h: h.copy(shi, s32),
                                   [prev.get("shifree")])
                        tsl = v_op(lambda h: h.tensor_sub(slo, s32, shi),
                                   [("s", tsh), prev.get("slofree")])
                        prev["s32free"] = ("v", tsl)
                        tt1 = sch.dmat(blocks(sThi, 4), blocks(shi, 4),
                                       [("s", tsh), prev.get("sthfree")],
                                       q="scalar")
                        tt2 = sch.dmat(blocks(sTlo, 4), blocks(slo, 4),
                                       [("v", tsl), prev.get("stlfree")],
                                       q="scalar")
                        prev["shifree"] = tt1
                        prev["slofree"] = tt2
                        if ft < 16:
                            dh, dl, col = s_u_h, s_u_l, ft * 128
                        else:
                            dh, dl, col = s_v_h, s_v_l, (ft - 16) * 128
                        rows = slice(mc * 512, (mc + 1) * 512)
                        su1 = sch.dma(
                            "gpsimd",
                            dh[rows, col:col + 128]
                            .rearrange("(b p) f -> p b f", p=128),
                            blocks(sThi, 4), waits=[tt1], inc="st")
                        su2 = sch.dma(
                            "gpsimd",
                            dl[rows, col:col + 128]
                            .rearrange("(b p) f -> p b f", p=128),
                            blocks(sTlo, 4), waits=[tt2], inc="st")
                        prev["sthfree"] = su1
                        prev["stlfree"] = su2
                prev["ymfree"] = [("pe", lastmm)]
                if SAFE_MODE:
                    sch.barrier()
            A.reset()

        def l0a_b():
            A.mark()
            vh_sb = A.alloc(16 * E, BF16)
            vl_sb = A.alloc(16 * E, BF16)
            qh_sb = A.alloc(T, BF16)
            ql_sb = A.alloc(T, BF16)
            kh_sb = A.alloc(T, BF16)
            kl_sb = A.alloc(T, BF16)
            r32 = A.alloc(E, F32)
            uav = A.alloc(E, F32)
            sq32 = uav
            sqhi = A.alloc(E, BF16)
            sqlo = A.alloc(E, BF16)
            sqThi = A.alloc(E, BF16)
            sqTlo = A.alloc(E, BF16)
            uth = A.alloc(E, BF16)
            utl = A.alloc(E, BF16)
            ut32 = r32
            uavhi = A.alloc(E, BF16)
            uavlo = A.alloc(E, BF16)
            uavThi = A.alloc(E, BF16)
            uavTlo = A.alloc(E, BF16)
            tv1 = sch.dma("sync", blocks(vh_sb, 16),
                          s_v_h.rearrange("(c p) e -> p c e", p=128))
            tv2 = sch.dma("sync", blocks(vl_sb, 16),
                          s_v_l.rearrange("(c p) e -> p c e", p=128))
            tq1 = sch.dma("sync", qh_sb, s_qh)
            tq2 = sch.dma("sync", ql_sb, s_ql)
            tk1 = sch.dma("sync", kh_sb, s_kh)
            tk2 = sch.dma("sync", kl_sb, s_kl)
            wload = [x for x in (tv1, tv2, tq1, tq2, tk1, tk2)]
            prev = {}
            for t in range(NT):
                tul1 = sch.dma("gpsimd", uth,
                               s_u_h[t * 128:(t + 1) * 128, :],
                               waits=prev.get("ufree", ()))
                tul2 = sch.dma("gpsimd", utl,
                               s_u_l[t * 128:(t + 1) * 128, :])
                lastmm = None
                qsl = slice(t * 128, (t + 1) * 128)
                pairs = ((qh_sb, kh_sb), (qh_sb, kl_sb), (ql_sb, kh_sb))
                for ms in range(4):
                    for pi, (qv, kv) in enumerate(pairs):
                        st_ = (pi == 0)
                        sp_ = (pi == 2)
                        lastmm = t_mm(
                            bank(ms), qv[:, qsl],
                            kv[:, ms * 512:(ms + 1) * 512], st_, sp_,
                            waits=(wload + [bank_free[ms]]) if st_ else (),
                            inc="pe" if (sp_ and ms == 3) else None)
                tr = None
                for bb in range(4):
                    tr = s_op(lambda h, bb=bb: h.activation(
                        r32[:, bb * 512:(bb + 1) * 512], bank(bb),
                        AF.Relu, scale=1.0 / T),
                        [("pe", lastmm), prev.get("r32free"),
                         prev.get("ut32free")] if bb == 0 else ())

                for ms in range(4):
                    bank_free[ms] = ("s", tr)
                tsq = s_op(lambda h: h.activation(sq32, r32, AF.Square),
                           [prev.get("sqfree")])
                prev["r32free"] = ("s", tsq)
                tsh = s_op(lambda h: h.copy(sqhi, sq32), [prev.get("shfree")])
                tsl = v_op(lambda h: h.tensor_sub(sqlo, sq32, sqhi),
                           [("s", tsh), prev.get("slfree")])
                prev["sqfree"] = ("v", tsl)

                t1 = sch.dmat(blocks(sqThi, 16), blocks(sqhi, 16),
                              [("s", tsh), prev.get("avdone")], q="scalar")
                t2 = sch.dmat(blocks(sqTlo, 16), blocks(sqlo, 16),
                              [("v", tsl), prev.get("avdone")], q="scalar")

                prev["shfree"] = t1
                prev["slfree"] = t2
                lastmm = None
                pairs = ((sqThi, vh_sb), (sqThi, vl_sb), (sqTlo, vh_sb))
                for es in range(4):
                    for pi, (sv, vv) in enumerate(pairs):
                        for kc in range(16):
                            st_ = (pi == 0 and kc == 0)
                            sp_ = (pi == 2 and kc == 15)
                            lastmm = t_mm(
                                bank(4 + es), sv[:, kc * 128:(kc + 1) * 128],
                                vv[:, kc * E + es * 512:
                                   kc * E + (es + 1) * 512], st_, sp_,
                                waits=([t1, t2,
                                        bank_free[4 + es]] if st_ else ()),
                                inc="pe" if (sp_ and es == 3) else None)
                prev["avdone"] = ("pe", lastmm)
                tadd = v_op(lambda h: h.tensor_add(ut32, uth, utl),
                            [tul1, tul2, ("s", tsq)])
                tua = None
                for bb in range(4):
                    tua = v_op(lambda h, bb=bb: h.tensor_mul(
                        uav[:, bb * 512:(bb + 1) * 512],
                        ut32[:, bb * 512:(bb + 1) * 512], bank(4 + bb)),
                        [("pe", lastmm), prev.get("uavfree"),
                         prev.get("tuhtok"), ("s", tsh), ("v", tadd)]
                        if bb == 0 else ())
                for es in range(4):
                    bank_free[4 + es] = ("v", tua)
                prev["ufree"] = [("v", tadd)]
                prev["ut32free"] = ("v", tua)
                tuh = s_op(lambda h: h.copy(uavhi, uav),
                           [("v", tua), prev.get("uhfree")])
                prev["tuhtok"] = ("s", tuh)
                tul2 = v_op(lambda h: h.tensor_sub(uavlo, uav, uavhi),
                            [("s", tuh), prev.get("ulfree")])
                prev["uavfree"] = ("v", tul2)
                prev["sqfree"] = [("v", tul2), ("s", tuh)]
                t3 = sch.dmat(blocks(uavThi, 16), blocks(uavhi, 16),
                              [("s", tuh), prev.get("uths")], q="scalar")
                t4 = sch.dmat(blocks(uavTlo, 16), blocks(uavlo, 16),
                              [("v", tul2), prev.get("utls")], q="scalar")
                prev["uhfree"] = t3
                prev["ulfree"] = t4
                su1 = sch.dma("gpsimd",
                              kpf(s_uavT_h[:, t * 128:(t + 1) * 128]),
                              blocks(uavThi, 16), waits=[t3],
                              inc="st")
                su2 = sch.dma("gpsimd",
                              kpf(s_uavT_l[:, t * 128:(t + 1) * 128]),
                              blocks(uavTlo, 16), waits=[t4],
                              inc="st")
                prev["uths"] = su1
                prev["utls"] = su2
                if SAFE_MODE:
                    sch.barrier()
            A.reset()

        def l0a_c(xsrc, xdst):
            A.mark()
            wouth = A.alloc(16 * D, BF16)
            woutl = A.alloc(16 * D, BF16)
            borep = A.alloc(D, F32)
            t1b = A.alloc(D, F32)
            slots = [{
                "uTh": A.alloc(E, BF16),
                "uTl": A.alloc(E, BF16),
                "xt": A.alloc(D, F32),
                "o": A.alloc(D, F32),
            } for _ in range(2)]
            tw1 = sch.dma("sync", blocks(wouth, 16), kpf(s_wout_h))
            tw2 = sch.dma("sync", blocks(woutl, 16), kpf(s_wout_l))
            tb0 = sch.dma("sync", borep,
                          row(inp["L0_attn_b_out"]).broadcast_to([128, D]))
            prev = {}
            for t in range(NT):
                sl = t % 2
                S = slots[sl]
                lw = prev.get(("free", sl), ())
                th = sch.dma("gpsimd", blocks(S["uTh"], 16),
                             kpf(s_uavT_h[:, t * 128:(t + 1) * 128]),
                             waits=lw, sem="eh%d" % sl)
                tl = sch.dma("gpsimd", blocks(S["uTl"], 16),
                             kpf(s_uavT_l[:, t * 128:(t + 1) * 128]),
                             sem="el%d" % sl)
                tx = sch.dma("gpsimd", S["xt"],
                             xsrc[t * 128:(t + 1) * 128, :],
                             sem="ex%d" % sl)
                pb = 0 if sl == 0 else 2
                opsum = bankpair(pb)
                mmtok = None
                pairs = ((S["uTh"], wouth), (S["uTh"], woutl),
                         (S["uTl"], wouth))
                for pi, (uv, wv) in enumerate(pairs):
                    for kc in range(16):
                        for n in range(2):
                            st_ = (pi == 0 and kc == 0)
                            sp_ = (pi == 2 and kc == 15)
                            w = ()
                            if st_:
                                w = [th, tl, tw1,
                                     tw2, bank_free[pb + n]]
                            mmtok = t_mm(
                                opsum[:, n * 512:(n + 1) * 512],
                                uv[:, kc * 128:(kc + 1) * 128],
                                wv[:, kc * D + n * 512:
                                   kc * D + (n + 1) * 512],
                                st_, sp_, waits=w,
                                inc="pe" if (sp_ and n == 1) else None)
                te0 = v_op(lambda h, S=S, opsum=opsum: h.scalar_tensor_tensor(
                    t1b, opsum, 1.0, S["xt"], ALU.mult, ALU.add),
                    [("pe", mmtok), tx, tb0,
                     prev.get("t1bfree"), prev.get(("sfree", sl))])
                bank_free[pb] = ("v", te0)
                bank_free[pb + 1] = ("v", te0)
                te1 = v_op(lambda h, S=S: h.tensor_add(S["o"], t1b, borep))
                prev["t1bfree"] = ("v", te1)
                tst = sch.dma("gpsimd", xdst[t * 128:(t + 1) * 128, :],
                              S["o"], waits=[("v", te1)], sem="bs%d" % sl)
                prev[("free", sl)] = [("pe", mmtok), ("v", te1)]
                prev[("sfree", sl)] = tst
                if SAFE_MODE:
                    sch.barrier()
            A.reset()

        def l1_ffn(xsrc, xdst):
            A.mark()
            wq = A.alloc(8 * D, BF16)
            hsc = A.alloc(1, F32)
            lnb = [alloc_ln_slot() for _ in range(2)]
            obuf = [A.alloc(D, F32) for _ in range(2)]
            twl = sch.dma("sync", blocks(wq, 8), kpf(s_wl1qT))
            thsc = sch.dma("sync", hsc,
                           row(s_scal[5:6]).broadcast_to([128, 1]))
            prev = {}
            ptok = [None, None]
            for t in range(NT):
                sl = t % 2
                tok = emit_ln_tile(xsrc, t, lnb[sl], prevtok=ptok[sl],
                                   yT_waits=prev.get(("ytfree", sl), ()),
                                   extra_x_waits=[prev.get(("xlate", sl))],
                                   tag="f%d" % sl)
                ptok[sl] = tok
                pb = 0 if sl == 0 else 2
                opsum = bankpair(pb)
                mmtok = None
                for pi in range(2):
                    yT = (lnb[sl]["yThi"], lnb[sl]["yTlo"])[pi]
                    for kd in range(8):
                        for n in range(2):
                            st_ = (pi == 0 and kd == 0)
                            sp_ = (pi == 1 and kd == 7)
                            w = ()
                            if st_:
                                w = [tok["yThi"], tok["yTlo"],
                                     twl, bank_free[pb + n]]
                            mmtok = t_mm(
                                opsum[:, n * 512:(n + 1) * 512],
                                yT[:, kd * 128:(kd + 1) * 128],
                                wq[:, kd * D + n * 512:
                                   kd * D + (n + 1) * 512],
                                st_, sp_, waits=w,
                                inc="pe" if (sp_ and n == 1) else None)
                te1 = v_op(lambda h, sl=sl, opsum=opsum:
                           h.scalar_tensor_tensor(obuf[sl], opsum, hsc,
                                                  lnb[sl]["xt"], ALU.mult,
                                                  ALU.add),
                           [("pe", mmtok), thsc,
                            prev.get(("sfree", sl))])
                bank_free[pb] = ("v", te1)
                bank_free[pb + 1] = ("v", te1)
                tst = sch.dma("gpsimd", xdst[t * 128:(t + 1) * 128, :],
                              obuf[sl], waits=[("v", te1)], sem="fs%d" % sl)
                prev[("ytfree", sl)] = [("pe", mmtok)]
                prev[("sfree", sl)] = tst
                prev[("xlate", sl)] = ("v", te1)
                if SAFE_MODE:
                    sch.barrier()
            A.reset()

        def l1a_a(xsrc):
            A.mark()
            wq = {}
            for w in ("q", "k", "v"):
                wq[w] = (A.alloc(8 * D, BF16), A.alloc(8 * D, BF16))
            lnb = [alloc_ln_slot(with_yT=False) for _ in range(2)]
            yTh_m = A.alloc(8 * 512, BF16)
            yTl_m = A.alloc(8 * 512, BF16)
            eh = A.alloc(512, BF16)
            el = A.alloc(512, BF16)
            vh = A.alloc(D, BF16)
            vl = A.alloc(D, BF16)
            tld = {}
            for w in ("q", "k", "v"):
                tld[w] = (sch.dma("sync", blocks(wq[w][0], 8),
                                  kpf(s_wT[w][0]), sem="w%sh" % w),
                          sch.dma("sync", blocks(wq[w][1], 8),
                                  kpf(s_wT[w][1]), sem="w%sl" % w))
            dstmap = {"q": (s_q1T_h, s_q1T_l), "k": (s_k1T_h, s_k1T_l)}
            prev = {}
            ptok = [None, None]
            for mc in range(4):
                mtoks = []
                for si in range(4):
                    t = mc * 4 + si
                    sl = t % 2
                    tok = emit_ln_tile(
                        xsrc, t, lnb[sl], prevtok=ptok[sl],
                        yT_dsts=(blocks(yTh_m, 8)
                                 [:, :, si * 128:(si + 1) * 128],
                                 blocks(yTl_m, 8)
                                 [:, :, si * 128:(si + 1) * 128]),
                        yT_waits=prev.get("ymfree", ()), tag="g%d" % sl)
                    ptok[sl] = tok
                    mtoks.append(tok)
                wmac = []
                for tok in mtoks:
                    wmac += [tok["yThi"], tok["yTlo"]]
                lastmm = None
                for w in ("q", "k"):
                    wh, wl_ = wq[w]
                    dh, dl = dstmap[w]
                    for mt in range(8):
                        b = mt % 4
                        pairs = ((wh, yTh_m), (wh, yTl_m), (wl_, yTh_m))
                        for pi, (wv, yv) in enumerate(pairs):
                            for kd in range(8):
                                st_ = (pi == 0 and kd == 0)
                                sp_ = (pi == 2 and kd == 7)
                                lastmm = t_mm(
                                    bank(b),
                                    wv[:, kd * D + mt * 128:
                                       kd * D + (mt + 1) * 128],
                                    yv[:, kd * 512:(kd + 1) * 512], st_, sp_,
                                    waits=(wmac + [tld[w][0],
                                                   tld[w][1],
                                                   bank_free[b]])
                                    if st_ else (),
                                    inc="pe" if sp_ else None)
                        teh = s_op(lambda h, b=b: h.copy(eh, bank(b)),
                                   [("pe", lastmm), prev.get("ehfree")])
                        tel = v_op(lambda h, b=b: h.tensor_sub(el, bank(b),
                                                               eh),
                                   [("s", teh), prev.get("elfree")])
                        bank_free[b] = ("v", tel)
                        se1 = sch.dma("gpsimd",
                                      dh[mt * 128:(mt + 1) * 128,
                                         mc * 512:(mc + 1) * 512], eh,
                                      waits=[("s", teh)], inc="st")
                        se2 = sch.dma("gpsimd",
                                      dl[mt * 128:(mt + 1) * 128,
                                         mc * 512:(mc + 1) * 512], el,
                                      waits=[("v", tel)], inc="st")
                        prev["ehfree"] = [se1, ("v", tel)]
                        prev["elfree"] = se2
                wh, wl_ = wq["v"]
                for si in range(4):
                    t = mc * 4 + si
                    pb = 4 if si % 2 == 0 else 6
                    opsum = bankpair(pb)
                    lastmm = None
                    pairs = ((yTh_m, wh), (yTh_m, wl_), (yTl_m, wh))
                    for pi, (yv, wv) in enumerate(pairs):
                        for kd in range(8):
                            for n in range(2):
                                st_ = (pi == 0 and kd == 0)
                                sp_ = (pi == 2 and kd == 7)
                                w_ = ()
                                if st_:
                                    w_ = wmac + [tld["v"][0],
                                                 tld["v"][1],
                                                 bank_free[pb + n]]
                                lastmm = t_mm(
                                    opsum[:, n * 512:(n + 1) * 512],
                                    yv[:, kd * 512 + si * 128:
                                       kd * 512 + (si + 1) * 128],
                                    wv[:, kd * D + n * 512:
                                       kd * D + (n + 1) * 512],
                                    st_, sp_, waits=w_,
                                    inc="pe" if (sp_ and n == 1) else None)
                    tvh = s_op(lambda h, opsum=opsum: h.copy(vh, opsum),
                               [("pe", lastmm), prev.get("vhfree")])
                    tvl = v_op(lambda h, opsum=opsum: h.tensor_sub(vl, opsum,
                                                                   vh),
                               [("s", tvh), prev.get("vlfree")])
                    bank_free[pb] = ("v", tvl)
                    bank_free[pb + 1] = ("v", tvl)
                    sv1 = sch.dma("gpsimd", s_v1_h[t * 128:(t + 1) * 128, :],
                                  vh, waits=[("s", tvh)], inc="st")
                    sv2 = sch.dma("gpsimd", s_v1_l[t * 128:(t + 1) * 128, :],
                                  vl, waits=[("v", tvl)], inc="st")
                    prev["vhfree"] = [sv1, ("v", tvl)]
                    prev["vlfree"] = sv2
                prev["ymfree"] = [("pe", lastmm)]
                if SAFE_MODE:
                    sch.barrier()
            A.reset()

        def l1a_b(xsrc, xdst):
            A.mark()
            k1h = A.alloc(8 * T, BF16)
            k1l = A.alloc(8 * T, BF16)
            v1h = A.alloc(16 * D, BF16)
            v1l = A.alloc(16 * D, BF16)
            woh = A.alloc(8 * D, BF16)
            wol = A.alloc(8 * D, BF16)
            m_mid = A.alloc(512, BF16)
            m_r1 = A.alloc(384, BF16)
            m_r15 = A.alloc(384, BF16)
            m_r0a = A.alloc(512, BF16)
            m_r0b = A.alloc(512, BF16)
            qrh = A.alloc(D, BF16)
            qrl = A.alloc(D, BF16)
            xt = A.alloc(D, F32)
            BS = []
            for p_ in range(2):
                BS.append({
                    "sm32": A.alloc(T if p_ == 0 else 512, F32),
                    "e32": A.alloc(512, F32),
                    "phi": A.alloc(512, BF16),
                    "plo": A.alloc(512, BF16),
                    "pThi": A.alloc(512, BF16),
                    "pTlo": A.alloc(512, BF16),
                    "sst": A.alloc(4, F32),
                    "scols": A.alloc(4, F32),
                })
            ao = A.alloc(D, F32)
            aohi = A.alloc(D, BF16)
            aolo = A.alloc(D, BF16)
            aoThi = A.alloc(D, BF16)
            aoTlo = A.alloc(D, BF16)
            ob = ao
            tk1 = sch.dma("sync", blocks(k1h, 8), kpf(s_k1T_h))
            tk2 = sch.dma("sync", blocks(k1l, 8), kpf(s_k1T_l))
            tv1 = sch.dma("sync", blocks(v1h, 16),
                          s_v1_h.rearrange("(c p) e -> p c e", p=128))
            tv2 = sch.dma("sync", blocks(v1l, 16),
                          s_v1_l.rearrange("(c p) e -> p c e", p=128))
            tw1 = sch.dma("sync", blocks(woh, 8), kpf(s_wT["o"][0]))
            tw2 = sch.dma("sync", blocks(wol, 8), kpf(s_wT["o"][1]))
            wres = [x for x in (tk1, tk2, tv1, tv2, tw1, tw2)]

            tmask = None

            def mk_mask(ap, sels):
                nonlocal tmask
                tmask = g_op(lambda h, ap=ap: h.memset(ap, 0.0))
                for (cm, coef, base, fill) in sels:
                    tmask = g_op(lambda h, ap=ap, cm=cm, coef=coef, base=base,
                                 fill=fill: h.affine_select(
                                     out=ap, in_=ap, compare_op=ALU.is_ge,
                                     fill=fill, base=base,
                                     pattern=[[coef, ap.shape[-1]]],
                                     channel_multiplier=cm))

            mk_mask(m_mid[:, 0:128], [(0, -1, 0, NEG)])
            mk_mask(m_mid[:, 128:256], [(-1, 1, 0, NEG)])
            mk_mask(m_mid[:, 256:384], [])
            mk_mask(m_mid[:, 384:512], [(1, -1, 0, NEG)])
            mk_mask(m_r1[:, 0:128], [(-1, 1, 0, NEG), (0, 1, -1, 0.0)])
            mk_mask(m_r1[:, 128:256], [])
            mk_mask(m_r1[:, 256:384], [(1, -1, 0, NEG)])
            mk_mask(m_r15[:, 0:128], [(0, -1, 0, NEG)])
            mk_mask(m_r15[:, 128:256], [(-1, 1, 0, NEG)])
            mk_mask(m_r15[:, 256:384], [])
            mk_mask(m_r0a[:, 0:128], [])
            mk_mask(m_r0a[:, 128:256], [(1, -1, 0, NEG), (1, 0, -1, 0.0)])
            mk_mask(m_r0a[:, 256:512], [(-1, 0, 0, NEG)])
            mk_mask(m_r0b, [(-1, 0, 0, NEG)])
            wmask = ("g", tmask)

            prev = {}
            for r in range(NT):
                tq1 = sch.dma("gpsimd", blocks(qrh, 8),
                              kpf(s_q1T_h[:, r * 128:(r + 1) * 128]),
                              waits=prev.get("qfree", ()))
                tq2 = sch.dma("gpsimd", blocks(qrl, 8),
                              kpf(s_q1T_l[:, r * 128:(r + 1) * 128]))
                txl = sch.dma("gpsimd", xt, xsrc[r * 128:(r + 1) * 128, :],
                              waits=prev.get("xfree", ()))
                if r == 0:
                    W = 2048
                    mbs = list(range(16))
                    rhs_slices = [(i * 512, 512, i * 512) for i in range(4)]
                elif r == 1:
                    W = 384
                    mbs = [0, 1, 2]
                    rhs_slices = [(0, 384, 0)]
                elif r == 15:
                    W = 384
                    mbs = [0, 14, 15]
                    rhs_slices = [(0, 128, 0), (14 * 128, 256, 128)]
                else:
                    W = 512
                    mbs = [0, r - 1, r, r + 1]
                    rhs_slices = [(0, 128, 0), ((r - 1) * 128, 384, 128)]
                pieces = []
                for pi in range(3):
                    for (mcol, wdt, pcol) in rhs_slices:
                        pieces.append((pi, mcol, wdt, pcol))
                first_by_bank = {}
                last_by_bank = {}
                for idx, (pi, mc2, wd2, pc2) in enumerate(pieces):
                    b = pc2 // 512
                    if b not in first_by_bank:
                        first_by_bank[b] = idx
                    last_by_bank[b] = idx
                lastao = None
                for hd_ in range(NH):
                    c = hd_ // 2
                    rw = slice((hd_ % 2) * 64, (hd_ % 2) * 64 + 64)
                    P = 0 if r == 0 else (hd_ % 2)
                    B = BS[P]
                    sm32 = B["sm32"]
                    e32 = B["e32"]
                    phi = B["phi"]
                    plo = B["plo"]
                    pThi = B["pThi"]
                    pTlo = B["pTlo"]
                    nmx = B["sst"][:, 0:1]
                    sume = B["sst"][:, 1:2]
                    rcp = B["sst"][:, 2:3]
                    scols = B["scols"]
                    if r == 0:
                        ps = pbig1[:, 0:2048]
                        pbanks = [0, 1, 2, 3]
                    else:
                        ps = bank(hd_ % 2)[:, 0:W]
                        pbanks = [hd_ % 2]
                    qpairs = ((qrh, k1h), (qrh, k1l), (qrl, k1h))
                    lastmm = None
                    for idx, (pi, mc2, wd2, pc2) in enumerate(pieces):
                        qv, kv = qpairs[pi]
                        b = pc2 // 512
                        st_ = (idx == first_by_bank[b])
                        sp_ = (idx == last_by_bank[b])
                        w_ = ()
                        if idx == 0:
                            w_ = (wres + [tq1, tq2] +
                                  [bank_free[bb] for bb in pbanks])
                        lastmm = t_mm(
                            ps[:, pc2:pc2 + wd2],
                            qv[rw, c * 128:(c + 1) * 128],
                            kv[rw, c * T + mc2: c * T + mc2 + wd2],
                            st_, sp_, waits=w_,
                            inc="pe" if idx == len(pieces) - 1 else None)
                    tsm = None
                    smw = [("pe", lastmm), wmask, prev.get(("smfree", P))]
                    if r == 0:
                        for ck in range(4):
                            mk = m_r0a if ck == 0 else m_r0b
                            tsm = v_op(lambda h, ck=ck, mk=mk, sm32=sm32:
                                       h.scalar_tensor_tensor(
                                           sm32[:, ck * 512:(ck + 1) * 512],
                                           pbig1[:, ck * 512:(ck + 1) * 512],
                                           0.125, mk, ALU.mult, ALU.add),
                                       smw if ck == 0 else ())
                    else:
                        mk = m_r1 if r == 1 else (m_r15 if r == 15 else m_mid)
                        tsm = v_op(lambda h, ps=ps, mk=mk, W=W, sm32=sm32:
                                   h.scalar_tensor_tensor(
                                       sm32[:, 0:W], ps, 0.125, mk[:, 0:W],
                                       ALU.mult, ALU.add), smw)
                    for bb in pbanks:
                        bank_free[bb] = ("v", tsm)
                    tnm = v_op(lambda h, W=W, nmx=nmx, sm32=sm32:
                        h.reduce_max(
                        nmx, sm32[:, 0:W], axis=AX.X, negate=True),
                        [prev.get(("nmxfree", P))])
                    nch = max(1, W // 512)
                    avb = 4 + (hd_ % 2)
                    avp = bank(avb)[:, 0:64]
                    ppairs = ((pThi, v1h), (pThi, v1l), (pTlo, v1h))
                    lastav = None
                    texp = None
                    for ck in range(nch):
                        cw = min(512, W)
                        texp = s_op(lambda h, ck=ck, cw=cw, e32=e32,
                                    sm32=sm32, nmx=nmx, scols=scols:
                            h.activation(
                            e32[:, 0:cw], sm32[:, ck * 512:ck * 512 + cw],
                            AF.Exp, bias=nmx,
                            accum_out=scols[:, ck:ck + 1]),
                            [("v", tnm), prev.get(("sumefree", P)),
                             prev.get(("e32free", P))])
                        tph = s_op(lambda h, cw=cw, phi=phi, e32=e32:
                                   h.copy(phi[:, 0:cw], e32[:, 0:cw]),
                                   [prev.get(("phifree", P))])
                        tpl = v_op(lambda h, cw=cw, plo=plo, e32=e32,
                                   phi=phi: h.tensor_sub(
                            plo[:, 0:cw], e32[:, 0:cw], phi[:, 0:cw]),
                            [("s", tph), prev.get(("plofree", P))])
                        prev[("e32free", P)] = ("v", tpl)
                        tt1 = sch.dmat(blocks(pThi[:, 0:cw], cw // 128),
                                       blocks(phi[:, 0:cw], cw // 128),
                                       [("s", tph), prev.get(("avdone", P))],
                                       q="scalar")
                        tt2 = sch.dmat(blocks(pTlo[:, 0:cw], cw // 128),
                                       blocks(plo[:, 0:cw], cw // 128),
                                       [("v", tpl), prev.get(("avdone", P))],
                                       q="scalar")
                        prev[("phifree", P)] = tt1
                        prev[("plofree", P)] = tt2
                        mbs_ck = mbs[ck * 4:ck * 4 + 4] if r == 0 else mbs
                        for bi, mb in enumerate(mbs_ck):
                            for pi2, (pv, vv) in enumerate(ppairs):
                                st_ = (ck == 0 and bi == 0 and pi2 == 0)
                                sp_ = (ck == nch - 1 and
                                       bi == len(mbs_ck) - 1 and pi2 == 2)
                                cl_ = (bi == len(mbs_ck) - 1 and pi2 == 2)
                                lastav = t_mm(
                                    avp, pv[:, bi * 128:(bi + 1) * 128],
                                    vv[:, mb * D + hd_ * 64:
                                       mb * D + hd_ * 64 + 64],
                                    st_, sp_,
                                    waits=[tt1, tt2] +
                                    ([bank_free[avb]] if st_ else []),
                                    inc="pe" if cl_ else None)
                        prev[("avdone", P)] = ("pe", lastav)
                    prev[("smfree", P)] = ("s", texp)
                    prev[("nmxfree", P)] = ("s", texp)
                    tsum = v_op(lambda h, nch=nch, sume=sume, scols=scols:
                        h.reduce_sum(
                        sume, scols[:, 0:nch], axis=AX.X), [("s", texp)])
                    trcp = v_op(lambda h, rcp=rcp, sume=sume:
                                h.reciprocal(rcp, sume))
                    prev[("sumefree", P)] = ("v", trcp)
                    aow = [("pe", lastav), ("v", trcp)]
                    if hd_ == 0:
                        aow += [prev.get("aofree1"), prev.get("aofree2"),
                                prev.get("obfree")]
                    tao = v_op(lambda h, hd_=hd_, avp=avp, rcp=rcp:
                        h.tensor_scalar(
                        ao[:, hd_ * 64:(hd_ + 1) * 64], avp, rcp, None,
                        ALU.mult), aow)
                    bank_free[avb] = ("v", tao)
                    lastao = tao
                prev["qfree"] = [("pe", lastmm)]
                taoh = s_op(lambda h: h.copy(aohi, ao),
                            [("v", lastao), prev.get("t3free")])
                taol = v_op(lambda h: h.tensor_sub(aolo, ao, aohi),
                            [("s", taoh), prev.get("t4free")])
                prev["aofree1"] = ("s", taoh)
                prev["aofree2"] = ("v", taol)
                t3 = sch.dmat(blocks(aoThi, 8), blocks(aohi, 8),
                              [("s", taoh), prev.get("odone")])
                t4 = sch.dmat(blocks(aoTlo, 8), blocks(aolo, 8),
                              [("v", taol), prev.get("odone")])
                prev["t3free"] = t3
                prev["t4free"] = t4
                opsum = bankpair(6)
                opairs = ((aoThi, woh), (aoThi, wol), (aoTlo, woh))
                mmo = None
                for pi, (av, wv) in enumerate(opairs):
                    for kd in range(8):
                        for n in range(2):
                            st_ = (pi == 0 and kd == 0)
                            sp_ = (pi == 2 and kd == 7)
                            w_ = ()
                            if st_:
                                w_ = [t3, t4,
                                      bank_free[6 + n]]
                            mmo = t_mm(
                                opsum[:, n * 512:(n + 1) * 512],
                                av[:, kd * 128:(kd + 1) * 128],
                                wv[:, kd * D + n * 512:
                                   kd * D + (n + 1) * 512],
                                st_, sp_, waits=w_,
                                inc="pe" if (sp_ and n == 1) else None)
                prev["odone"] = ("pe", mmo)
                te = v_op(lambda h: h.scalar_tensor_tensor(
                    ob, opsum, 1.0, xt, ALU.mult, ALU.add),
                    [("pe", mmo), txl, prev.get("obfree")])
                bank_free[6] = ("v", te)
                bank_free[7] = ("v", te)
                tst = sch.dma("gpsimd", xdst[r * 128:(r + 1) * 128, :], ob,
                              waits=[("v", te)], inc="st")
                prev["obfree"] = tst
                prev["xfree"] = [("v", te)]
                if SAFE_MODE:
                    sch.barrier()
            A.reset()

        cur = inp["L0_attn_inputs"]
        np_ = max(0, min(n_phases, N_PHASES))
        done = 0

        def take():
            nonlocal done
            if done < np_:
                done += 1
                return True
            return False

        if take():
            ffn0_a(cur)
            sch.barrier()
        if take():
            ffn0_b(cur, xB)
            cur = xB
            sch.barrier()
        if take():
            l0a_a(cur)
            sch.barrier()
        if take():
            l0a_b()
            sch.barrier()
        if take():
            l0a_c(cur, xA)
            cur = xA
            sch.barrier()
        if take():
            ffn0_a(cur)
            sch.barrier()
        if take():
            ffn0_b(cur, xB)
            cur = xB
            sch.barrier()
        if take():
            l1_ffn(cur, xA)
            cur = xA
            sch.barrier()
        if take():
            l1a_a(cur)
            sch.barrier()
        if take():
            l1a_b(cur, xB)
            cur = xB
            sch.barrier()
        if take():
            l1_ffn(cur, out)
            cur = out
            sch.barrier()
        if cur is not out:
            A.mark()
            tmp = [A.alloc(D, F32) for _ in range(2)]
            ptoks = [None, None]
            for t in range(NT):
                sl = t % 2
                tl = sch.dma("gpsimd", tmp[sl], cur[t * 128:(t + 1) * 128, :],
                             waits=[ptoks[sl]] if ptoks[sl] else (),
                             sem="oc%d" % sl)
                ts_ = sch.dma("gpsimd", out[t * 128:(t + 1) * 128, :],
                              tmp[sl], waits=[tl], sem="os%d" % sl)
                ptoks[sl] = ts_
            A.reset()
        if DEBUG_DUMP:
            scrmap = {k: v for k, v in locals().items()
                      if k.startswith("s_") or k in ("xA", "xB")}
            for nm in DEBUG_DUMP:
                ap = scrmap[nm]
                dbg = nc.dram_tensor("dbg_" + nm, list(ap.shape), ap.dtype,
                                     kind="ExternalOutput").ap()
                sch.barrier()
                sch.dma("gpsimd", dbg, ap, inc="st")
        sch.barrier()
        sch.emit("sync", lambda h: h.nop())

        blk = stk.enter_context(nc.Block())
        sch.run(blk)
    return nc


def kernel(**inputs):
    x = np.ascontiguousarray(np.asarray(inputs["L0_attn_inputs"],
                                        dtype=np.float32))
    base = {k: np.ascontiguousarray(np.asarray(v, dtype=np.float32))
            for k, v in inputs.items() if k != "L0_attn_inputs"}
    nc = build_nc(N_PHASES_RUN)
    in_maps = []
    for i in range(8):
        m = dict(base)
        m["L0_attn_inputs"] = np.ascontiguousarray(x[i])
        in_maps.append(m)
    res = run_bass_kernel_spmd(nc, in_maps, core_ids=list(range(8)))
    return np.stack([np.asarray(r["out"]) for r in res.results], axis=0)
